# version 19
# baseline (speedup 1.0000x reference)
import contextlib
import numpy as np

import concourse.bass as bass
import concourse.mybir as mybir
from concourse.bass_utils import run_bass_kernel_spmd
from concourse.alu_op_type import AluOpType as ALU

F32 = mybir.dt.float32
BF16 = mybir.dt.bfloat16
I32 = mybir.dt.int32
AF = mybir.ActivationFunctionType
AX = mybir.AxisListType.X

NCORES = 8
B = 2
S = 2048
D = 1024
NCH = 8
H = 16
HD = 64
LH = 8
NL = 64
POOL = 32
W = 256
TILE = 256
NT = S // TILE
QC = TILE // 128
NEG = -1.0e9
EPS = 1e-5
ISQ = 0.125

DBG_A_TILES = 16
DBG_B_TILES = 16
DBG_B_PRO = True
DBG_A_STEPS = 10 ** 9


class Sem:
    def __init__(self, es, nc, name):
        self.h = es.enter_context(nc.semaphore(name))
        self.n = 0


class Arena:

    def __init__(self, tensor, cols):
        self.t = tensor
        self.cols = cols
        self.off = {"P": 0}
        self.base_p = None

    def alloc(self, phase, ncols):
        if phase == "P":
            o = self.off["P"]
            self.off["P"] = o + ncols
            return o
        if self.base_p is None:
            self.base_p = self.off["P"]
        o = self.off.get(phase, self.base_p)
        self.off[phase] = o + ncols
        assert self.off[phase] <= self.cols, (phase, self.off[phase], self.cols)
        return o

    def view(self, off, ncols):
        return self.t.ap()[:, off:off + ncols]


def build_program():
    nc = bass.Bass("TRN2", target_bir_lowering=False)
    din = lambda name, shp, dtype: nc.dram_tensor(name, shp, dtype, kind="ExternalInput").ap()

    x_in = din("x_in", [B, S, D], F32)
    wq_d = din("wq", [D, D], F32)
    wk_d = din("wk", [D, D], F32)
    wv_d = din("wv", [D, D], F32)
    wgq_d = din("wgq", [D, 512], F32)
    wgk_d = din("wgk", [D, 512], F32)
    wgv_d = din("wgv", [D, 512], F32)
    rout_d = din("rout", [512, D], F32)
    wout_d = din("wout", [D, D], F32)
    glgr_d = din("glgr", [64, 32], F32)
    qT_d = din("qT", [6, D, D], mybir.dt.int8)
    scT_d = din("scT", [6, 64, D], F32)
    rsel_d = din("rsel", [64, NCH * 128], F32)
    laT_d = din("laT", [6, D, 32], F32)
    lb4_d = din("lb4", [3, 128, D], F32)
    biasT_d = din("biasT", [6, 128, 8], F32)
    lngT_d = din("lngT", [3, 128, 8], F32)
    lnbT_d = din("lnbT", [3, 128, 8], F32)
    mloc_d = din("mloc", [4, 128, 256], F32)
    mlat_d = din("mlat", [NT, NL, TILE], F32)
    id_d = din("idm", [128, 128], F32)
    ones_d = din("onesv", [128, 2], F32)
    onesr_d = din("onesr", [1, 128], F32)

    y_out = nc.dram_tensor("y", [B, S, D], F32, kind="ExternalOutput").ap()
    x2_dram = nc.dram_tensor("x2scr", [B, S, D], BF16, kind="Internal").ap()

    with contextlib.ExitStack() as es:
        E = es.enter_context
        sb = lambda name, shp, dtype: E(nc.sbuf_tensor(name, shp, dtype))

        WB = 6 * 8192
        wbig = sb("wbig", [128, WB], BF16)
        wv_ = lambda o, n: wbig.ap()[:, o:o + n]
        o_wq, o_wk, o_wv, o_wo = 0, 8192, 16384, 24576
        o_wgq, o_wgk, o_wgv, o_ro = 32768, 36864, 40960, 45056

        AC = 48640
        ar_t = sb("arena", [128, AC], BF16)
        ar = Arena(ar_t, AC)

        glgrS = sb("glgrS", [64, 32], BF16)
        mlocS = sb("mlocS", [128, 4 * 256], BF16)
        mlatS = sb("mlatS", [NL, NT * TILE], BF16)
        idS = sb("idS", [128, 128], BF16)
        onesS = sb("onesS", [128, 2], BF16)
        onesR = sb("onesR", [1, 128], BF16)
        laS = sb("laS", [128, 6 * NCH * 32], BF16)
        lbS = sb("lbS", [128, 3 * 1024], BF16)
        biasS = sb("biasS", [128, 6 * 8], F32)
        lngS = sb("lngS", [128, 3 * 8], F32)
        lnbS = sb("lnbS", [128, 3 * 8], F32)

        A = lambda n: ar.alloc("A", n)
        o_xt = A(2 * NCH * TILE)
        o_xn1 = A(NCH * TILE)
        o_XF = A(NCH * TILE)
        o_QF = A(H * TILE)
        o_GQF = A(LH * TILE)
        o_KR = A(H * 2 * W)
        o_VR = A(4 * 1024)
        o_expP = A(4 * 4 * TILE)
        o_expL = A(LH * TILE)
        o_XPOOL = A(NCH * NL)
        o_GKF = A(LH * NL)
        o_GVT = A(512)
        o_gvtmp = o_GVT
        o_remTM = A(QC * 1024)
        o_remNF = o_GQF
        o_remPF = A(NCH * TILE)
        o_aoF = o_remPF
        o_aoTM = A(QC * 1024)
        o_x2st = A(QC * 1024)
        o_mixF = A(NCH * TILE)

        Bv = lambda n: ar.alloc("B", n)
        o_x2t = Bv(2 * NCH * TILE)
        o_xn2 = Bv(NCH * TILE)
        o_hF = Bv(4 * NCH * TILE)
        o_y1F = Bv(NCH * TILE)
        o_sqF = Bv(NCH * TILE)
        o_t1F = Bv(NCH * TILE)
        o_gF = Bv(NCH * TILE)
        o_hTM = Bv(QC * 1024)
        o_qtmp = Bv(2 * 1024)
        o_sctmp = Bv(2 * 1024)
        o_aftmp = Bv(2 * 1024)
        o_scc = Bv(2 * 1024)
        o_rsel = Bv(1024)
        o_gF2 = Bv(NCH * TILE)
        o_btr = Bv(TILE)
        o_brow = Bv(2 * TILE)
        o_yout = Bv(2 * QC * 1024)
        o_srow = Bv(2 * 4 * TILE)
        o_x2t3 = Bv(NCH * TILE)
        o_invb = Bv(2 * 512)

        V = ar.view
        global OFFS
        OFFS = {k: v for k, v in locals().items() if k.startswith("o_")}

        glgrT = sb("glgrT", [128, QC * 32], F32)
        rl16 = sb("rl16", [128, QC * 16], F32)
        rr16 = sb("rr16", [128, QC * 16], F32)
        rden8 = sb("rden8", [128, QC * 8], F32)
        stat6 = sb("stat6", [128, 12], F32)
        stat2 = sb("stat2", [128, 2 * QC], F32)
        sigv = sb("sigv", [128, QC], F32)
        invv = sb("invv", [128, QC], F32)

        psA = E(nc.psum_tensor("psA", [128, 2048], F32))
        psB = E(nc.psum_tensor("psB", [128, 1024], F32))
        psC = E(nc.psum_tensor("psC", [128, 512], F32))
        psD = E(nc.psum_tensor("psD", [128, 512], F32))

        te = Sem(es, nc, "te")
        sc = Sem(es, nc, "sc")
        ve = Sem(es, nc, "ve")
        dwa = Sem(es, nc, "dwa")
        dx = [Sem(es, nc, f"dx{p}") for p in range(2)]
        dtr = [Sem(es, nc, f"dtr{p}") for p in range(2)]
        dgv = [Sem(es, nc, f"dgv{p}") for p in range(2)]
        dx2s = Sem(es, nc, "dx2s")
        dx2l = [Sem(es, nc, f"dx2l{p}") for p in range(2)]
        dwb = [Sem(es, nc, f"dwb{p}") for p in range(2)]
        dscc = [Sem(es, nc, f"dscc{p}") for p in range(2)]
        drsel = Sem(es, nc, "drsel")
        dys = Sem(es, nc, "dys")
        ev1 = Sem(es, nc, "ev1")
        ev2 = Sem(es, nc, "ev2")

        blk_raw = E(nc.Block())

        class _GatedBlock:
            def __init__(self, b):
                self._b = b
                self.count = 0

            def _gate(self, dec):
                def wrap(fn):
                    self.count += 1
                    if self.count <= DBG_A_STEPS:
                        return dec(fn)
                    return None
                return wrap

            @property
            def gpsimd(self):
                return self._gate(self._b.gpsimd)

            @property
            def vector(self):
                return self._gate(self._b.vector)

            @property
            def scalar(self):
                return self._gate(self._b.scalar)

            @property
            def tensor(self):
                return self._gate(self._b.tensor)

            @property
            def sync(self):
                return self._gate(self._b.sync)

        block = _GatedBlock(blk_raw)

        def I(sem, inst, k=1):
            sem.n += k
            inst.then_inc(sem.h, k)
            return sem.n

        def trbatch(sy, par, calls):
            sy.wait_ge(dtr[par].h, dtr[par].n)
            sy.wait_ge(dgv[0].h, dgv[0].n)
            sy.wait_ge(dgv[1].h, dgv[1].n)
            for dst, src in calls:
                sy.dma_start_transpose(dst, src).then_inc(dtr[par].h, 16)
                dtr[par].n += 16
            return dtr[par].n

        @block.gpsimd
        def _(g):
            r3 = lambda ap: ap.rearrange("(c p) o -> p c o", p=128)
            rr = lambda off, nch, w: V(off, nch * w).rearrange("p (c o) -> p c o", c=nch)
            wbr = lambda off, nch, w: wbig.ap()[:, off:off + nch * w].rearrange("p (c o) -> p c o", c=nch)
            I(dwa, g.dma_start(wbr(o_wq, NCH, 1024), r3(wq_d)), 16)
            I(dwa, g.dma_start(wbr(o_wk, NCH, 1024), r3(wk_d)), 16)
            I(dwa, g.dma_start(wbr(o_wv, NCH, 1024), r3(wv_d)), 16)
            I(dwa, g.dma_start(wbr(o_wo, NCH, 1024), r3(wout_d)), 16)
            I(dwa, g.dma_start(wbr(o_wgq, NCH, 512), r3(wgq_d)), 16)
            I(dwa, g.dma_start(wbr(o_wgk, NCH, 512), r3(wgk_d)), 16)
            I(dwa, g.dma_start(wbr(o_wgv, NCH, 512), r3(wgv_d)), 16)
            I(dwa, g.dma_start(wbr(o_ro, 4, 1024), r3(rout_d)), 16)
            I(dwa, g.dma_start(glgrS[:], glgr_d[:, :]), 16)
            I(dwa, g.dma_start(mlocS.ap().rearrange("p (c o) -> p c o", c=4),
                               mloc_d.transpose([1, 0, 2])), 16)
            I(dwa, g.dma_start(mlatS.ap().rearrange("p (c o) -> p c o", c=NT),
                               mlat_d.transpose([1, 0, 2])), 16)
            I(dwa, g.dma_start(idS[:], id_d[:, :]), 16)
            I(dwa, g.dma_start(onesS[:], ones_d[:, :]), 16)
            I(dwa, g.dma_start(onesR[:], onesr_d[:, :]), 16)
            I(dx[0], g.dma_start(
                V(o_xt, NCH * TILE).rearrange("p (tc f) -> p tc f", tc=QC),
                x_in[0, 0:TILE, :].rearrange("(tc p) f -> p tc f", p=128)), 16)

        @block.vector
        def _(v):
            v.wait_ge(dwa.h, dwa.n)
            v.memset(V(o_expP, 4 * 4 * TILE), 0.0)
            v.memset(V(o_expL, LH * TILE), 0.0)
            v.memset(ar_t.ap()[0:64, o_KR:o_KR + H * 2 * W], 0.0)
            v.memset(V(o_VR, 4 * 1024), 0.0)
            v.memset(ar_t.ap()[0:64, o_GKF:o_GKF + LH * NL], 0.0)
            v.memset(ar_t.ap()[0:64, o_GVT:o_GVT + 512], 0.0)
            v.memset(V(o_XPOOL, NCH * NL), 0.0)
            I(ve, v.drain())
        init_ve = ve.n

        te_after_latAV = {}
        te_after_tile = {}
        te_projq_d = {}
        kg_evac_d = {}
        exp_last_sc_d = {}
        qv_evac_d = {}
        sc_gate_d = {}
        ve_after_xuse = {}
        ve_after_x2 = {}
        ln1_done_d = {}
        xf_done_d = {}
        aof_done_d = {}
        mix_done_d = {}

        def emit_ln1_xf(tj):
            bj, Tj = divmod(tj, NT)
            pj = tj % 2
            xtvj = V(o_xt + pj * NCH * TILE, NCH * TILE)
            ve_stats = {}

            @block.vector
            def _(v, tj=tj, pj=pj, xtvj=xtvj):
                v.wait_ge(dx[pj].h, dx[pj].n)
                if tj == 0:
                    v.wait_ge(dwa.h, dwa.n)
                for tc in range(QC):
                    xv = xtvj[:, tc * 1024:(tc + 1) * 1024]
                    v.bn_stats(stat6[:, 0:6], xv[:, 0:512])
                    v.drain()
                    v.bn_stats(stat6[:, 6:12], xv[:, 512:1024])
                    v.drain()
                    v.bn_aggr(stat2[:, 2 * tc:2 * tc + 2], stat6[:, 0:12])
                    v.drain()
                    v.tensor_scalar(stat2[:, 2 * tc + 1:2 * tc + 2],
                                    stat2[:, 2 * tc + 1:2 * tc + 2], EPS, None, ALU.add)
                    v.drain()
                    ve_stats[tc] = I(ve, v.drain())

            sc_sig = {}

            @block.scalar
            def _(s):
                for tc in range(QC):
                    s.wait_ge(ve.h, ve_stats[tc])
                    sc_sig[tc] = I(sc, s.activation(sigv[:, tc:tc + 1],
                                                    stat2[:, 2 * tc + 1:2 * tc + 2],
                                                    AF.Sqrt))

            @block.vector
            def _(v, xtvj=xtvj):
                for tc in range(QC):
                    v.wait_ge(sc.h, sc_sig[tc])
                    v.reciprocal(invv[:, tc:tc + 1], sigv[:, tc:tc + 1])
                    v.drain()
                    v.tensor_scalar(V(o_xn1 + tc * 1024, 1024),
                                    xtvj[:, tc * 1024:(tc + 1) * 1024],
                                    stat2[:, 2 * tc:2 * tc + 1], invv[:, tc:tc + 1],
                                    ALU.subtract, ALU.mult)
                    v.drain()
                I(ve, v.drain())
            ln1_done_d[tj] = ve.n

            @block.sync
            def _(sy, tj=tj, pj=pj):
                sy.wait_ge(ve.h, ln1_done_d[tj])
                if tj >= 1:
                    sy.wait_ge(te.h, te_projq_d[tj - 1])
                    sy.wait_ge(ve.h, kg_evac_d[tj - 1])
                trbatch(sy, pj, [
                    (V(o_XF, NCH * TILE).rearrange("p (c t) -> p c t", c=NCH)[:, :, tc * 128:(tc + 1) * 128],
                     V(o_xn1 + tc * 1024, 1024)) for tc in range(QC)])
            xf_done_d[tj] = dtr[pj].n

        def emit_vq(tj):
            pj = tj % 2
            slotj = (tj % NT) % 2
            te_vq = {}
            vev = {}

            @block.tensor
            def _(t, pj=pj, tj=tj):
                t.wait_ge(dtr[pj].h, xf_done_d[tj])
                if tj == 0:
                    t.wait_ge(ve.h, init_ve)
                else:
                    t.wait_ge(sc.h, exp_last_sc_d[tj - 1])
                ve0v = ve.n
                for tc in range(QC):
                    for og in range(2):
                        if tc == 1:
                            t.wait_ge(ve.h, ve0v + og + 1)
                        reg = psA[:, 1024 + og * 512:1024 + (og + 1) * 512]
                        for c in range(NCH):
                            mm = t.matmul(reg,
                                          V(o_XF + c * TILE + tc * 128, 128),
                                          wv_(o_wv + c * 1024 + og * 512, 512),
                                          start=(c == 0), stop=(c == NCH - 1))
                        te_vq[("v", tc, og)] = I(te, mm)

            @block.vector
            def _(v, slotj=slotj):
                for tc in range(QC):
                    for og in range(2):
                        v.wait_ge(te.h, te_vq[("v", tc, og)])
                        vev[(tc, og)] = I(ve, v.tensor_copy(
                            V(o_VR + (slotj * 2 + tc) * 1024 + og * 512, 512),
                            psA[:, 1024 + og * 512:1024 + (og + 1) * 512]))

            @block.tensor
            def _(t, tj=tj):
                if tj >= 1:
                    t.wait_ge(ve.h, mix_done_d[tj - 1])
                ve0q = ve.n
                for oc in range(NCH):
                    reg = psA[:, (oc % 4) * 512:(oc % 4) * 512 + 256]
                    if oc >= 4:
                        t.wait_ge(ve.h, ve0q + oc - 3)
                    elif oc == 2:
                        t.wait_ge(ve.h, vev[(1, 0)])
                    elif oc == 3:
                        t.wait_ge(ve.h, vev[(1, 1)])
                    for c in range(NCH):
                        mm = t.matmul(reg, wv_(o_wq + c * 1024 + oc * 128, 128),
                                      V(o_XF + c * TILE, TILE),
                                      start=(c == 0), stop=(c == NCH - 1))
                    te_vq[("q", oc)] = I(te, mm)

            @block.vector
            def _(v):
                for oc in range(NCH):
                    v.wait_ge(te.h, te_vq[("q", oc)])
                    reg = psA[:, (oc % 4) * 512:(oc % 4) * 512 + 256]
                    v.tensor_copy(ar_t.ap()[0:64, o_QF + (2 * oc) * TILE:o_QF + (2 * oc + 1) * TILE],
                                  reg[0:64, :])
                    v.drain()
                    v.tensor_copy(ar_t.ap()[0:64, o_QF + (2 * oc + 1) * TILE:o_QF + (2 * oc + 2) * TILE],
                                  reg[64:128, :])
                    I(ve, v.drain())
            qv_evac_d[tj] = ve.n

        emit_ln1_xf(0)
        pend_tail = [None]

        for ti in range(min(B * NT, DBG_A_TILES)):
            b, T = divmod(ti, NT)
            par = ti % 2
            q0 = T * TILE
            n = T
            slot = n % 2
            pslot = (n - 1) % 2
            xtv = V(o_xt + par * NCH * TILE, NCH * TILE)

            if ti == 0:
                emit_vq(0)
            te_v = {}

            @block.tensor
            def _(t, ti=ti):
                t.wait_ge(ve.h, qv_evac_d[ti])
                if ti >= 1:
                    t.wait_ge(ve.h, aof_done_d[ti - 1])
                    t.wait_ge(sc.h, sc_gate_d[ti - 1])
                ve0k = ve.n
                for oc in range(NCH):
                    reg = psA[:, (oc % 4) * 512:(oc % 4) * 512 + 256]
                    if oc >= 4:
                        t.wait_ge(ve.h, ve0k + oc - 3)
                    for c in range(NCH):
                        mm = t.matmul(reg, wv_(o_wk + c * 1024 + oc * 128, 128),
                                      V(o_XF + c * TILE, TILE),
                                      start=(c == 0), stop=(c == NCH - 1))
                    te_v[("k", oc)] = I(te, mm)
                for oc in range(4):
                    reg = psB[:, (oc % 2) * 512:(oc % 2) * 512 + 256]
                    if oc >= 2:
                        t.wait_ge(ve.h, ve0k + NCH + oc - 1)
                    for c in range(NCH):
                        mm = t.matmul(reg, wv_(o_wgq + c * 512 + oc * 128, 128),
                                      V(o_XF + c * TILE, TILE),
                                      start=(c == 0), stop=(c == NCH - 1))
                    te_v[("gq", oc)] = I(te, mm)
                for qc in range(QC):
                    reg = psC[:, 64:96] if qc == 0 else psD[:, 64:96]
                    for h in range(H):
                        mm = t.matmul(reg[:, h * 2:h * 2 + 2],
                                      ar_t.ap()[0:64, o_QF + h * TILE + qc * 128:
                                                o_QF + h * TILE + qc * 128 + 128],
                                      glgrS[:, h * 2:h * 2 + 2],
                                      start=True, stop=True)
                    te_v[("gate", qc)] = I(te, mm)
            te_projq_d[ti] = te.n

            @block.vector
            def _(v, T=T, slot=slot):
                for oc in range(NCH):
                    v.wait_ge(te.h, te_v[("k", oc)])
                    reg = psA[:, (oc % 4) * 512:(oc % 4) * 512 + 256]
                    v.tensor_scalar(ar_t.ap()[0:64, o_KR + (2 * oc) * 2 * W + slot * W:
                                              o_KR + (2 * oc) * 2 * W + (slot + 1) * W],
                                    reg[0:64, :], ISQ, None, ALU.mult)
                    v.drain()
                    v.tensor_scalar(ar_t.ap()[0:64, o_KR + (2 * oc + 1) * 2 * W + slot * W:
                                              o_KR + (2 * oc + 1) * 2 * W + (slot + 1) * W],
                                    reg[64:128, :], ISQ, None, ALU.mult)
                    I(ve, v.drain())
                for oc in range(4):
                    v.wait_ge(te.h, te_v[("gq", oc)])
                    reg = psB[:, (oc % 2) * 512:(oc % 2) * 512 + 256]
                    v.tensor_copy(ar_t.ap()[0:64, o_GQF + (2 * oc) * TILE:o_GQF + (2 * oc + 1) * TILE],
                                  reg[0:64, :])
                    v.drain()
                    v.tensor_copy(ar_t.ap()[0:64, o_GQF + (2 * oc + 1) * TILE:o_GQF + (2 * oc + 2) * TILE],
                                  reg[64:128, :])
                    I(ve, v.drain())
                for c in range(NCH):
                    v.tensor_reduce(
                        stat6[:, 0:8],
                        V(o_XF + c * TILE, TILE).rearrange("p (s w) -> p s w", w=POOL),
                        axis=AX, op=ALU.add)
                    v.drain()
                    v.tensor_scalar(V(o_XPOOL + c * NL + T * 8, 8),
                                    stat6[:, 0:8], 1.0 / POOL, None, ALU.mult)
                    v.drain()
                I(ve, v.drain())
            kg_evac = ve.n
            kg_evac_d[ti] = kg_evac

            if pend_tail[0] is not None:
                pend_tail[0]()
                pend_tail[0] = None

            if ti + 1 < B * NT:
                b2, T2 = divmod(ti + 1, NT)
                p2 = (ti + 1) % 2

                @block.gpsimd
                def _(g, b2=b2, T2=T2, p2=p2, ti=ti):
                    if ti >= 1:
                        g.wait_ge(ve.h, ve_after_xuse[ti - 1])
                    I(dx[p2], g.dma_start(
                        V(o_xt + p2 * NCH * TILE, NCH * TILE).rearrange(
                            "p (tc f) -> p tc f", tc=QC),
                        x_in[b2, T2 * TILE:(T2 + 1) * TILE, :].rearrange(
                            "(tc p) f -> p tc f", p=128)), 16)

            sc_gate = {}

            @block.scalar
            def _(s):
                for qc in range(QC):
                    s.wait_ge(te.h, te_v[("gate", qc)])
                    reg = psC[:, 64:96] if qc == 0 else psD[:, 64:96]
                    dst = glgrT.ap()[:, qc * 32:(qc + 1) * 32].rearrange(
                        "p (g h) -> p h g", g=2)
                    sc_gate[qc] = I(sc, s.activation(dst, reg, AF.Sigmoid))
            sc_gate_d[ti] = sc_gate[QC - 1]

            @block.tensor
            def _(t, T=T):
                t.wait_ge(ve.h, kg_evac)
                t.wait_ge(sc.h, sc_gate[QC - 1])
                for oc in range(4):
                    reg = psA[:, oc * 512:oc * 512 + 8]
                    for c in range(NCH):
                        mm = t.matmul(reg, wv_(o_wgk + c * 512 + oc * 128, 128),
                                      V(o_XPOOL + c * NL + T * 8, 8),
                                      start=(c == 0), stop=(c == NCH - 1))
                    te_v[("gk", oc)] = I(te, mm)
                for og in range(2):
                    for c in range(NCH):
                        mm = t.matmul(psD[0:8, 256 + og * 128:256 + (og + 1) * 128],
                                      V(o_XPOOL + c * NL + T * 8, 8),
                                      wv_(o_wgv + c * 512 + og * 128, 128),
                                      start=(c == 0), stop=(c == NCH - 1))
                    if og == 0:
                        I(te, mm)
                te_v[("gv",)] = I(te, mm)
                for og in range(2):
                    for c in range(NCH):
                        mm = t.matmul(psC[0:8, 256 + og * 128:256 + (og + 1) * 128],
                                      V(o_XPOOL + c * NL + T * 8, 8),
                                      wv_(o_wgv + c * 512 + 256 + og * 128, 128),
                                      start=(c == 0), stop=(c == NCH - 1))
                    if og == 0:
                        I(te, mm)
                te_v[("gv2",)] = I(te, mm)

            @block.vector
            def _(v, T=T, par=par, ti=ti):
                for oc in range(4):
                    v.wait_ge(te.h, te_v[("gk", oc)])
                    reg = psA[:, oc * 512:oc * 512 + 8]
                    v.tensor_scalar(ar_t.ap()[0:64, o_GKF + (2 * oc) * NL + T * 8:
                                              o_GKF + (2 * oc) * NL + T * 8 + 8],
                                    reg[0:64, :], ISQ, None, ALU.mult)
                    v.drain()
                    v.tensor_scalar(ar_t.ap()[0:64, o_GKF + (2 * oc + 1) * NL + T * 8:
                                              o_GKF + (2 * oc + 1) * NL + T * 8 + 8],
                                    reg[64:128, :], ISQ, None, ALU.mult)
                    I(ve, v.drain())
                v.wait_ge(te.h, te_v[("gv2",)])
                v.wait_ge(dgv[par].h, dgv[par].n)
                v.tensor_copy(ar_t.ap()[64:72, o_gvtmp:o_gvtmp + 256], psD[0:8, 256:512])
                v.drain()
                v.tensor_copy(ar_t.ap()[64:72, o_gvtmp + 256:o_gvtmp + 512], psC[0:8, 256:512])
                I(ve, v.drain())
            gv_evac = ve.n

            @block.gpsimd
            def _(g, T=T, par=par, ti=ti):
                g.wait_ge(ve.h, gv_evac)
                if ti >= 1:
                    g.wait_ge(te.h, te_after_latAV[ti - 1])
                g.wait_ge(dtr[0].h, dtr[0].n)
                g.wait_ge(dtr[1].h, dtr[1].n)
                I(dgv[par], g.dma_start(ar_t.ap()[T * 8:T * 8 + 8, o_GVT:o_GVT + 512],
                                        ar_t.ap()[64:72, o_gvtmp:o_gvtmp + 512]), 16)

            te_lat = {}

            @block.tensor
            def _(t, T=T, par=par):
                t.wait_ge(dgv[par].h, dgv[par].n)
                t.wait_ge(ve.h, gv_evac)
                sc0 = sc.n
                for lh in range(LH):
                    reg = psA[0:64, 1024 + (lh % 2) * 512:1024 + (lh % 2) * 512 + 256]
                    if lh >= 2:
                        t.wait_ge(sc.h, sc0 + lh - 1)
                    mm = t.matmul(reg, ar_t.ap()[0:64, o_GKF + lh * NL:o_GKF + (lh + 1) * NL],
                                  ar_t.ap()[0:64, o_GQF + lh * TILE:o_GQF + (lh + 1) * TILE],
                                  start=True, stop=True)
                    te_lat[lh] = I(te, mm)

            sc_lexp = {}

            @block.scalar
            def _(s):
                for lh in range(LH):
                    s.wait_ge(te.h, te_lat[lh])
                    reg = psA[0:64, 1024 + (lh % 2) * 512:1024 + (lh % 2) * 512 + 256]
                    sc_lexp[lh] = I(sc, s.activation(
                        ar_t.ap()[0:64, o_expL + lh * TILE:o_expL + (lh + 1) * TILE],
                        reg, AF.Exp))

            @block.vector
            def _(v, T=T):
                for lh in range(LH):
                    v.wait_ge(sc.h, sc_lexp[lh])
                    v.tensor_tensor(ar_t.ap()[0:64, o_expL + lh * TILE:o_expL + (lh + 1) * TILE],
                                    ar_t.ap()[0:64, o_expL + lh * TILE:o_expL + (lh + 1) * TILE],
                                    mlatS[:, T * TILE:(T + 1) * TILE], op=ALU.mult)
                    I(ve, v.drain())
            latexp_done = ve.n

            te_lav = {}

            @block.tensor
            def _(t):
                t.wait_ge(ve.h, latexp_done)
                ve0 = ve.n
                for qc in range(QC):
                    if qc == 1:
                        t.wait_ge(ve.h, ve0 + 1)
                    mmB = None
                    for lh in range(LH):
                        st = ar_t.ap()[0:64, o_expL + lh * TILE + qc * 128:
                                       o_expL + lh * TILE + qc * 128 + 128]
                        mmA = t.matmul(psB[:, lh * 64:(lh + 1) * 64], st,
                                       ar_t.ap()[0:64, o_GVT + lh * 64:o_GVT + (lh + 1) * 64],
                                       start=True, stop=True)
                        mmB = t.matmul(psC[:, 32 + qc * 8 + lh:32 + qc * 8 + lh + 1], st,
                                       onesS[0:64, 0:1], start=True, stop=True)
                    I(te, mmA)
                    te_lav[qc] = I(te, mmB)
                te_after_latAV[ti] = te.n

            @block.vector
            def _(v):
                for qc in range(QC):
                    v.wait_ge(te.h, te_lav[qc])
                    v.reciprocal(rden8[:, qc * 8:(qc + 1) * 8],
                                 psC[:, 32 + qc * 8:32 + (qc + 1) * 8])
                    v.drain()
                    rv = rden8.ap()[:, qc * 8:(qc + 1) * 8].unsqueeze(2).broadcast_to((128, 8, 64))
                    v.tensor_tensor(
                        V(o_remTM + qc * 1024, 512).rearrange("p (l e) -> p l e", l=8),
                        psB[:, 0:512].rearrange("p (l e) -> p l e", l=8), rv, op=ALU.mult)
                    I(ve, v.drain())
            remnorm_done = ve.n

            def emit_remgate():
                @block.vector
                def _(v, par=par):
                    for qc in range(QC):
                        v.wait_ge(sc.h, sc_gate[qc])
                        v.wait_ge(dtr[par].h, rem_marks['remtm'])
                        rrv = rden8.ap()[:, qc * 8:(qc + 1) * 8].unsqueeze(2) \
                            .broadcast_to((128, 8, 2))
                        v.tensor_tensor(rr16[:, qc * 16:(qc + 1) * 16],
                                        glgrT[:, qc * 32 + 16:qc * 32 + 32], rrv, op=ALU.mult)
                        v.drain()
                        rrv2 = rr16.ap()[:, qc * 16:(qc + 1) * 16].unsqueeze(2).broadcast_to((128, 16, 64))
                        v.tensor_tensor(
                            V(o_aoTM + qc * 1024, 1024).rearrange("p (h e) -> p h e", h=H),
                            V(o_aoTM + qc * 1024, 1024).rearrange("p (h e) -> p h e", h=H),
                            rrv2, op=ALU.mult)
                        I(ve, v.drain())
                remgate_done = ve.n

            rem_marks = {}

            def emit_remote_tail():
                @block.sync
                def _(sy, par=par):
                    sy.wait_ge(ve.h, remnorm_done)
                    trbatch(sy, par, [
                        (V(o_remNF, 4 * TILE).rearrange("p (c t) -> p c t", c=4)[:, :, qc * 128:(qc + 1) * 128],
                         V(o_remTM + qc * 1024, 512)) for qc in range(QC)])
                remnf_done = dtr[par].n

                te_ro = {}

                @block.tensor
                def _(t, par=par):
                    t.wait_ge(dtr[par].h, remnf_done)
                    sc0 = sc.n
                    for oc in range(NCH):
                        reg = psD[:, 0:256] if oc % 2 == 0 else psC[:, 256:512]
                        if oc >= 2:
                            t.wait_ge(sc.h, sc0 + oc - 1)
                        for c in range(4):
                            mm = t.matmul(reg, wv_(o_ro + c * 1024 + oc * 128, 128),
                                          V(o_remNF + c * TILE, TILE),
                                          start=(c == 0), stop=(c == 3))
                        te_ro[oc] = I(te, mm)

                @block.scalar
                def _(s):
                    for oc in range(NCH):
                        s.wait_ge(te.h, te_ro[oc])
                        reg = psD[:, 0:256] if oc % 2 == 0 else psC[:, 256:512]
                        for qc in range(QC):
                            s.activation(V(o_remPF + qc * 1024 + oc * 128, 128),
                                         reg[:, qc * 128:(qc + 1) * 128], AF.Copy)
                            s.drain()
                        I(sc, s.drain())
                rem_marks['rempf'] = sc.n

                @block.sync
                def _(sy, par=par):
                    sy.wait_ge(sc.h, rem_marks['rempf'])
                    trbatch(sy, par, [
                        (V(o_aoTM + qc * 1024, 1024).rearrange("p (c f) -> p c f", c=NCH),
                         V(o_remPF + qc * 1024, 1024)) for qc in range(QC)])
                rem_marks['remtm'] = dtr[par].n

            mix_marks = {}
            kc_all = [2, 3] if n == 0 else [0, 1, 2, 3]
            for hf in range(4):
                if hf == 2 and ti + 1 < B * NT:
                    emit_ln1_xf(ti + 1)
                te_sb = {}
                sc_exp = {}
                sregs = [psA[:, 1024:1536], psA[:, 1536:2048]]

                @block.tensor
                def _(t, hf=hf, slot=slot, pslot=pslot, n=n, kc_all=kc_all):
                    t.wait_ge(ve.h, kg_evac)
                    sc0 = sc.n
                    nb = 0
                    for kc in kc_all:
                        sl = pslot if kc < 2 else slot
                        koff = (kc % 2) * 128
                        for hg in range(2):
                            reg = sregs[nb % 2]
                            if nb >= 2:
                                t.wait_ge(sc.h, sc0 + nb - 1)
                            mm = None
                            for hh in range(2):
                                h = hf * 4 + hg * 2 + hh
                                mm = t.matmul(
                                    reg[:, hh * 256:(hh + 1) * 256],
                                    ar_t.ap()[0:64,
                                              o_KR + h * 2 * W + sl * W + koff:
                                              o_KR + h * 2 * W + sl * W + koff + 128],
                                    ar_t.ap()[0:64,
                                              o_QF + h * TILE:o_QF + (h + 1) * TILE],
                                    start=True, stop=True)
                            te_sb[nb] = I(te, mm)
                            nb += 1

                nbt = len(kc_all) * 2

                @block.scalar
                def _(s, hf=hf, kc_all=kc_all, nbt=nbt):
                    for nb in range(nbt):
                        s.wait_ge(te.h, te_sb[nb])
                        reg = sregs[nb % 2]
                        kc = kc_all[nb // 2]
                        hg = nb % 2
                        dst = ar_t.ap()[:, o_expP + kc * 4 * TILE + hg * 2 * TILE:
                                        o_expP + kc * 4 * TILE + (hg + 1) * 2 * TILE]
                        sc_exp[nb] = I(sc, s.activation(dst, reg, AF.Exp))

                @block.vector
                def _(v, hf=hf, kc_all=kc_all, nbt=nbt):
                    for nb in range(nbt):
                        v.wait_ge(sc.h, sc_exp[nb])
                        kc = kc_all[nb // 2]
                        hg = nb % 2
                        dst = ar_t.ap()[:, o_expP + kc * 4 * TILE + hg * 2 * TILE:
                                        o_expP + kc * 4 * TILE + (hg + 1) * 2 * TILE]
                        mkv = mlocS.ap()[:, kc * 256:(kc + 1) * 256].unsqueeze(1) \
                            .broadcast_to((128, 2, 256))
                        v.tensor_tensor(dst.rearrange("p (g q) -> p g q", g=2), 
                                        dst.rearrange("p (g q) -> p g q", g=2), mkv,
                                        op=ALU.mult)
                        I(ve, v.drain())
                locexp_done = ve.n

                if hf == 0:
                    emit_remote_tail()
                    emit_remgate()

                @block.tensor
                def _(t, hf=hf, slot=slot, pslot=pslot, n=n, kc_all=kc_all):
                    t.wait_ge(ve.h, locexp_done)
                    if hf == 0:
                        t.wait_ge(ve.h, remnorm_done)
                        t.wait_ge(sc.h, rem_marks['rempf'])
                    for qc in range(QC):
                        avreg = psB[:, 0:1024] if qc == 0 else psA[:, 0:1024]
                        dreg = psC[:, 0:16] if qc == 0 else psD[:, 0:16]
                        for hh in range(4):
                            h = hf * 4 + hh
                            for ki, kc in enumerate(kc_all):
                                st = ar_t.ap()[:, o_expP + kc * 4 * TILE + hh * TILE + qc * 128:
                                               o_expP + kc * 4 * TILE + hh * TILE + qc * 128 + 128]
                                vsl = (slot * 2 + kc - 2) if kc >= 2 else (pslot * 2 + kc)
                                first = ki == 0
                                last = ki == len(kc_all) - 1
                                mmA = t.matmul(avreg[:, h * 64:(h + 1) * 64], st,
                                               V(o_VR + vsl * 1024 + h * 64, 64),
                                               start=first, stop=last)
                                mmB = t.matmul(dreg[:, h:h + 1], st,
                                               onesS[:, 0:1], start=first, stop=last)
                        I(te, mmA)
                        mix_marks[(hf, qc)] = I(te, mmB)

            exp_last_sc_d[ti] = sc.n

            @block.vector
            def _(v, par=par):
                for qc in range(QC):
                    v.wait_ge(te.h, mix_marks[(3, qc)])
                    if qc == 0:
                        v.wait_ge(dx2s.h, dx2s.n)
                    avreg = psB[:, 0:1024] if qc == 0 else psA[:, 0:1024]
                    dreg = psC[:, 0:16] if qc == 0 else psD[:, 0:16]
                    v.reciprocal(rl16[:, qc * 16:(qc + 1) * 16], dreg)
                    v.drain()
                    v.tensor_tensor(rl16[:, qc * 16:(qc + 1) * 16],
                                    glgrT[:, qc * 32:qc * 32 + 16],
                                    rl16[:, qc * 16:(qc + 1) * 16], op=ALU.mult)
                    v.drain()
                    rlv = rl16.ap()[:, qc * 16:(qc + 1) * 16].unsqueeze(2).broadcast_to((128, 16, 64))
                    v.tensor_tensor(
                        V(o_x2st + qc * 1024, 1024).rearrange("p (h e) -> p h e", h=H),
                        avreg.rearrange("p (h e) -> p h e", h=H), rlv, op=ALU.mult)
                    v.drain()
                    v.tensor_tensor(V(o_x2st + qc * 1024, 1024),
                                    V(o_x2st + qc * 1024, 1024),
                                    V(o_aoTM + qc * 1024, 1024), op=ALU.add)
                    I(ve, v.drain())
            mix_done = ve.n
            mix_done_d[ti] = mix_done

            if ti + 1 < B * NT:
                emit_vq(ti + 1)

            @block.sync
            def _(sy, par=par):
                sy.wait_ge(ve.h, mix_done)
                trbatch(sy, par, [
                    (V(o_mixF, NCH * TILE).rearrange("p (c t) -> p c t", c=NCH)[:, :, qc * 128:(qc + 1) * 128],
                     V(o_x2st + qc * 1024, 1024)) for qc in range(QC)])
            mixf_done = dtr[par].n

            te_wo = {}

            @block.tensor
            def _(t, par=par):
                t.wait_ge(dtr[par].h, mixf_done)
                ve0wo = ve.n
                for oc in range(NCH):
                    reg = psB[:, (oc % 4) * 256:(oc % 4) * 256 + 256]
                    if oc >= 4:
                        t.wait_ge(ve.h, ve0wo + ((oc - 4) | 1) + 1)
                    for c in range(NCH):
                        mm = t.matmul(reg, wv_(o_wo + c * 1024 + oc * 128, 128),
                                      V(o_mixF + c * TILE, TILE),
                                      start=(c == 0), stop=(c == NCH - 1))
                    te_wo[oc] = I(te, mm)
            te_after_tile[ti] = te.n

            @block.vector
            def _(v):
                for oc in range(NCH):
                    v.wait_ge(te.h, te_wo[min(oc | 1, NCH - 1)])
                    reg = psB[:, (oc % 4) * 256:(oc % 4) * 256 + 256]
                    aov = V(o_aoF, QC * 1024).rearrange(
                        "p (qc c f) -> p qc c f", qc=QC, c=NCH)[:, :, oc, :]
                    v.tensor_scalar(aov, reg.rearrange("p (qc f) -> p qc f", qc=QC),
                                    1.0, None, ALU.mult)
                    I(ve, v.drain())
            aof_done = ve.n
            aof_done_d[ti] = aof_done

            def emit_tail_store(par=par, ti=ti, xtv=xtv, b=b, q0=q0,
                                aof_done=aof_done):
                @block.sync
                def _(sy):
                    sy.wait_ge(ve.h, aof_done)
                    trbatch(sy, par, [
                        (V(o_x2st + qc * 1024, 1024).rearrange("p (c f) -> p c f", c=NCH),
                         V(o_aoF + qc * 1024, 1024)) for qc in range(QC)])
                aotm_done = dtr[par].n

                @block.vector
                def _(v):
                    v.wait_ge(dtr[par].h, aotm_done)
                    for qc in range(QC):
                        v.tensor_tensor(V(o_x2st + qc * 1024, 1024),
                                        xtv[:, qc * 1024:(qc + 1) * 1024],
                                        V(o_x2st + qc * 1024, 1024), op=ALU.add)
                        v.drain()
                    I(ve, v.drain())
                    ve_after_xuse[ti] = ve.n
                x2_ready = ve.n

                @block.sync
                def _(sy):
                    sy.wait_ge(ve.h, x2_ready)
                    I(dx2s, sy.dma_start(
                        x2_dram[b, q0:q0 + TILE, :].rearrange("(tc p) f -> p tc f", p=128),
                        V(o_x2st, QC * 1024).rearrange("p (tc f) -> p tc f", tc=QC)), 16)

            pend_tail[0] = emit_tail_store
            if ti + 1 >= min(B * NT, DBG_A_TILES):
                emit_tail_store()
                pend_tail[0] = None

        ve_phA = ve.n
        sc_phA = sc.n
        te_phA = te.n

        deq_sc = {}
        deq_ve = {}
        deq_te = {}
        scc_mark = {}
        q_mark = {}
        rsel_mark = {}
        deq_te_pred = {}
        deq_ve_pred = {}
        _ten = te.n
        _ven = ve.n
        _scn = sc.n
        deq_sc_pred = {}
        for _i in range(6):
            for _c in range(NCH):
                _ten += 1
                _ven += 1
                _scn += 1
                deq_te_pred[(_i, _c)] = _ten
                deq_ve_pred[(_i, _c)] = _ven
                deq_sc_pred[(_i, _c)] = _scn

        @block.gpsimd
        def _(g):
            if not DBG_B_PRO:
                return
            I(dwa, g.dma_start(laS.ap().rearrange("p (i c r) -> p i c r", i=6, c=NCH),
                               laT_d.rearrange("i (c p) r -> p i c r", p=128)), 16)
            I(dwa, g.dma_start(lbS.ap().rearrange("p (i o) -> p i o", i=3),
                               lb4_d.transpose([1, 0, 2])), 16)
            I(dwa, g.dma_start(biasS.ap().rearrange("p (i c) -> p i c", i=6),
                               biasT_d.transpose([1, 0, 2])), 16)
            I(dwa, g.dma_start(lngS.ap().rearrange("p (i c) -> p i c", i=3),
                               lngT_d.transpose([1, 0, 2])), 16)
            I(dwa, g.dma_start(lnbS.ap().rearrange("p (i c) -> p i c", i=3),
                               lnbT_d.transpose([1, 0, 2])), 16)
            g.wait_ge(ve.h, ve_phA)
            g.wait_ge(sc.h, sc_phA)
            g.wait_ge(te.h, te_phA)
            g.wait_ge(dx2s.h, dx2s.n)
            g.wait_ge(dtr[0].h, dtr[0].n)
            g.wait_ge(dtr[1].h, dtr[1].n)
            rsel_mark[0] = I(drsel, g.dma_start(
                ar_t.ap()[0:64, o_rsel:o_rsel + 1024], rsel_d[:, :]), 16)
            for i in range(6):
                if i >= 2:
                    g.wait_ge(te.h, deq_te_pred[(i - 2, NCH - 1)])
                scc_mark[i] = I(dscc[i % 2], g.dma_start(
                    ar_t.ap()[0:64, o_scc + (i % 2) * 1024:o_scc + (i % 2) * 1024 + 1024],
                    scT_d[i]), 16)
                for c in range(NCH):
                    k = i * NCH + c
                    par = k % 2
                    if k >= 2:
                        pk = k - 2
                        g.wait_ge(ve.h, deq_ve_pred[(pk // NCH, pk % NCH)])
                    q_mark[(i, c)] = I(dwb[par], g.dma_start(
                        V(o_qtmp + par * 1024, 1024),
                        qT_d[i, c * 128:(c + 1) * 128, :]), 16)

        @block.tensor
        def _(t):
            if not DBG_B_PRO:
                return
            t.wait_ge(drsel.h, rsel_mark[0])
            for i in range(6):
                t.wait_ge(dscc[i % 2].h, scc_mark[i])
                for c in range(NCH):
                    k = i * NCH + c
                    reg = psA[:, (k % 2) * 1024:(k % 2) * 1024 + 1024]
                    if k >= 2:
                        pk = k - 2
                        t.wait_ge(sc.h, deq_sc_pred[(pk // NCH, pk % NCH)])
                    for og in range(2):
                        mm = t.matmul(reg[:, og * 512:(og + 1) * 512],
                                      ar_t.ap()[0:64, o_rsel + c * 128:o_rsel + (c + 1) * 128],
                                      ar_t.ap()[0:64, o_scc + (i % 2) * 1024 + og * 512:
                                                o_scc + (i % 2) * 1024 + (og + 1) * 512],
                                      start=True, stop=True)
                    deq_te[(i, c)] = I(te, mm)
                    assert deq_te[(i, c)] == deq_te_pred[(i, c)]

        @block.scalar
        def _(s):
            if not DBG_B_PRO:
                return
            for i in range(6):
                for c in range(NCH):
                    k = i * NCH + c
                    par = k % 2
                    s.wait_ge(te.h, deq_te[(i, c)])
                    if k >= 2:
                        pk = k - 2
                        s.wait_ge(ve.h, deq_ve_pred[(pk // NCH, pk % NCH)])
                    s.activation(V(o_sctmp + par * 1024, 1024),
                                 psA[:, par * 1024:par * 1024 + 1024], AF.Copy)
                    s.drain()
                    s.wait_ge(dwb[par].h, q_mark[(i, c)])
                    deq_sc[(i, c)] = I(sc, s.activation(V(o_aftmp + par * 1024, 1024),
                                                        V(o_qtmp + par * 1024, 1024),
                                                        AF.Copy, bias=-1.0, scale=1.0 / 7.5))
                    assert deq_sc[(i, c)] == deq_sc_pred[(i, c)]

        @block.vector
        def _(v):
            if not DBG_B_PRO:
                return
            v.wait_ge(te.h, te_phA)
            for i in range(6):
                for c in range(NCH):
                    k = i * NCH + c
                    par = k % 2
                    v.wait_ge(sc.h, deq_sc[(i, c)])
                    deq_ve[(i, c)] = I(ve, v.tensor_tensor(
                        wbig.ap()[:, i * 8192 + c * 1024:i * 8192 + (c + 1) * 1024],
                        V(o_aftmp + par * 1024, 1024),
                        V(o_sctmp + par * 1024, 1024), op=ALU.mult))
                    assert deq_ve[(i, c)] == deq_ve_pred[(i, c)]
        deq_done = ve.n

        wTv = lambda i, c, lo, nn: wbig.ap()[:, i * 8192 + c * 1024 + lo:i * 8192 + c * 1024 + lo + nn]

        pbank = {}
        pbank['A0'] = ('sc', deq_sc[(5, 6)])
        pbank['A1'] = ('sc', deq_sc[(5, 6)])
        pbank['A2'] = ('sc', deq_sc[(5, 7)])
        pbank['A3'] = ('sc', deq_sc[(5, 7)])
        pbank['B4'] = ('ve', aof_done_d[B * NT - 1]) if (B * NT - 1) in aof_done_d else None
        pbank['B5'] = pbank['B4']
        pbank['C6'] = ('ve', mix_done_d[B * NT - 1]) if (B * NT - 1) in mix_done_d else None
        pbank['D7'] = pbank['C6']
        pbank = {k: v for k, v in pbank.items() if v}

        def bank_wait(t, b):
            if b in pbank:
                eng, val = pbank[b]
                t.wait_ge({'ve': ve.h, 'sc': sc.h, 'te': te.h,
                           'ev1': ev1.h, 'ev2': ev2.h}[eng], val)

        def bank_set(b, eng, val):
            pbank[b] = (eng, val)

        YREG = {'A0': (psA, 0), 'A1': (psA, 512), 'A2': (psA, 1024), 'A3': (psA, 1536),
                'B4': (psB, 0), 'B5': (psB, 512)}

        ln2_done_d = {}
        h0_done_d = {}
        hfin_ve_d = {}
        hfin_ev2_d = {}
        htm_done_d = {}
        var_ve_d = {}
        nbt_total = min(B * NT, DBG_B_TILES) if DBG_B_PRO else 0

        X2SLOT = lambda tj: (o_x2t + (tj % 3) * NCH * TILE
                             if tj % 3 < 2 else o_x2t3)

        def emit_x2load(tj):
            bj, Tj = divmod(tj, NT)
            pj = tj % 2

            @block.gpsimd
            def _(g, tj=tj, bj=bj, Tj=Tj, pj=pj):
                if tj == 0:
                    g.wait_ge(dx2s.h, dx2s.n)
                if tj >= 3:
                    g.wait_ge(ve.h, ve_after_x2[tj - 3])
                g.wait_ge(dx2l[pj].h, dx2l[pj].n)
                I(dx2l[pj], g.dma_start(
                    V(X2SLOT(tj), NCH * TILE).rearrange("p (tc f) -> p tc f", tc=QC),
                    x2_dram[bj, Tj * TILE:(Tj + 1) * TILE, :].rearrange(
                        "(tc p) f -> p tc f", p=128)), 16)

        def emit_ln2(tj):
            pj = tj % 2
            x2vj = V(X2SLOT(tj), NCH * TILE)
            ve_stats2 = {}

            @block.vector
            def _(v, tj=tj, pj=pj, x2vj=x2vj):
                v.wait_ge(dx2l[pj].h, dx2l[pj].n)
                if tj == 0:
                    v.wait_ge(ve.h, deq_done)
                    v.wait_ge(dwa.h, dwa.n)
                for tc in range(QC):
                    xv = x2vj[:, tc * 1024:(tc + 1) * 1024]
                    v.bn_stats(stat6[:, 0:6], xv[:, 0:512])
                    v.drain()
                    v.bn_stats(stat6[:, 6:12], xv[:, 512:1024])
                    v.drain()
                    v.bn_aggr(stat2[:, 2 * tc:2 * tc + 2], stat6[:, 0:12])
                    v.drain()
                    v.tensor_scalar(stat2[:, 2 * tc + 1:2 * tc + 2],
                                    stat2[:, 2 * tc + 1:2 * tc + 2], EPS, None, ALU.add)
                    v.drain()
                    ve_stats2[tc] = I(ve, v.drain())

            sc_sig2 = {}

            @block.scalar
            def _(s):
                for tc in range(QC):
                    s.wait_ge(ve.h, ve_stats2[tc])
                    sc_sig2[tc] = I(sc, s.activation(sigv[:, tc:tc + 1],
                                                     stat2[:, 2 * tc + 1:2 * tc + 2],
                                                     AF.Sqrt))

            @block.vector
            def _(v, x2vj=x2vj, tj=tj, pj=pj):
                if tj >= 1:
                    v.wait_ge(dtr[1 - pj].h, h0_done_d[tj - 1])
                for tc in range(QC):
                    v.wait_ge(sc.h, sc_sig2[tc])
                    v.reciprocal(invv[:, tc:tc + 1], sigv[:, tc:tc + 1])
                    v.drain()
                    v.tensor_scalar(V(o_xn2 + tc * 1024, 1024),
                                    x2vj[:, tc * 1024:(tc + 1) * 1024],
                                    stat2[:, 2 * tc:2 * tc + 1], invv[:, tc:tc + 1],
                                    ALU.subtract, ALU.mult)
                    v.drain()
                I(ve, v.drain())
            ln2_done_d[tj] = ve.n

        def emit_h0(tj):
            pj = tj % 2

            @block.sync
            def _(sy, tj=tj, pj=pj):
                sy.wait_ge(ve.h, ln2_done_d[tj])
                if tj >= 2:
                    sy.wait_ge(ev2.h, hfin_ev2_d[tj - 2])
                    sy.wait_ge(te.h, te_after_tile[(tj - 2, 'B')])
                ha = (2 * (tj % 2)) * NCH * TILE
                trbatch(sy, pj, [
                    (V(o_hF + ha, NCH * TILE).rearrange(
                        "p (c t) -> p c t", c=NCH)[:, :, tc * 128:(tc + 1) * 128],
                     V(o_xn2 + tc * 1024, 1024)) for tc in range(QC)])
            h0_done_d[tj] = dtr[pj].n

        if nbt_total > 0:
            emit_x2load(0)
            if nbt_total > 1:
                emit_x2load(1)
            emit_ln2(0)
            emit_h0(0)
            if nbt_total > 1:
                emit_ln2(1)
                emit_h0(1)

        Y1BANKS = ['A0', 'A0', 'A1', 'A1']
        Y2BANKS = ['A3', 'A3', 'B4', 'B4', 'B5', 'B5']
        STATS = {0: ('D7', psD, 0), 1: ('A2', psA, 1024)}

        Y1REGS = [psA[:, r * 256:(r + 1) * 256] for r in range(4)]
        Y2REGS = ([psA[:, 1536 + r * 256:1536 + (r + 1) * 256] for r in range(2)]
                  + [psB[:, r * 256:(r + 1) * 256] for r in range(4)])
        last_yout = {'ve': 0}

        def b_state(ti):
            b, T = divmod(ti, NT)
            pp = ti % 2
            return dict(
                ti=ti, b=b, T=T, pp=pp, q0=T * TILE,
                x2v=V(X2SLOT(ti), NCH * TILE),
                y1F=(o_y1F if pp == 0 else o_t1F),
                gF=(o_gF if pp == 0 else o_qtmp),
                sqF=(o_sqF if pp == 0 else o_aftmp),
                btr=(o_btr if pp == 0 else o_sctmp),
                brow=(o_brow if pp == 0 else o_sctmp + 1024),
                srow=(o_srow if pp == 0 else o_scc),
                ha=2 * pp, hb=2 * pp + 1,
                stb=STATS[pp],
            )

        def b_hio(st, sbk):
            a, bb = st['ha'], st['hb']
            hin = a if sbk != 1 else bb
            hout = bb if sbk != 1 else a
            return o_hF + hin * NCH * TILE, o_hF + hout * NCH * TILE

        def b_Y1a(st, sbk):
            i0 = 2 * sbk
            hin_o, _ = b_hio(st, sbk)
            ti, pp = st['ti'], st['pp']
            btr_o = st['btr']

            @block.tensor
            def _(t, i0=i0, hin_o=hin_o, sbk=sbk, ti=ti, pp=pp):
                if sbk == 0:
                    t.wait_ge(dtr[pp].h, h0_done_d[ti])
                    if ti == 0:
                        t.wait_ge(ve.h, deq_done)
                else:
                    t.wait_ge(ev2.h, st['h_ready'])
                bank_wait(t, 'C6')
                for c in range(NCH):
                    mm = t.matmul(psC[0:32, 0:TILE],
                                  laS[:, i0 * NCH * 32 + c * 32:i0 * NCH * 32 + (c + 1) * 32],
                                  V(hin_o + c * TILE, TILE),
                                  start=(c == 0), stop=(c == NCH - 1))
                I(te, mm)
            st['btr1_te'] = te.n

            @block.vector
            def _(v, i0=i0, btr_o=btr_o):
                v.wait_ge(te.h, st['btr1_te'])
                bp = (i0 % 2) * 64
                v.tensor_scalar(ar_t.ap()[bp:bp + 32, btr_o:btr_o + TILE],
                                psC[0:32, 0:TILE], 1.0, None, ALU.mult)
                bank_set('C6', 've', I(ve, v.drain()))
            st['btr1_ve'] = ve.n

        def b_Y1b(st, sbk):
            i0 = 2 * sbk
            hin_o, _ = b_hio(st, sbk)
            btr_o = st['btr']
            te_y1 = {}

            @block.tensor
            def _(t, i0=i0, hin_o=hin_o, btr_o=btr_o):
                ev0 = ev1.n
                for oc in range(NCH):
                    reg = Y1REGS[oc % 4]
                    if oc >= 4:
                        t.wait_ge(ev1.h, ev0 + ((oc - 4) | 1) + 1)
                    else:
                        bank_wait(t, Y1BANKS[oc % 4])
                    for c in range(NCH):
                        t.matmul(reg, wTv(i0, c, oc * 128, 128),
                                 V(hin_o + c * TILE, TILE),
                                 start=(c == 0), stop=False)
                    if oc == 0:
                        t.wait_ge(ve.h, st['btr1_ve'])
                    bp = (i0 % 2) * 64
                    mm = t.matmul(reg,
                                  lbS[bp:bp + 32,
                                      (i0 // 2) * 1024 + oc * 128:(i0 // 2) * 1024 + oc * 128 + 128],
                                  ar_t.ap()[bp:bp + 32, btr_o:btr_o + TILE], start=False, stop=True)
                    te_y1[oc] = I(te, mm)
            st['te_y1'] = te_y1

        def b_Y1(st, sbk):
            b_Y1a(st, sbk)
            b_Y1b(st, sbk)

        def b_L1e(st, sbk):
            i0 = 2 * sbk
            te_y1 = st['te_y1']
            y1o, sqo = st['y1F'], st['sqF']

            @block.vector
            def _(v, i0=i0, y1o=y1o):
                for oc in range(NCH):
                    v.wait_ge(te.h, te_y1[min(oc | 1, NCH - 1)])
                    v.tensor_scalar(V(y1o + oc * TILE, TILE),
                                    Y1REGS[oc % 4],
                                    biasS[:, i0 * 8 + oc:i0 * 8 + oc + 1], None, ALU.add)
                    bank_set(Y1BANKS[oc % 4], 'ev1', I(ev1, v.drain()))
            st['y1_evac'] = ev1.n

            @block.vector
            def _(v, y1o=y1o, sqo=sqo):
                v.tensor_tensor(V(sqo, NCH * TILE), V(y1o, NCH * TILE),
                                V(y1o, NCH * TILE), op=ALU.mult)
                I(ve, v.drain())
            st['sq_ve'] = ve.n

        def b_L1m(st, sbk):
            y1o, sqo = st['y1F'], st['sqF']
            sb_name, sb_ps, sb_off = st['stb']

            @block.tensor
            def _(t, y1o=y1o, sqo=sqo, sb_ps=sb_ps, sb_off=sb_off):
                t.wait_ge(ev1.h, st['y1_evac'])
                bank_wait(t, sb_name)
                for c in range(NCH):
                    mm = t.matmul(sb_ps[0:1, sb_off:sb_off + TILE], onesS[:, 0:1],
                                  V(y1o + c * TILE, TILE),
                                  start=(c == 0), stop=(c == NCH - 1))
                I(te, mm)
                t.wait_ge(ve.h, st['sq_ve'])
                for c in range(NCH):
                    mm = t.matmul(sb_ps[0:1, sb_off + 256:sb_off + 256 + TILE], onesS[:, 0:1],
                                  V(sqo + c * TILE, TILE),
                                  start=(c == 0), stop=(c == NCH - 1))
                I(te, mm)
            st['sums_te'] = te.n

        def b_L1v(st, sbk):
            sro, bro = st['srow'], st['brow']
            sb_name, sb_ps, sb_off = st['stb']
            sr = lambda a, bn: ar_t.ap()[0:1, sro + 2 * a:sro + 2 * a + 2 * bn].bitcast(F32)
            br = lambda a, bn: ar_t.ap()[0:1, bro + a:bro + a + bn]

            @block.vector
            def _(v, sb_ps=sb_ps, sb_off=sb_off):
                v.wait_ge(te.h, st['sums_te'])
                v.tensor_scalar(br(0, TILE), sb_ps[0:1, sb_off:sb_off + TILE],
                                -1.0 / D, None, ALU.mult)
                v.drain()
                v.tensor_tensor(sr(0, TILE), br(0, TILE), br(0, TILE), op=ALU.mult)
                v.drain()
                v.scalar_tensor_tensor(sr(TILE, TILE),
                                       sb_ps[0:1, sb_off + 256:sb_off + 256 + TILE],
                                       1.0 / D, sr(0, TILE),
                                       op0=ALU.mult, op1=ALU.subtract)
                v.drain()
                v.tensor_scalar(sr(TILE, TILE), sr(TILE, TILE), EPS, None, ALU.add)
                I(ve, v.drain())
            st['var_ve'] = ve.n

            @block.scalar
            def _(s):
                s.wait_ge(ve.h, st['var_ve'])
                with nc.allow_low_precision(reason="bf16 LN sigma for broadcast matmul"):
                    I(sc, s.activation(br(TILE, TILE), sr(TILE, TILE), AF.Sqrt))
            st['sig_sc'] = sc.n

        def b_L1c(st, sbk):
            bro, pp = st['brow'], st['pp']
            sb_name, sb_ps, sb_off = st['stb']
            br = lambda a, bn: ar_t.ap()[0:1, bro + a:bro + a + bn]

            @block.tensor
            def _(t, sb_ps=sb_ps, sb_off=sb_off):
                t.wait_ge(sc.h, st['sig_sc'])
                mm = t.matmul(sb_ps[:, sb_off:sb_off + 512], onesR[0:1, :], br(0, 512),
                              start=True, stop=True)
                I(te, mm)
            st['bc_te'] = te.n

            @block.vector
            def _(v, sb_ps=sb_ps, sb_off=sb_off, pp=pp):
                v.wait_ge(te.h, st['bc_te'])
                v.reciprocal(V(o_invb + pp * 512, 512).bitcast(F32),
                             sb_ps[:, sb_off + TILE:sb_off + 2 * TILE])
                I(ve, v.drain())
            st['inv_ve'] = ve.n

        def b_L2(st, sbk):
            y1o, go = st['y1F'], st['gF']
            sb_name, sb_ps, sb_off = st['stb']

            @block.vector
            def _(v, sbk=sbk, sb_ps=sb_ps, sb_off=sb_off, pp=st['pp']):
                v.wait_ge(te.h, st['bc_te'])
                mv = sb_ps.ap()[:, sb_off:sb_off + TILE].unsqueeze(1).broadcast_to(
                    (128, NCH, TILE))
                v.tensor_tensor(V(y1o, NCH * TILE).rearrange("p (c t) -> p c t", c=NCH),
                                V(y1o, NCH * TILE).rearrange("p (c t) -> p c t", c=NCH),
                                mv, op=ALU.add)
                v.drain()
                iv = V(o_invb + pp * 512, 512).bitcast(F32).unsqueeze(1).broadcast_to(
                    (128, NCH, TILE))
                v.tensor_tensor(V(go, NCH * TILE).rearrange("p (c t) -> p c t", c=NCH),
                                V(y1o, NCH * TILE).rearrange("p (c t) -> p c t", c=NCH),
                                iv, op=ALU.mult)
                I(ve, v.drain())
            st['t1_ve'] = ve.n
            bank_set(sb_name, 've', st['t1_ve'])

        def b_Ga(st, sbk):
            go = st['gF']

            @block.scalar
            def _(s, sbk=sbk, go=go, sro=st['srow']):
                s.wait_ge(ve.h, st['t1_ve'])
                for c in range(NCH):
                    I(sc, s.activation(
                        V(go + c * TILE, TILE), V(go + c * TILE, TILE), AF.Gelu,
                        bias=lnbS[:, sbk * 8 + c:sbk * 8 + c + 1],
                        scale=lngS[:, sbk * 8 + c:sbk * 8 + c + 1]))
                st['g_done'] = sc.n
                I(sc, s.activation(ar_t.ap()[0:1, sro + 4 * TILE:sro + 4 * TILE + 1],
                                   onesS[0:1, 0:1], AF.Sqrt))

        def b_Y2(st, sbk):
            i1 = 2 * sbk + 1
            go, btr_o = st['gF'], st['btr']

            @block.tensor
            def _(t, i1=i1, go=go):
                t.wait_ge(sc.h, st['g_done'])
                bank_wait(t, 'C6')
                for c in range(NCH):
                    mm = t.matmul(psC[0:32, 0:TILE],
                                  laS[:, i1 * NCH * 32 + c * 32:i1 * NCH * 32 + (c + 1) * 32],
                                  V(go + c * TILE, TILE),
                                  start=(c == 0), stop=(c == NCH - 1))
                I(te, mm)
            st['btr2_te'] = te.n

            @block.vector
            def _(v, i1=i1, btr_o=btr_o):
                v.wait_ge(te.h, st['btr2_te'])
                bp = (i1 % 2) * 64
                v.tensor_scalar(ar_t.ap()[bp:bp + 32, btr_o:btr_o + TILE],
                                psC[0:32, 0:TILE], 1.0, None, ALU.mult)
                bank_set('C6', 've', I(ve, v.drain()))
            st['btr2_ve'] = ve.n

            te_y2 = {}

            @block.tensor
            def _(t, i1=i1, go=go, btr_o=btr_o):
                ev0 = ev2.n
                for oc in range(NCH):
                    reg = Y2REGS[oc % 6]
                    if oc >= 6:
                        t.wait_ge(ev2.h, ev0 + ((oc - 6) | 1) + 1)
                    else:
                        bank_wait(t, Y2BANKS[oc % 6])
                    for c in range(NCH):
                        t.matmul(reg, wTv(i1, c, oc * 128, 128),
                                 V(go + c * TILE, TILE),
                                 start=(c == 0), stop=False)
                    if oc == 0:
                        t.wait_ge(ve.h, st['btr2_ve'])
                    bp = (i1 % 2) * 64
                    mm = t.matmul(reg,
                                  lbS[bp:bp + 32,
                                      (i1 // 2) * 1024 + oc * 128:(i1 // 2) * 1024 + oc * 128 + 128],
                                  ar_t.ap()[bp:bp + 32, btr_o:btr_o + TILE], start=False, stop=True)
                    te_y2[oc] = I(te, mm)
            st['te_y2'] = te_y2

        def b_E2(st, sbk):
            i1 = 2 * sbk + 1
            hin_o, hout_o = b_hio(st, sbk)
            te_y2 = st['te_y2']
            ti, pp = st['ti'], st['pp']

            @block.vector
            def _(v, i1=i1, hin_o=hin_o, hout_o=hout_o, sbk=sbk, ti=ti, pp=pp):
                if sbk == 0 and ti >= 2:
                    v.wait_ge(dtr[pp].h, htm_done_d[ti - 2])
                for oc in range(NCH):
                    v.wait_ge(te.h, te_y2[min(oc | 1, NCH - 1)])
                    reg = Y2REGS[oc % 6]
                    if sbk < 2:
                        v.scalar_tensor_tensor(V(hout_o + oc * TILE, TILE), reg,
                                               biasS[:, i1 * 8 + oc:i1 * 8 + oc + 1],
                                               V(hin_o + oc * TILE, TILE),
                                               op0=ALU.add, op1=ALU.add)
                        bank_set(Y2BANKS[oc % 6], 'ev2', I(ev2, v.drain()))
                    else:
                        hov = V(hout_o, QC * 1024).rearrange(
                            "p (qc c f) -> p qc c f", qc=QC, c=NCH)[:, :, oc, :]
                        v.scalar_tensor_tensor(
                            hov, reg.rearrange("p (qc f) -> p qc f", qc=QC),
                            biasS[:, i1 * 8 + oc:i1 * 8 + oc + 1],
                            V(hin_o + oc * TILE, TILE).rearrange(
                                "p (qc f) -> p qc f", qc=QC),
                            op0=ALU.add, op1=ALU.add)
                        bank_set(Y2BANKS[oc % 6], 'ev2', I(ev2, v.drain()))
            st['h_ready'] = ev2.n
            if sbk == 2:
                te_after_tile[(st['ti'], 'B')] = te.n
                hfin_ev2_d[st['ti']] = ev2.n

        def b_tail(st):
            ti, pp, b, q0 = st['ti'], st['pp'], st['b'], st['q0']
            x2v = st['x2v']
            hfin_o = o_hF + st['hb'] * NCH * TILE

            @block.sync
            def _(sy, pp=pp, hfin_o=hfin_o):
                sy.wait_ge(ev2.h, ev2.n)
                sy.wait_ge(ve.h, last_yout['ve'])
                trbatch(sy, pp, [
                    (V(o_hTM + qc * 1024, 1024).rearrange("p (c f) -> p c f", c=NCH),
                     V(hfin_o + qc * 1024, 1024)) for qc in range(QC)])
            htm_done_d[ti] = dtr[pp].n

            @block.vector
            def _(v, pp=pp, ti=ti, x2v=x2v):
                v.wait_ge(dtr[pp].h, htm_done_d[ti])
                v.wait_ge(dys.h, dys.n)
                for qc in range(QC):
                    v.tensor_tensor(V(o_yout + qc * 2048, 2048).bitcast(F32),
                                    x2v[:, qc * 1024:(qc + 1) * 1024],
                                    V(o_hTM + qc * 1024, 1024), op=ALU.add)
                    v.drain()
                I(ve, v.drain())
                ve_after_x2[ti] = ve.n
            last_yout['ve'] = ve.n

            @block.gpsimd
            def _(g, b=b, q0=q0):
                g.wait_ge(ve.h, last_yout['ve'])
                I(dys, g.dma_start(
                    y_out[b, q0:q0 + TILE, :].rearrange("(tc p) f -> p tc f", p=128),
                    V(o_yout, 2 * QC * 1024).bitcast(F32).rearrange(
                        "p (tc f) -> p tc f", tc=QC)), 16)

        assert nbt_total % 2 == 0 or nbt_total == 0
        for pair in range(nbt_total // 2):
            A = b_state(2 * pair)
            Bst = b_state(2 * pair + 1)
            b_Y1(A, 0)
            b_L1e(A, 0)
            b_Y1(Bst, 0)
            b_L1e(Bst, 0)
            for k in range(3):
                b_L1m(A, k)
                b_L1v(A, k)
                if k > 0:
                    b_Y1a(Bst, k)
                b_L1c(A, k)
                if k > 0:
                    b_Y1b(Bst, k)
                    b_L2(A, k)
                    b_L1e(Bst, k)
                else:
                    b_L2(A, k)
                if k == 1 and 2 * pair + 2 < nbt_total:
                    emit_x2load(2 * pair + 2)
                b_Ga(A, k)
                b_Y2(A, k)
                b_E2(A, k)
                b_L1m(Bst, k)
                b_L1v(Bst, k)
                if k < 2:
                    b_Y1a(A, k + 1)
                b_L1c(Bst, k)
                if k < 2:
                    b_Y1b(A, k + 1)
                    b_L2(Bst, k)
                    b_L1e(A, k + 1)
                else:
                    b_L2(Bst, k)
                    b_tail(A)
                b_Ga(Bst, k)
                b_Y2(Bst, k)
                b_E2(Bst, k)
            b_tail(Bst)
            if 2 * pair + 2 < nbt_total:
                emit_ln2(2 * pair + 2)
                emit_h0(2 * pair + 2)
            if 2 * pair + 3 < nbt_total:
                emit_x2load(2 * pair + 3)
                emit_ln2(2 * pair + 3)
                emit_h0(2 * pair + 3)

        @block.sync
        def _(sy):
            sy.wait_ge(dys.h, dys.n)

    return nc


def _host_inputs(L0_attn_x, L0_attn_wq, L0_attn_wk, L0_attn_wv, L0_attn_wgq,
                 L0_attn_wgk, L0_attn_wgv, L0_attn_r_out_w, L0_attn_gate_local_w,
                 L0_attn_gate_remote_w, L0_attn_w_out, L0_ffn_qweights,
                 L0_ffn_scales, L0_ffn_biases, L0_ffn_lora_a, L0_ffn_lora_b,
                 L0_ffn_ln_gamma, L0_ffn_ln_beta):
    f32 = np.float32
    x = np.ascontiguousarray(np.asarray(L0_attn_x, dtype=f32))

    gl = np.asarray(L0_attn_gate_local_w, dtype=f32)
    gr = np.asarray(L0_attn_gate_remote_w, dtype=f32)
    glgr = np.zeros((64, 32), dtype=f32)
    for h in range(H):
        glgr[:, h * 2] = gl[h]
        glgr[:, h * 2 + 1] = gr[h]

    qTi = np.ascontiguousarray(np.asarray(L0_ffn_qweights, dtype=np.int32).transpose(0, 2, 1))
    qT = np.ascontiguousarray(qTi.view(np.int8)[:, :, 0::4])
    scT = np.ascontiguousarray(np.asarray(L0_ffn_scales, dtype=f32).transpose(0, 2, 1))
    rsel = np.zeros((64, NCH * 128), dtype=f32)
    for c in range(NCH):
        for p in range(128):
            rsel[8 * c + p // 16, c * 128 + p] = 1.0
    laT = np.ascontiguousarray(np.asarray(L0_ffn_lora_a, dtype=f32).transpose(0, 2, 1))
    lbT = np.asarray(L0_ffn_lora_b, dtype=f32).transpose(0, 2, 1)
    lb4 = np.zeros((3, 128, D), dtype=f32)
    for i in range(6):
        lb4[i // 2, (i % 2) * 64:(i % 2) * 64 + 32, :] = lbT[i]
    biasT = np.ascontiguousarray(
        np.asarray(L0_ffn_biases, dtype=f32).reshape(6, 8, 128).transpose(0, 2, 1))
    lngT = np.ascontiguousarray(
        np.asarray(L0_ffn_ln_gamma, dtype=f32).reshape(3, 8, 128).transpose(0, 2, 1))
    lnbT = np.ascontiguousarray(
        np.asarray(L0_ffn_ln_beta, dtype=f32).reshape(3, 8, 128).transpose(0, 2, 1))

    i_ = np.arange(128)
    m1 = (i_[None, :] < i_[:, None]).astype(f32)
    m2 = (i_[None, :] >= i_[:, None]).astype(f32)
    ones_t = np.ones((128, 128), f32)
    zeros_t = np.zeros((128, 128), f32)
    mloc = np.zeros((4, 128, 256), dtype=f32)
    mloc[0] = np.concatenate([m1, zeros_t], axis=1)
    mloc[1] = np.concatenate([ones_t, m1], axis=1)
    mloc[2] = np.concatenate([m2, ones_t], axis=1)
    mloc[3] = np.concatenate([zeros_t, m2], axis=1)
    mlat = np.zeros((NT, NL, TILE), dtype=f32)
    for T in range(NT):
        qabs = T * TILE + np.arange(TILE)
        mlat[T] = (np.arange(NL)[:, None] <= (qabs[None, :] // POOL)).astype(f32)
    idm = np.eye(128, dtype=f32)
    onesv = np.zeros((128, 2), dtype=f32)
    onesv[:, 0] = 1.0
    onesr = np.ones((1, 128), dtype=f32)

    com = dict(
        wq=np.ascontiguousarray(np.asarray(L0_attn_wq, dtype=f32)),
        wk=np.ascontiguousarray(np.asarray(L0_attn_wk, dtype=f32)),
        wv=np.ascontiguousarray(np.asarray(L0_attn_wv, dtype=f32)),
        wgq=np.ascontiguousarray(np.asarray(L0_attn_wgq, dtype=f32)),
        wgk=np.ascontiguousarray(np.asarray(L0_attn_wgk, dtype=f32)),
        wgv=np.ascontiguousarray(np.asarray(L0_attn_wgv, dtype=f32)),
        rout=np.ascontiguousarray(np.asarray(L0_attn_r_out_w, dtype=f32)),
        wout=np.ascontiguousarray(np.asarray(L0_attn_w_out, dtype=f32)),
        glgr=glgr, qT=qT, scT=scT, rsel=rsel, laT=laT, lb4=lb4, biasT=biasT,
        lngT=lngT, lnbT=lnbT, mloc=mloc, mlat=mlat, idm=idm, onesv=onesv,
        onesr=onesr,
    )
    in_maps = []
    for c in range(NCORES):
        m = dict(com)
        m["x_in"] = np.ascontiguousarray(x[c * B:(c + 1) * B])
        in_maps.append(m)
    return in_maps


def kernel(**inputs):
    in_maps = _host_inputs(**inputs)
    nc = build_program()
    res = run_bass_kernel_spmd(nc, in_maps, core_ids=list(range(NCORES)))
    return np.concatenate([r["y"] for r in res.results], axis=0)


# version 20
# speedup vs baseline: 1.0169x; 1.0169x over previous
import contextlib
import numpy as np

import concourse.bass as bass
import concourse.mybir as mybir
from concourse.bass_utils import run_bass_kernel_spmd
from concourse.alu_op_type import AluOpType as ALU

F32 = mybir.dt.float32
BF16 = mybir.dt.bfloat16
I32 = mybir.dt.int32
AF = mybir.ActivationFunctionType
AX = mybir.AxisListType.X

NCORES = 8
B = 2
S = 2048
D = 1024
NCH = 8
H = 16
HD = 64
LH = 8
NL = 64
POOL = 32
W = 256
TILE = 256
NT = S // TILE
QC = TILE // 128
NEG = -1.0e9
EPS = 1e-5
ISQ = 0.125

DBG_A_TILES = 16
DBG_B_TILES = 16
DBG_B_PRO = True
DBG_A_STEPS = 10 ** 9


class Sem:
    def __init__(self, es, nc, name):
        self.h = es.enter_context(nc.semaphore(name))
        self.n = 0


class Arena:

    def __init__(self, tensor, cols):
        self.t = tensor
        self.cols = cols
        self.off = {"P": 0}
        self.base_p = None

    def alloc(self, phase, ncols):
        if phase == "P":
            o = self.off["P"]
            self.off["P"] = o + ncols
            return o
        if self.base_p is None:
            self.base_p = self.off["P"]
        o = self.off.get(phase, self.base_p)
        self.off[phase] = o + ncols
        assert self.off[phase] <= self.cols, (phase, self.off[phase], self.cols)
        return o

    def view(self, off, ncols):
        return self.t.ap()[:, off:off + ncols]


def build_program():
    nc = bass.Bass("TRN2", target_bir_lowering=False)
    din = lambda name, shp, dtype: nc.dram_tensor(name, shp, dtype, kind="ExternalInput").ap()

    x_in = din("x_in", [B, S, D], F32)
    wq_d = din("wq", [D, D], F32)
    wk_d = din("wk", [D, D], F32)
    wv_d = din("wv", [D, D], F32)
    wgq_d = din("wgq", [D, 512], F32)
    wgk_d = din("wgk", [D, 512], F32)
    wgv_d = din("wgv", [D, 512], F32)
    rout_d = din("rout", [512, D], F32)
    wout_d = din("wout", [D, D], F32)
    glgr_d = din("glgr", [64, 32], F32)
    qT_d = din("qT", [6, D, D], mybir.dt.int8)
    scT_d = din("scT", [6, 64, D], F32)
    rsel_d = din("rsel", [64, NCH * 128], F32)
    laT_d = din("laT", [6, D, 32], F32)
    lb4_d = din("lb4", [3, 128, D], F32)
    biasT_d = din("biasT", [6, 128, 8], F32)
    lngT_d = din("lngT", [3, 128, 8], F32)
    lnbT_d = din("lnbT", [3, 128, 8], F32)
    mloc_d = din("mloc", [4, 128, 256], F32)
    mlat_d = din("mlat", [NT, NL, TILE], F32)
    id_d = din("idm", [128, 128], F32)
    ones_d = din("onesv", [128, 2], F32)
    onesr_d = din("onesr", [1, 128], F32)

    y_out = nc.dram_tensor("y", [B, S, D], F32, kind="ExternalOutput").ap()
    x2_dram = nc.dram_tensor("x2scr", [B, S, D], BF16, kind="Internal").ap()

    with contextlib.ExitStack() as es:
        E = es.enter_context
        sb = lambda name, shp, dtype: E(nc.sbuf_tensor(name, shp, dtype))

        WB = 6 * 8192
        wbig = sb("wbig", [128, WB], BF16)
        wv_ = lambda o, n: wbig.ap()[:, o:o + n]
        o_wq, o_wk, o_wv, o_wo = 0, 8192, 16384, 24576
        o_wgq, o_wgk, o_wgv, o_ro = 32768, 36864, 40960, 45056

        AC = 48640
        ar_t = sb("arena", [128, AC], BF16)
        ar = Arena(ar_t, AC)

        glgrS = sb("glgrS", [64, 32], BF16)
        mlocS = sb("mlocS", [128, 4 * 256], BF16)
        mlatS = sb("mlatS", [NL, NT * TILE], BF16)
        idS = sb("idS", [128, 128], BF16)
        onesS = sb("onesS", [128, 2], BF16)
        onesR = sb("onesR", [1, 128], BF16)
        laS = sb("laS", [128, 6 * NCH * 32], BF16)
        lbS = sb("lbS", [128, 3 * 1024], BF16)
        biasS = sb("biasS", [128, 6 * 8], F32)
        lngS = sb("lngS", [128, 3 * 8], F32)
        lnbS = sb("lnbS", [128, 3 * 8], F32)

        A = lambda n: ar.alloc("A", n)
        o_xt = A(2 * NCH * TILE)
        o_xn1 = A(NCH * TILE)
        o_XF = A(NCH * TILE)
        o_QF = A(H * TILE)
        o_GQF = A(LH * TILE)
        o_KR = A(H * 2 * W)
        o_VR = A(4 * 1024)
        o_expP = A(4 * 4 * TILE)
        o_expL = A(LH * TILE)
        o_XPOOL = A(NCH * NL)
        o_GKF = A(LH * NL)
        o_GVT = A(512)
        o_gvtmp = o_GVT
        o_remTM = A(QC * 1024)
        o_remNF = o_GQF
        o_remPF = A(NCH * TILE)
        o_aoF = o_remPF
        o_aoTM = A(QC * 1024)
        o_x2st = A(QC * 1024)
        o_mixF = A(NCH * TILE)

        Bv = lambda n: ar.alloc("B", n)
        o_x2t = Bv(2 * NCH * TILE)
        o_xn2 = Bv(NCH * TILE)
        o_hF = Bv(4 * NCH * TILE)
        o_y1F = Bv(NCH * TILE)
        o_sqF = Bv(NCH * TILE)
        o_t1F = Bv(NCH * TILE)
        o_gF = Bv(NCH * TILE)
        o_hTM = Bv(QC * 1024)
        o_qtmp = Bv(2 * 1024)
        o_sctmp = Bv(2 * 1024)
        o_aftmp = Bv(2 * 1024)
        o_scc = Bv(2 * 1024)
        o_rsel = Bv(1024)
        o_gF2 = Bv(NCH * TILE)
        o_btr = Bv(TILE)
        o_brow = Bv(2 * TILE)
        o_yout = Bv(2 * QC * 1024)
        o_srow = Bv(2 * 4 * TILE)
        o_x2t3 = Bv(NCH * TILE)
        o_invb = Bv(2 * 512)

        V = ar.view
        global OFFS
        OFFS = {k: v for k, v in locals().items() if k.startswith("o_")}

        glgrT = sb("glgrT", [128, QC * 32], F32)
        rl16 = sb("rl16", [128, QC * 16], F32)
        rr16 = sb("rr16", [128, QC * 16], F32)
        rden8 = sb("rden8", [128, QC * 8], F32)
        stat6 = sb("stat6", [128, 12], F32)
        stat2 = sb("stat2", [128, 2 * QC], F32)
        sigv = sb("sigv", [128, QC], F32)
        invv = sb("invv", [128, QC], F32)

        psA = E(nc.psum_tensor("psA", [128, 2048], F32))
        psB = E(nc.psum_tensor("psB", [128, 1024], F32))
        psC = E(nc.psum_tensor("psC", [128, 512], F32))
        psD = E(nc.psum_tensor("psD", [128, 512], F32))

        te = Sem(es, nc, "te")
        sc = Sem(es, nc, "sc")
        ve = Sem(es, nc, "ve")
        dwa = Sem(es, nc, "dwa")
        dx = [Sem(es, nc, f"dx{p}") for p in range(2)]
        dtr = [Sem(es, nc, f"dtr{p}") for p in range(2)]
        dgv = [Sem(es, nc, f"dgv{p}") for p in range(2)]
        dx2s = Sem(es, nc, "dx2s")
        dx2l = [Sem(es, nc, f"dx2l{p}") for p in range(2)]
        dwb = [Sem(es, nc, f"dwb{p}") for p in range(2)]
        dscc = [Sem(es, nc, f"dscc{p}") for p in range(2)]
        drsel = Sem(es, nc, "drsel")
        dys = Sem(es, nc, "dys")
        ev1 = Sem(es, nc, "ev1")
        ev2 = Sem(es, nc, "ev2")

        blk_raw = E(nc.Block())

        class _GatedBlock:
            def __init__(self, b):
                self._b = b
                self.count = 0

            def _gate(self, dec):
                def wrap(fn):
                    self.count += 1
                    if self.count <= DBG_A_STEPS:
                        return dec(fn)
                    return None
                return wrap

            @property
            def gpsimd(self):
                return self._gate(self._b.gpsimd)

            @property
            def vector(self):
                return self._gate(self._b.vector)

            @property
            def scalar(self):
                return self._gate(self._b.scalar)

            @property
            def tensor(self):
                return self._gate(self._b.tensor)

            @property
            def sync(self):
                return self._gate(self._b.sync)

        block = _GatedBlock(blk_raw)

        def I(sem, inst, k=1):
            sem.n += k
            inst.then_inc(sem.h, k)
            return sem.n

        def trbatch(sy, par, calls):
            sy.wait_ge(dtr[par].h, dtr[par].n)
            sy.wait_ge(dgv[0].h, dgv[0].n)
            sy.wait_ge(dgv[1].h, dgv[1].n)
            for dst, src in calls:
                sy.dma_start_transpose(dst, src).then_inc(dtr[par].h, 16)
                dtr[par].n += 16
            return dtr[par].n

        @block.gpsimd
        def _(g):
            r3 = lambda ap: ap.rearrange("(c p) o -> p c o", p=128)
            rr = lambda off, nch, w: V(off, nch * w).rearrange("p (c o) -> p c o", c=nch)
            wbr = lambda off, nch, w: wbig.ap()[:, off:off + nch * w].rearrange("p (c o) -> p c o", c=nch)
            I(dwa, g.dma_start(wbr(o_wq, NCH, 1024), r3(wq_d)), 16)
            I(dwa, g.dma_start(wbr(o_wk, NCH, 1024), r3(wk_d)), 16)
            I(dwa, g.dma_start(wbr(o_wv, NCH, 1024), r3(wv_d)), 16)
            I(dwa, g.dma_start(wbr(o_wo, NCH, 1024), r3(wout_d)), 16)
            I(dwa, g.dma_start(wbr(o_wgq, NCH, 512), r3(wgq_d)), 16)
            I(dwa, g.dma_start(wbr(o_wgk, NCH, 512), r3(wgk_d)), 16)
            I(dwa, g.dma_start(wbr(o_wgv, NCH, 512), r3(wgv_d)), 16)
            I(dwa, g.dma_start(wbr(o_ro, 4, 1024), r3(rout_d)), 16)
            I(dwa, g.dma_start(glgrS[:], glgr_d[:, :]), 16)
            I(dwa, g.dma_start(mlocS.ap().rearrange("p (c o) -> p c o", c=4),
                               mloc_d.transpose([1, 0, 2])), 16)
            I(dwa, g.dma_start(mlatS.ap().rearrange("p (c o) -> p c o", c=NT),
                               mlat_d.transpose([1, 0, 2])), 16)
            I(dwa, g.dma_start(idS[:], id_d[:, :]), 16)
            I(dwa, g.dma_start(onesS[:], ones_d[:, :]), 16)
            I(dwa, g.dma_start(onesR[:], onesr_d[:, :]), 16)
            I(dx[0], g.dma_start(
                V(o_xt, NCH * TILE).rearrange("p (tc f) -> p tc f", tc=QC),
                x_in[0, 0:TILE, :].rearrange("(tc p) f -> p tc f", p=128)), 16)

        @block.vector
        def _(v):
            v.wait_ge(dwa.h, dwa.n)
            v.memset(V(o_expP, 4 * 4 * TILE), 0.0)
            v.memset(V(o_expL, LH * TILE), 0.0)
            v.memset(ar_t.ap()[0:64, o_KR:o_KR + H * 2 * W], 0.0)
            v.memset(V(o_VR, 4 * 1024), 0.0)
            v.memset(ar_t.ap()[0:64, o_GKF:o_GKF + LH * NL], 0.0)
            v.memset(ar_t.ap()[0:64, o_GVT:o_GVT + 512], 0.0)
            v.memset(V(o_XPOOL, NCH * NL), 0.0)
            I(ve, v.drain())
        init_ve = ve.n

        te_after_latAV = {}
        te_after_tile = {}
        te_projq_d = {}
        kg_evac_d = {}
        exp_last_sc_d = {}
        qv_evac_d = {}
        sc_gate_d = {}
        ve_after_xuse = {}
        ve_after_x2 = {}
        ln1_done_d = {}
        xf_done_d = {}
        aof_done_d = {}
        mix_done_d = {}

        def emit_ln1_xf(tj):
            bj, Tj = divmod(tj, NT)
            pj = tj % 2
            xtvj = V(o_xt + pj * NCH * TILE, NCH * TILE)
            ve_stats = {}

            @block.vector
            def _(v, tj=tj, pj=pj, xtvj=xtvj):
                v.wait_ge(dx[pj].h, dx[pj].n)
                if tj == 0:
                    v.wait_ge(dwa.h, dwa.n)
                for tc in range(QC):
                    xv = xtvj[:, tc * 1024:(tc + 1) * 1024]
                    v.bn_stats(stat6[:, 0:6], xv[:, 0:512])
                    v.drain()
                    v.bn_stats(stat6[:, 6:12], xv[:, 512:1024])
                    v.drain()
                    v.bn_aggr(stat2[:, 2 * tc:2 * tc + 2], stat6[:, 0:12])
                    v.drain()
                    v.tensor_scalar(stat2[:, 2 * tc + 1:2 * tc + 2],
                                    stat2[:, 2 * tc + 1:2 * tc + 2], EPS, None, ALU.add)
                    v.drain()
                    ve_stats[tc] = I(ve, v.drain())

            sc_sig = {}

            @block.scalar
            def _(s):
                for tc in range(QC):
                    s.wait_ge(ve.h, ve_stats[tc])
                    sc_sig[tc] = I(sc, s.activation(sigv[:, tc:tc + 1],
                                                    stat2[:, 2 * tc + 1:2 * tc + 2],
                                                    AF.Sqrt))

            @block.vector
            def _(v, xtvj=xtvj):
                for tc in range(QC):
                    v.wait_ge(sc.h, sc_sig[tc])
                    v.reciprocal(invv[:, tc:tc + 1], sigv[:, tc:tc + 1])
                    v.drain()
                    v.tensor_scalar(V(o_xn1 + tc * 1024, 1024),
                                    xtvj[:, tc * 1024:(tc + 1) * 1024],
                                    stat2[:, 2 * tc:2 * tc + 1], invv[:, tc:tc + 1],
                                    ALU.subtract, ALU.mult)
                    v.drain()
                I(ve, v.drain())
            ln1_done_d[tj] = ve.n

            @block.sync
            def _(sy, tj=tj, pj=pj):
                sy.wait_ge(ve.h, ln1_done_d[tj])
                if tj >= 1:
                    sy.wait_ge(te.h, te_projq_d[tj - 1])
                    sy.wait_ge(ve.h, kg_evac_d[tj - 1])
                trbatch(sy, pj, [
                    (V(o_XF, NCH * TILE).rearrange("p (c t) -> p c t", c=NCH)[:, :, tc * 128:(tc + 1) * 128],
                     V(o_xn1 + tc * 1024, 1024)) for tc in range(QC)])
            xf_done_d[tj] = dtr[pj].n

        def emit_vq(tj):
            pj = tj % 2
            slotj = (tj % NT) % 2
            te_vq = {}
            vev = {}

            @block.tensor
            def _(t, pj=pj, tj=tj):
                t.wait_ge(dtr[pj].h, xf_done_d[tj])
                if tj == 0:
                    t.wait_ge(ve.h, init_ve)
                else:
                    t.wait_ge(sc.h, exp_last_sc_d[tj - 1])
                ve0v = ve.n
                for tc in range(QC):
                    for og in range(2):
                        if tc == 1:
                            t.wait_ge(ve.h, ve0v + og + 1)
                        reg = psA[:, 1024 + og * 512:1024 + (og + 1) * 512]
                        for c in range(NCH):
                            mm = t.matmul(reg,
                                          V(o_XF + c * TILE + tc * 128, 128),
                                          wv_(o_wv + c * 1024 + og * 512, 512),
                                          start=(c == 0), stop=(c == NCH - 1))
                        te_vq[("v", tc, og)] = I(te, mm)

            @block.vector
            def _(v, slotj=slotj):
                for tc in range(QC):
                    for og in range(2):
                        v.wait_ge(te.h, te_vq[("v", tc, og)])
                        vev[(tc, og)] = I(ve, v.tensor_copy(
                            V(o_VR + (slotj * 2 + tc) * 1024 + og * 512, 512),
                            psA[:, 1024 + og * 512:1024 + (og + 1) * 512]))

            @block.tensor
            def _(t, tj=tj):
                if tj >= 1:
                    t.wait_ge(ve.h, mix_done_d[tj - 1])
                ve0q = ve.n
                for oc in range(NCH):
                    reg = psA[:, (oc % 4) * 512:(oc % 4) * 512 + 256]
                    if oc >= 4:
                        t.wait_ge(ve.h, ve0q + oc - 3)
                    elif oc == 2:
                        t.wait_ge(ve.h, vev[(1, 0)])
                    elif oc == 3:
                        t.wait_ge(ve.h, vev[(1, 1)])
                    for c in range(NCH):
                        mm = t.matmul(reg, wv_(o_wq + c * 1024 + oc * 128, 128),
                                      V(o_XF + c * TILE, TILE),
                                      start=(c == 0), stop=(c == NCH - 1))
                    te_vq[("q", oc)] = I(te, mm)

            @block.vector
            def _(v):
                for oc in range(NCH):
                    v.wait_ge(te.h, te_vq[("q", oc)])
                    reg = psA[:, (oc % 4) * 512:(oc % 4) * 512 + 256]
                    v.tensor_copy(ar_t.ap()[0:64, o_QF + (2 * oc) * TILE:o_QF + (2 * oc + 1) * TILE],
                                  reg[0:64, :])
                    v.drain()
                    v.tensor_copy(ar_t.ap()[0:64, o_QF + (2 * oc + 1) * TILE:o_QF + (2 * oc + 2) * TILE],
                                  reg[64:128, :])
                    I(ve, v.drain())
            qv_evac_d[tj] = ve.n

        emit_ln1_xf(0)
        pend_tail = [None]

        for ti in range(min(B * NT, DBG_A_TILES)):
            b, T = divmod(ti, NT)
            par = ti % 2
            q0 = T * TILE
            n = T
            slot = n % 2
            pslot = (n - 1) % 2
            xtv = V(o_xt + par * NCH * TILE, NCH * TILE)

            if ti == 0:
                emit_vq(0)
            te_v = {}

            @block.tensor
            def _(t, ti=ti):
                t.wait_ge(ve.h, qv_evac_d[ti])
                if ti >= 1:
                    t.wait_ge(ve.h, aof_done_d[ti - 1])
                    t.wait_ge(sc.h, sc_gate_d[ti - 1])
                ve0k = ve.n
                for oc in range(NCH):
                    reg = psA[:, (oc % 4) * 512:(oc % 4) * 512 + 256]
                    if oc >= 4:
                        t.wait_ge(ve.h, ve0k + oc - 3)
                    for c in range(NCH):
                        mm = t.matmul(reg, wv_(o_wk + c * 1024 + oc * 128, 128),
                                      V(o_XF + c * TILE, TILE),
                                      start=(c == 0), stop=(c == NCH - 1))
                    te_v[("k", oc)] = I(te, mm)
                for oc in range(4):
                    reg = psB[:, (oc % 2) * 512:(oc % 2) * 512 + 256]
                    if oc >= 2:
                        t.wait_ge(ve.h, ve0k + NCH + oc - 1)
                    for c in range(NCH):
                        mm = t.matmul(reg, wv_(o_wgq + c * 512 + oc * 128, 128),
                                      V(o_XF + c * TILE, TILE),
                                      start=(c == 0), stop=(c == NCH - 1))
                    te_v[("gq", oc)] = I(te, mm)
                for qc in range(QC):
                    reg = psC[:, 64:96] if qc == 0 else psD[:, 64:96]
                    for h in range(H):
                        mm = t.matmul(reg[:, h * 2:h * 2 + 2],
                                      ar_t.ap()[0:64, o_QF + h * TILE + qc * 128:
                                                o_QF + h * TILE + qc * 128 + 128],
                                      glgrS[:, h * 2:h * 2 + 2],
                                      start=True, stop=True)
                    te_v[("gate", qc)] = I(te, mm)
            te_projq_d[ti] = te.n

            @block.vector
            def _(v, T=T, slot=slot):
                for oc in range(NCH):
                    v.wait_ge(te.h, te_v[("k", oc)])
                    reg = psA[:, (oc % 4) * 512:(oc % 4) * 512 + 256]
                    v.tensor_scalar(ar_t.ap()[0:64, o_KR + (2 * oc) * 2 * W + slot * W:
                                              o_KR + (2 * oc) * 2 * W + (slot + 1) * W],
                                    reg[0:64, :], ISQ, None, ALU.mult)
                    v.drain()
                    v.tensor_scalar(ar_t.ap()[0:64, o_KR + (2 * oc + 1) * 2 * W + slot * W:
                                              o_KR + (2 * oc + 1) * 2 * W + (slot + 1) * W],
                                    reg[64:128, :], ISQ, None, ALU.mult)
                    I(ve, v.drain())
                for oc in range(4):
                    v.wait_ge(te.h, te_v[("gq", oc)])
                    reg = psB[:, (oc % 2) * 512:(oc % 2) * 512 + 256]
                    v.tensor_copy(ar_t.ap()[0:64, o_GQF + (2 * oc) * TILE:o_GQF + (2 * oc + 1) * TILE],
                                  reg[0:64, :])
                    v.drain()
                    v.tensor_copy(ar_t.ap()[0:64, o_GQF + (2 * oc + 1) * TILE:o_GQF + (2 * oc + 2) * TILE],
                                  reg[64:128, :])
                    I(ve, v.drain())
                for c in range(NCH):
                    v.tensor_reduce(
                        stat6[:, 0:8],
                        V(o_XF + c * TILE, TILE).rearrange("p (s w) -> p s w", w=POOL),
                        axis=AX, op=ALU.add)
                    v.drain()
                    v.tensor_scalar(V(o_XPOOL + c * NL + T * 8, 8),
                                    stat6[:, 0:8], 1.0 / POOL, None, ALU.mult)
                    v.drain()
                I(ve, v.drain())
            kg_evac = ve.n
            kg_evac_d[ti] = kg_evac

            if pend_tail[0] is not None:
                pend_tail[0]()
                pend_tail[0] = None

            if ti + 1 < B * NT:
                b2, T2 = divmod(ti + 1, NT)
                p2 = (ti + 1) % 2

                @block.gpsimd
                def _(g, b2=b2, T2=T2, p2=p2, ti=ti):
                    if ti >= 1:
                        g.wait_ge(ve.h, ve_after_xuse[ti - 1])
                    I(dx[p2], g.dma_start(
                        V(o_xt + p2 * NCH * TILE, NCH * TILE).rearrange(
                            "p (tc f) -> p tc f", tc=QC),
                        x_in[b2, T2 * TILE:(T2 + 1) * TILE, :].rearrange(
                            "(tc p) f -> p tc f", p=128)), 16)

            sc_gate = {}

            @block.scalar
            def _(s):
                for qc in range(QC):
                    s.wait_ge(te.h, te_v[("gate", qc)])
                    reg = psC[:, 64:96] if qc == 0 else psD[:, 64:96]
                    dst = glgrT.ap()[:, qc * 32:(qc + 1) * 32].rearrange(
                        "p (g h) -> p h g", g=2)
                    sc_gate[qc] = I(sc, s.activation(dst, reg, AF.Sigmoid))
            sc_gate_d[ti] = sc_gate[QC - 1]

            @block.tensor
            def _(t, T=T):
                t.wait_ge(ve.h, kg_evac)
                t.wait_ge(sc.h, sc_gate[QC - 1])
                for oc in range(4):
                    reg = psA[:, oc * 512:oc * 512 + 8]
                    for c in range(NCH):
                        mm = t.matmul(reg, wv_(o_wgk + c * 512 + oc * 128, 128),
                                      V(o_XPOOL + c * NL + T * 8, 8),
                                      start=(c == 0), stop=(c == NCH - 1))
                    te_v[("gk", oc)] = I(te, mm)
                for og in range(2):
                    for c in range(NCH):
                        mm = t.matmul(psD[0:8, 256 + og * 128:256 + (og + 1) * 128],
                                      V(o_XPOOL + c * NL + T * 8, 8),
                                      wv_(o_wgv + c * 512 + og * 128, 128),
                                      start=(c == 0), stop=(c == NCH - 1))
                    if og == 0:
                        I(te, mm)
                te_v[("gv",)] = I(te, mm)
                for og in range(2):
                    for c in range(NCH):
                        mm = t.matmul(psC[0:8, 256 + og * 128:256 + (og + 1) * 128],
                                      V(o_XPOOL + c * NL + T * 8, 8),
                                      wv_(o_wgv + c * 512 + 256 + og * 128, 128),
                                      start=(c == 0), stop=(c == NCH - 1))
                    if og == 0:
                        I(te, mm)
                te_v[("gv2",)] = I(te, mm)

            @block.vector
            def _(v, T=T, par=par, ti=ti):
                for oc in range(4):
                    v.wait_ge(te.h, te_v[("gk", oc)])
                    reg = psA[:, oc * 512:oc * 512 + 8]
                    v.tensor_scalar(ar_t.ap()[0:64, o_GKF + (2 * oc) * NL + T * 8:
                                              o_GKF + (2 * oc) * NL + T * 8 + 8],
                                    reg[0:64, :], ISQ, None, ALU.mult)
                    v.drain()
                    v.tensor_scalar(ar_t.ap()[0:64, o_GKF + (2 * oc + 1) * NL + T * 8:
                                              o_GKF + (2 * oc + 1) * NL + T * 8 + 8],
                                    reg[64:128, :], ISQ, None, ALU.mult)
                    I(ve, v.drain())
                v.wait_ge(te.h, te_v[("gv2",)])
                v.wait_ge(dgv[par].h, dgv[par].n)
                v.tensor_copy(ar_t.ap()[64:72, o_gvtmp:o_gvtmp + 256], psD[0:8, 256:512])
                v.drain()
                v.tensor_copy(ar_t.ap()[64:72, o_gvtmp + 256:o_gvtmp + 512], psC[0:8, 256:512])
                I(ve, v.drain())
            gv_evac = ve.n

            @block.gpsimd
            def _(g, T=T, par=par, ti=ti):
                g.wait_ge(ve.h, gv_evac)
                if ti >= 1:
                    g.wait_ge(te.h, te_after_latAV[ti - 1])
                g.wait_ge(dtr[0].h, dtr[0].n)
                g.wait_ge(dtr[1].h, dtr[1].n)
                I(dgv[par], g.dma_start(ar_t.ap()[T * 8:T * 8 + 8, o_GVT:o_GVT + 512],
                                        ar_t.ap()[64:72, o_gvtmp:o_gvtmp + 512]), 16)

            te_lat = {}

            @block.tensor
            def _(t, T=T, par=par):
                t.wait_ge(dgv[par].h, dgv[par].n)
                t.wait_ge(ve.h, gv_evac)
                sc0 = sc.n
                for lh in range(LH):
                    reg = psA[0:64, 1024 + (lh % 2) * 512:1024 + (lh % 2) * 512 + 256]
                    if lh >= 2:
                        t.wait_ge(sc.h, sc0 + lh - 1)
                    mm = t.matmul(reg, ar_t.ap()[0:64, o_GKF + lh * NL:o_GKF + (lh + 1) * NL],
                                  ar_t.ap()[0:64, o_GQF + lh * TILE:o_GQF + (lh + 1) * TILE],
                                  start=True, stop=True)
                    te_lat[lh] = I(te, mm)

            sc_lexp = {}

            @block.scalar
            def _(s):
                for lh in range(LH):
                    s.wait_ge(te.h, te_lat[lh])
                    reg = psA[0:64, 1024 + (lh % 2) * 512:1024 + (lh % 2) * 512 + 256]
                    sc_lexp[lh] = I(sc, s.activation(
                        ar_t.ap()[0:64, o_expL + lh * TILE:o_expL + (lh + 1) * TILE],
                        reg, AF.Exp))

            @block.vector
            def _(v, T=T):
                for lh in range(LH):
                    v.wait_ge(sc.h, sc_lexp[lh])
                    v.tensor_tensor(ar_t.ap()[0:64, o_expL + lh * TILE:o_expL + (lh + 1) * TILE],
                                    ar_t.ap()[0:64, o_expL + lh * TILE:o_expL + (lh + 1) * TILE],
                                    mlatS[:, T * TILE:(T + 1) * TILE], op=ALU.mult)
                    I(ve, v.drain())
            latexp_done = ve.n

            te_lav = {}

            @block.tensor
            def _(t):
                t.wait_ge(ve.h, latexp_done)
                ve0 = ve.n
                for qc in range(QC):
                    if qc == 1:
                        t.wait_ge(ve.h, ve0 + 1)
                    mmB = None
                    for lh in range(LH):
                        st = ar_t.ap()[0:64, o_expL + lh * TILE + qc * 128:
                                       o_expL + lh * TILE + qc * 128 + 128]
                        mmA = t.matmul(psB[:, lh * 64:(lh + 1) * 64], st,
                                       ar_t.ap()[0:64, o_GVT + lh * 64:o_GVT + (lh + 1) * 64],
                                       start=True, stop=True)
                        mmB = t.matmul(psC[:, 32 + qc * 8 + lh:32 + qc * 8 + lh + 1], st,
                                       onesS[0:64, 0:1], start=True, stop=True)
                    I(te, mmA)
                    te_lav[qc] = I(te, mmB)
                te_after_latAV[ti] = te.n

            @block.vector
            def _(v):
                for qc in range(QC):
                    v.wait_ge(te.h, te_lav[qc])
                    v.reciprocal(rden8[:, qc * 8:(qc + 1) * 8],
                                 psC[:, 32 + qc * 8:32 + (qc + 1) * 8])
                    v.drain()
                    rv = rden8.ap()[:, qc * 8:(qc + 1) * 8].unsqueeze(2).broadcast_to((128, 8, 64))
                    v.tensor_tensor(
                        V(o_remTM + qc * 1024, 512).rearrange("p (l e) -> p l e", l=8),
                        psB[:, 0:512].rearrange("p (l e) -> p l e", l=8), rv, op=ALU.mult)
                    I(ve, v.drain())
            remnorm_done = ve.n

            def emit_remgate():
                @block.vector
                def _(v, par=par):
                    for qc in range(QC):
                        v.wait_ge(sc.h, sc_gate[qc])
                        v.wait_ge(dtr[par].h, rem_marks['remtm'])
                        rrv = rden8.ap()[:, qc * 8:(qc + 1) * 8].unsqueeze(2) \
                            .broadcast_to((128, 8, 2))
                        v.tensor_tensor(rr16[:, qc * 16:(qc + 1) * 16],
                                        glgrT[:, qc * 32 + 16:qc * 32 + 32], rrv, op=ALU.mult)
                        v.drain()
                        rrv2 = rr16.ap()[:, qc * 16:(qc + 1) * 16].unsqueeze(2).broadcast_to((128, 16, 64))
                        v.tensor_tensor(
                            V(o_aoTM + qc * 1024, 1024).rearrange("p (h e) -> p h e", h=H),
                            V(o_aoTM + qc * 1024, 1024).rearrange("p (h e) -> p h e", h=H),
                            rrv2, op=ALU.mult)
                        I(ve, v.drain())
                remgate_done = ve.n

            rem_marks = {}

            def emit_remote_tail():
                @block.sync
                def _(sy, par=par):
                    sy.wait_ge(ve.h, remnorm_done)
                    trbatch(sy, par, [
                        (V(o_remNF, 4 * TILE).rearrange("p (c t) -> p c t", c=4)[:, :, qc * 128:(qc + 1) * 128],
                         V(o_remTM + qc * 1024, 512)) for qc in range(QC)])
                remnf_done = dtr[par].n

                te_ro = {}

                @block.tensor
                def _(t, par=par):
                    t.wait_ge(dtr[par].h, remnf_done)
                    sc0 = sc.n
                    for oc in range(NCH):
                        reg = psD[:, 0:256] if oc % 2 == 0 else psC[:, 256:512]
                        if oc >= 2:
                            t.wait_ge(sc.h, sc0 + oc - 1)
                        for c in range(4):
                            mm = t.matmul(reg, wv_(o_ro + c * 1024 + oc * 128, 128),
                                          V(o_remNF + c * TILE, TILE),
                                          start=(c == 0), stop=(c == 3))
                        te_ro[oc] = I(te, mm)

                @block.scalar
                def _(s):
                    for oc in range(NCH):
                        s.wait_ge(te.h, te_ro[oc])
                        reg = psD[:, 0:256] if oc % 2 == 0 else psC[:, 256:512]
                        for qc in range(QC):
                            s.activation(V(o_remPF + qc * 1024 + oc * 128, 128),
                                         reg[:, qc * 128:(qc + 1) * 128], AF.Copy)
                            s.drain()
                        I(sc, s.drain())
                rem_marks['rempf'] = sc.n

                @block.sync
                def _(sy, par=par):
                    sy.wait_ge(sc.h, rem_marks['rempf'])
                    trbatch(sy, par, [
                        (V(o_aoTM + qc * 1024, 1024).rearrange("p (c f) -> p c f", c=NCH),
                         V(o_remPF + qc * 1024, 1024)) for qc in range(QC)])
                rem_marks['remtm'] = dtr[par].n

            mix_marks = {}
            kc_all = [2, 3] if n == 0 else [0, 1, 2, 3]
            for hf in range(4):
                if hf == 2 and ti + 1 < B * NT:
                    emit_ln1_xf(ti + 1)
                te_sb = {}
                sc_exp = {}
                sregs = [psA[:, 1024:1536], psA[:, 1536:2048]]

                @block.tensor
                def _(t, hf=hf, slot=slot, pslot=pslot, n=n, kc_all=kc_all):
                    t.wait_ge(ve.h, kg_evac)
                    sc0 = sc.n
                    nb = 0
                    for kc in kc_all:
                        sl = pslot if kc < 2 else slot
                        koff = (kc % 2) * 128
                        for hg in range(2):
                            reg = sregs[nb % 2]
                            if nb >= 2:
                                t.wait_ge(sc.h, sc0 + nb - 1)
                            mm = None
                            for hh in range(2):
                                h = hf * 4 + hg * 2 + hh
                                mm = t.matmul(
                                    reg[:, hh * 256:(hh + 1) * 256],
                                    ar_t.ap()[0:64,
                                              o_KR + h * 2 * W + sl * W + koff:
                                              o_KR + h * 2 * W + sl * W + koff + 128],
                                    ar_t.ap()[0:64,
                                              o_QF + h * TILE:o_QF + (h + 1) * TILE],
                                    start=True, stop=True)
                            te_sb[nb] = I(te, mm)
                            nb += 1

                nbt = len(kc_all) * 2

                @block.scalar
                def _(s, hf=hf, kc_all=kc_all, nbt=nbt):
                    for nb in range(nbt):
                        s.wait_ge(te.h, te_sb[nb])
                        reg = sregs[nb % 2]
                        kc = kc_all[nb // 2]
                        hg = nb % 2
                        dst = ar_t.ap()[:, o_expP + kc * 4 * TILE + hg * 2 * TILE:
                                        o_expP + kc * 4 * TILE + (hg + 1) * 2 * TILE]
                        sc_exp[nb] = I(sc, s.activation(dst, reg, AF.Exp))

                @block.vector
                def _(v, hf=hf, kc_all=kc_all, nbt=nbt):
                    for nb in range(nbt):
                        v.wait_ge(sc.h, sc_exp[nb])
                        kc = kc_all[nb // 2]
                        hg = nb % 2
                        dst = ar_t.ap()[:, o_expP + kc * 4 * TILE + hg * 2 * TILE:
                                        o_expP + kc * 4 * TILE + (hg + 1) * 2 * TILE]
                        mkv = mlocS.ap()[:, kc * 256:(kc + 1) * 256].unsqueeze(1) \
                            .broadcast_to((128, 2, 256))
                        v.tensor_tensor(dst.rearrange("p (g q) -> p g q", g=2), 
                                        dst.rearrange("p (g q) -> p g q", g=2), mkv,
                                        op=ALU.mult)
                        I(ve, v.drain())
                locexp_done = ve.n

                if hf == 0:
                    emit_remote_tail()
                    emit_remgate()

                @block.tensor
                def _(t, hf=hf, slot=slot, pslot=pslot, n=n, kc_all=kc_all):
                    t.wait_ge(ve.h, locexp_done)
                    if hf == 0:
                        t.wait_ge(ve.h, remnorm_done)
                        t.wait_ge(sc.h, rem_marks['rempf'])
                    for qc in range(QC):
                        avreg = psB[:, 0:1024] if qc == 0 else psA[:, 0:1024]
                        dreg = psC[:, 0:16] if qc == 0 else psD[:, 0:16]
                        for hh in range(4):
                            h = hf * 4 + hh
                            for ki, kc in enumerate(kc_all):
                                st = ar_t.ap()[:, o_expP + kc * 4 * TILE + hh * TILE + qc * 128:
                                               o_expP + kc * 4 * TILE + hh * TILE + qc * 128 + 128]
                                vsl = (slot * 2 + kc - 2) if kc >= 2 else (pslot * 2 + kc)
                                first = ki == 0
                                last = ki == len(kc_all) - 1
                                mmA = t.matmul(avreg[:, h * 64:(h + 1) * 64], st,
                                               V(o_VR + vsl * 1024 + h * 64, 64),
                                               start=first, stop=last)
                                mmB = t.matmul(dreg[:, h:h + 1], st,
                                               onesS[:, 0:1], start=first, stop=last)
                        I(te, mmA)
                        mix_marks[(hf, qc)] = I(te, mmB)

            exp_last_sc_d[ti] = sc.n

            @block.vector
            def _(v, par=par):
                for qc in range(QC):
                    v.wait_ge(te.h, mix_marks[(3, qc)])
                    if qc == 0:
                        v.wait_ge(dx2s.h, dx2s.n)
                    avreg = psB[:, 0:1024] if qc == 0 else psA[:, 0:1024]
                    dreg = psC[:, 0:16] if qc == 0 else psD[:, 0:16]
                    v.reciprocal(rl16[:, qc * 16:(qc + 1) * 16], dreg)
                    v.drain()
                    v.tensor_tensor(rl16[:, qc * 16:(qc + 1) * 16],
                                    glgrT[:, qc * 32:qc * 32 + 16],
                                    rl16[:, qc * 16:(qc + 1) * 16], op=ALU.mult)
                    v.drain()
                    rlv = rl16.ap()[:, qc * 16:(qc + 1) * 16].unsqueeze(2).broadcast_to((128, 16, 64))
                    v.tensor_tensor(
                        V(o_x2st + qc * 1024, 1024).rearrange("p (h e) -> p h e", h=H),
                        avreg.rearrange("p (h e) -> p h e", h=H), rlv, op=ALU.mult)
                    v.drain()
                    v.tensor_tensor(V(o_x2st + qc * 1024, 1024),
                                    V(o_x2st + qc * 1024, 1024),
                                    V(o_aoTM + qc * 1024, 1024), op=ALU.add)
                    I(ve, v.drain())
            mix_done = ve.n
            mix_done_d[ti] = mix_done

            if ti + 1 < B * NT:
                emit_vq(ti + 1)

            @block.sync
            def _(sy, par=par):
                sy.wait_ge(ve.h, mix_done)
                trbatch(sy, par, [
                    (V(o_mixF, NCH * TILE).rearrange("p (c t) -> p c t", c=NCH)[:, :, qc * 128:(qc + 1) * 128],
                     V(o_x2st + qc * 1024, 1024)) for qc in range(QC)])
            mixf_done = dtr[par].n

            te_wo = {}

            @block.tensor
            def _(t, par=par):
                t.wait_ge(dtr[par].h, mixf_done)
                ve0wo = ve.n
                for oc in range(NCH):
                    reg = psB[:, (oc % 4) * 256:(oc % 4) * 256 + 256]
                    if oc >= 4:
                        t.wait_ge(ve.h, ve0wo + ((oc - 4) | 1) + 1)
                    for c in range(NCH):
                        mm = t.matmul(reg, wv_(o_wo + c * 1024 + oc * 128, 128),
                                      V(o_mixF + c * TILE, TILE),
                                      start=(c == 0), stop=(c == NCH - 1))
                    te_wo[oc] = I(te, mm)
            te_after_tile[ti] = te.n

            @block.vector
            def _(v):
                for oc in range(NCH):
                    v.wait_ge(te.h, te_wo[min(oc | 1, NCH - 1)])
                    reg = psB[:, (oc % 4) * 256:(oc % 4) * 256 + 256]
                    aov = V(o_aoF, QC * 1024).rearrange(
                        "p (qc c f) -> p qc c f", qc=QC, c=NCH)[:, :, oc, :]
                    v.tensor_scalar(aov, reg.rearrange("p (qc f) -> p qc f", qc=QC),
                                    1.0, None, ALU.mult)
                    I(ve, v.drain())
            aof_done = ve.n
            aof_done_d[ti] = aof_done

            def emit_tail_store(par=par, ti=ti, xtv=xtv, b=b, q0=q0,
                                aof_done=aof_done):
                @block.sync
                def _(sy):
                    sy.wait_ge(ve.h, aof_done)
                    trbatch(sy, par, [
                        (V(o_x2st + qc * 1024, 1024).rearrange("p (c f) -> p c f", c=NCH),
                         V(o_aoF + qc * 1024, 1024)) for qc in range(QC)])
                aotm_done = dtr[par].n

                @block.vector
                def _(v):
                    v.wait_ge(dtr[par].h, aotm_done)
                    for qc in range(QC):
                        v.tensor_tensor(V(o_x2st + qc * 1024, 1024),
                                        xtv[:, qc * 1024:(qc + 1) * 1024],
                                        V(o_x2st + qc * 1024, 1024), op=ALU.add)
                        v.drain()
                    I(ve, v.drain())
                    ve_after_xuse[ti] = ve.n
                x2_ready = ve.n

                @block.sync
                def _(sy):
                    sy.wait_ge(ve.h, x2_ready)
                    I(dx2s, sy.dma_start(
                        x2_dram[b, q0:q0 + TILE, :].rearrange("(tc p) f -> p tc f", p=128),
                        V(o_x2st, QC * 1024).rearrange("p (tc f) -> p tc f", tc=QC)), 16)

            pend_tail[0] = emit_tail_store
            if ti + 1 >= min(B * NT, DBG_A_TILES):
                emit_tail_store()
                pend_tail[0] = None

        ve_phA = ve.n
        sc_phA = sc.n
        te_phA = te.n

        deq_sc = {}
        deq_ve = {}
        deq_te = {}
        scc_mark = {}
        q_mark = {}
        rsel_mark = {}
        deq_te_pred = {}
        deq_ve_pred = {}
        _ten = te.n
        _ven = ve.n
        _scn = sc.n
        deq_sc_pred = {}
        for _i in range(6):
            for _c in range(NCH):
                _ten += 1
                _ven += 1
                _scn += 1
                deq_te_pred[(_i, _c)] = _ten
                deq_ve_pred[(_i, _c)] = _ven
                deq_sc_pred[(_i, _c)] = _scn

        @block.gpsimd
        def _(g):
            if not DBG_B_PRO:
                return
            I(dwa, g.dma_start(laS.ap().rearrange("p (i c r) -> p i c r", i=6, c=NCH),
                               laT_d.rearrange("i (c p) r -> p i c r", p=128)), 16)
            I(dwa, g.dma_start(lbS.ap().rearrange("p (i o) -> p i o", i=3),
                               lb4_d.transpose([1, 0, 2])), 16)
            I(dwa, g.dma_start(biasS.ap().rearrange("p (i c) -> p i c", i=6),
                               biasT_d.transpose([1, 0, 2])), 16)
            I(dwa, g.dma_start(lngS.ap().rearrange("p (i c) -> p i c", i=3),
                               lngT_d.transpose([1, 0, 2])), 16)
            I(dwa, g.dma_start(lnbS.ap().rearrange("p (i c) -> p i c", i=3),
                               lnbT_d.transpose([1, 0, 2])), 16)
            g.wait_ge(ve.h, ve_phA)
            g.wait_ge(sc.h, sc_phA)
            g.wait_ge(te.h, te_phA)
            g.wait_ge(dx2s.h, dx2s.n)
            g.wait_ge(dtr[0].h, dtr[0].n)
            g.wait_ge(dtr[1].h, dtr[1].n)
            rsel_mark[0] = I(drsel, g.dma_start(
                ar_t.ap()[0:64, o_rsel:o_rsel + 1024], rsel_d[:, :]), 16)
            for i in range(6):
                if i >= 2:
                    g.wait_ge(te.h, deq_te_pred[(i - 2, NCH - 1)])
                scc_mark[i] = I(dscc[i % 2], g.dma_start(
                    ar_t.ap()[0:64, o_scc + (i % 2) * 1024:o_scc + (i % 2) * 1024 + 1024],
                    scT_d[i]), 16)
                for c in range(NCH):
                    k = i * NCH + c
                    par = k % 2
                    if k >= 2:
                        pk = k - 2
                        g.wait_ge(ve.h, deq_ve_pred[(pk // NCH, pk % NCH)])
                    q_mark[(i, c)] = I(dwb[par], g.dma_start(
                        V(o_qtmp + par * 1024, 1024),
                        qT_d[i, c * 128:(c + 1) * 128, :]), 16)

        @block.tensor
        def _(t):
            if not DBG_B_PRO:
                return
            t.wait_ge(drsel.h, rsel_mark[0])
            for i in range(6):
                t.wait_ge(dscc[i % 2].h, scc_mark[i])
                for c in range(NCH):
                    k = i * NCH + c
                    reg = psA[:, (k % 2) * 1024:(k % 2) * 1024 + 1024]
                    if k >= 2:
                        pk = k - 2
                        t.wait_ge(sc.h, deq_sc_pred[(pk // NCH, pk % NCH)])
                    for og in range(2):
                        mm = t.matmul(reg[:, og * 512:(og + 1) * 512],
                                      ar_t.ap()[0:64, o_rsel + c * 128:o_rsel + (c + 1) * 128],
                                      ar_t.ap()[0:64, o_scc + (i % 2) * 1024 + og * 512:
                                                o_scc + (i % 2) * 1024 + (og + 1) * 512],
                                      start=True, stop=True)
                    deq_te[(i, c)] = I(te, mm)
                    assert deq_te[(i, c)] == deq_te_pred[(i, c)]

        @block.scalar
        def _(s):
            if not DBG_B_PRO:
                return
            for i in range(6):
                for c in range(NCH):
                    k = i * NCH + c
                    par = k % 2
                    s.wait_ge(te.h, deq_te[(i, c)])
                    if k >= 2:
                        pk = k - 2
                        s.wait_ge(ve.h, deq_ve_pred[(pk // NCH, pk % NCH)])
                    s.activation(V(o_sctmp + par * 1024, 1024),
                                 psA[:, par * 1024:par * 1024 + 1024], AF.Copy)
                    s.drain()
                    s.wait_ge(dwb[par].h, q_mark[(i, c)])
                    deq_sc[(i, c)] = I(sc, s.activation(V(o_aftmp + par * 1024, 1024),
                                                        V(o_qtmp + par * 1024, 1024),
                                                        AF.Copy, bias=-1.0, scale=1.0 / 7.5))
                    assert deq_sc[(i, c)] == deq_sc_pred[(i, c)]

        @block.vector
        def _(v):
            if not DBG_B_PRO:
                return
            v.wait_ge(te.h, te_phA)
            for i in range(6):
                for c in range(NCH):
                    k = i * NCH + c
                    par = k % 2
                    v.wait_ge(sc.h, deq_sc[(i, c)])
                    deq_ve[(i, c)] = I(ve, v.tensor_tensor(
                        wbig.ap()[:, i * 8192 + c * 1024:i * 8192 + (c + 1) * 1024],
                        V(o_aftmp + par * 1024, 1024),
                        V(o_sctmp + par * 1024, 1024), op=ALU.mult))
                    assert deq_ve[(i, c)] == deq_ve_pred[(i, c)]
        deq_done = ve.n

        wTv = lambda i, c, lo, nn: wbig.ap()[:, i * 8192 + c * 1024 + lo:i * 8192 + c * 1024 + lo + nn]

        pbank = {}
        pbank['A0'] = ('sc', deq_sc[(5, 6)])
        pbank['A1'] = ('sc', deq_sc[(5, 6)])
        pbank['A2'] = ('sc', deq_sc[(5, 7)])
        pbank['A3'] = ('sc', deq_sc[(5, 7)])
        pbank['B4'] = ('ve', aof_done_d[B * NT - 1]) if (B * NT - 1) in aof_done_d else None
        pbank['B5'] = pbank['B4']
        pbank['C6'] = ('ve', mix_done_d[B * NT - 1]) if (B * NT - 1) in mix_done_d else None
        pbank['D7'] = pbank['C6']
        pbank = {k: v for k, v in pbank.items() if v}

        def bank_wait(t, b):
            if b in pbank:
                eng, val = pbank[b]
                t.wait_ge({'ve': ve.h, 'sc': sc.h, 'te': te.h,
                           'ev1': ev1.h, 'ev2': ev2.h}[eng], val)

        def bank_set(b, eng, val):
            pbank[b] = (eng, val)

        YREG = {'A0': (psA, 0), 'A1': (psA, 512), 'A2': (psA, 1024), 'A3': (psA, 1536),
                'B4': (psB, 0), 'B5': (psB, 512)}

        ln2_done_d = {}
        h0_done_d = {}
        hfin_ve_d = {}
        hfin_ev2_d = {}
        htm_done_d = {}
        var_ve_d = {}
        nbt_total = min(B * NT, DBG_B_TILES) if DBG_B_PRO else 0

        X2SLOT = lambda tj: (o_x2t + (tj % 3) * NCH * TILE
                             if tj % 3 < 2 else o_x2t3)

        def emit_x2load(tj):
            bj, Tj = divmod(tj, NT)
            pj = tj % 2

            @block.gpsimd
            def _(g, tj=tj, bj=bj, Tj=Tj, pj=pj):
                if tj == 0:
                    g.wait_ge(dx2s.h, dx2s.n)
                if tj >= 3:
                    g.wait_ge(ve.h, ve_after_x2[tj - 3])
                g.wait_ge(dx2l[pj].h, dx2l[pj].n)
                I(dx2l[pj], g.dma_start(
                    V(X2SLOT(tj), NCH * TILE).rearrange("p (tc f) -> p tc f", tc=QC),
                    x2_dram[bj, Tj * TILE:(Tj + 1) * TILE, :].rearrange(
                        "(tc p) f -> p tc f", p=128)), 16)

        def emit_ln2(tj):
            pj = tj % 2
            x2vj = V(X2SLOT(tj), NCH * TILE)
            ve_stats2 = {}

            @block.vector
            def _(v, tj=tj, pj=pj, x2vj=x2vj):
                v.wait_ge(dx2l[pj].h, dx2l[pj].n)
                if tj == 0:
                    v.wait_ge(ve.h, deq_done)
                    v.wait_ge(dwa.h, dwa.n)
                for tc in range(QC):
                    xv = x2vj[:, tc * 1024:(tc + 1) * 1024]
                    v.bn_stats(stat6[:, 0:6], xv[:, 0:512])
                    v.drain()
                    v.bn_stats(stat6[:, 6:12], xv[:, 512:1024])
                    v.drain()
                    v.bn_aggr(stat2[:, 2 * tc:2 * tc + 2], stat6[:, 0:12])
                    v.drain()
                    v.tensor_scalar(stat2[:, 2 * tc + 1:2 * tc + 2],
                                    stat2[:, 2 * tc + 1:2 * tc + 2], EPS, None, ALU.add)
                    v.drain()
                    ve_stats2[tc] = I(ve, v.drain())

            sc_sig2 = {}

            @block.scalar
            def _(s):
                for tc in range(QC):
                    s.wait_ge(ve.h, ve_stats2[tc])
                    sc_sig2[tc] = I(sc, s.activation(sigv[:, tc:tc + 1],
                                                     stat2[:, 2 * tc + 1:2 * tc + 2],
                                                     AF.Sqrt))

            @block.vector
            def _(v, x2vj=x2vj, tj=tj, pj=pj):
                if tj >= 1:
                    v.wait_ge(dtr[1 - pj].h, h0_done_d[tj - 1])
                for tc in range(QC):
                    v.wait_ge(sc.h, sc_sig2[tc])
                    v.reciprocal(invv[:, tc:tc + 1], sigv[:, tc:tc + 1])
                    v.drain()
                    v.tensor_scalar(V(o_xn2 + tc * 1024, 1024),
                                    x2vj[:, tc * 1024:(tc + 1) * 1024],
                                    stat2[:, 2 * tc:2 * tc + 1], invv[:, tc:tc + 1],
                                    ALU.subtract, ALU.mult)
                    v.drain()
                I(ve, v.drain())
            ln2_done_d[tj] = ve.n

        def emit_h0(tj):
            pj = tj % 2

            @block.sync
            def _(sy, tj=tj, pj=pj):
                sy.wait_ge(ve.h, ln2_done_d[tj])
                if tj >= 2:
                    sy.wait_ge(ev2.h, hfin_ev2_d[tj - 2])
                    sy.wait_ge(te.h, te_after_tile[(tj - 2, 'B')])
                ha = (2 * (tj % 2)) * NCH * TILE
                trbatch(sy, pj, [
                    (V(o_hF + ha, NCH * TILE).rearrange(
                        "p (c t) -> p c t", c=NCH)[:, :, tc * 128:(tc + 1) * 128],
                     V(o_xn2 + tc * 1024, 1024)) for tc in range(QC)])
            h0_done_d[tj] = dtr[pj].n

        if nbt_total > 0:
            emit_x2load(0)
            if nbt_total > 1:
                emit_x2load(1)
            emit_ln2(0)
            emit_h0(0)
            if nbt_total > 1:
                emit_ln2(1)
                emit_h0(1)

        Y1BANKS = ['A0', 'A0', 'A1', 'A1']
        Y2BANKS = ['A3', 'A3', 'B4', 'B4', 'B5', 'B5']
        STATS = {0: ('D7', psD, 0), 1: ('A2', psA, 1024)}

        Y1REGS = [psA[:, r * 256:(r + 1) * 256] for r in range(4)]
        Y2REGS = ([psA[:, 1536 + r * 256:1536 + (r + 1) * 256] for r in range(2)]
                  + [psB[:, r * 256:(r + 1) * 256] for r in range(4)])
        last_yout = {'ve': 0}

        def b_state(ti):
            b, T = divmod(ti, NT)
            pp = ti % 2
            return dict(
                ti=ti, b=b, T=T, pp=pp, q0=T * TILE,
                x2v=V(X2SLOT(ti), NCH * TILE),
                y1F=(o_y1F if pp == 0 else o_t1F),
                gF=(o_gF if pp == 0 else o_qtmp),
                sqF=(o_sqF if pp == 0 else o_aftmp),
                btr=(o_btr if pp == 0 else o_sctmp),
                brow=(o_brow if pp == 0 else o_sctmp + 1024),
                srow=(o_srow if pp == 0 else o_scc),
                ha=2 * pp, hb=2 * pp + 1,
                stb=STATS[pp],
            )

        def b_hio(st, sbk):
            a, bb = st['ha'], st['hb']
            hin = a if sbk != 1 else bb
            hout = bb if sbk != 1 else a
            return o_hF + hin * NCH * TILE, o_hF + hout * NCH * TILE

        def b_Y1a(st, sbk):
            i0 = 2 * sbk
            hin_o, _ = b_hio(st, sbk)
            ti, pp = st['ti'], st['pp']
            btr_o = st['btr']

            @block.tensor
            def _(t, i0=i0, hin_o=hin_o, sbk=sbk, ti=ti, pp=pp):
                if sbk == 0:
                    t.wait_ge(dtr[pp].h, h0_done_d[ti])
                    if ti == 0:
                        t.wait_ge(ve.h, deq_done)
                else:
                    t.wait_ge(ev2.h, st['h_ready'])
                bank_wait(t, 'C6')
                for c in range(NCH):
                    mm = t.matmul(psC[0:32, 0:TILE],
                                  laS[:, i0 * NCH * 32 + c * 32:i0 * NCH * 32 + (c + 1) * 32],
                                  V(hin_o + c * TILE, TILE),
                                  start=(c == 0), stop=(c == NCH - 1))
                I(te, mm)
            st['btr1_te'] = te.n

            @block.vector
            def _(v, i0=i0, btr_o=btr_o):
                v.wait_ge(te.h, st['btr1_te'])
                bp = (i0 % 2) * 64
                v.tensor_scalar(ar_t.ap()[bp:bp + 32, btr_o:btr_o + TILE],
                                psC[0:32, 0:TILE], 1.0, None, ALU.mult)
                bank_set('C6', 've', I(ve, v.drain()))
            st['btr1_ve'] = ve.n

        def b_Y1b(st, sbk):
            i0 = 2 * sbk
            hin_o, _ = b_hio(st, sbk)
            btr_o = st['btr']
            te_y1 = {}

            @block.tensor
            def _(t, i0=i0, hin_o=hin_o, btr_o=btr_o):
                ev0 = ev1.n
                for oc in range(NCH):
                    reg = Y1REGS[oc % 4]
                    if oc >= 4:
                        t.wait_ge(ev1.h, ev0 + ((oc - 4) | 1) + 1)
                    else:
                        bank_wait(t, Y1BANKS[oc % 4])
                    for c in range(NCH):
                        t.matmul(reg, wTv(i0, c, oc * 128, 128),
                                 V(hin_o + c * TILE, TILE),
                                 start=(c == 0), stop=False)
                    if oc == 0:
                        t.wait_ge(ve.h, st['btr1_ve'])
                    bp = (i0 % 2) * 64
                    mm = t.matmul(reg,
                                  lbS[bp:bp + 32,
                                      (i0 // 2) * 1024 + oc * 128:(i0 // 2) * 1024 + oc * 128 + 128],
                                  ar_t.ap()[bp:bp + 32, btr_o:btr_o + TILE], start=False, stop=True)
                    te_y1[oc] = I(te, mm)
            st['te_y1'] = te_y1

        def b_Y1(st, sbk):
            b_Y1a(st, sbk)
            b_Y1b(st, sbk)

        def b_L1e(st, sbk):
            i0 = 2 * sbk
            te_y1 = st['te_y1']
            y1o, sqo = st['y1F'], st['sqF']

            @block.vector
            def _(v, i0=i0, y1o=y1o):
                for oc in range(NCH):
                    v.wait_ge(te.h, te_y1[min(oc | 1, NCH - 1)])
                    v.tensor_scalar(V(y1o + oc * TILE, TILE),
                                    Y1REGS[oc % 4],
                                    biasS[:, i0 * 8 + oc:i0 * 8 + oc + 1], None, ALU.add)
                    bank_set(Y1BANKS[oc % 4], 'ev1', I(ev1, v.drain()))
            st['y1_evac'] = ev1.n

            @block.scalar
            def _(s, y1o=y1o, sqo=sqo, sro=st['srow']):
                s.wait_ge(ev1.h, st['y1_evac'])
                I(sc, s.activation(V(sqo, NCH * TILE), V(y1o, NCH * TILE), AF.Square))
                I(sc, s.activation(ar_t.ap()[0:1, sro + 4 * TILE:sro + 4 * TILE + 1],
                                   onesS[0:1, 0:1], AF.Sqrt))
            st['sq_sc'] = sc.n - 1

        def b_L1m(st, sbk):
            y1o, sqo = st['y1F'], st['sqF']
            sb_name, sb_ps, sb_off = st['stb']

            @block.tensor
            def _(t, y1o=y1o, sqo=sqo, sb_ps=sb_ps, sb_off=sb_off):
                t.wait_ge(ev1.h, st['y1_evac'])
                bank_wait(t, sb_name)
                for c in range(NCH):
                    mm = t.matmul(sb_ps[0:1, sb_off:sb_off + TILE], onesS[:, 0:1],
                                  V(y1o + c * TILE, TILE),
                                  start=(c == 0), stop=(c == NCH - 1))
                I(te, mm)
                t.wait_ge(sc.h, st['sq_sc'])
                for c in range(NCH):
                    mm = t.matmul(sb_ps[0:1, sb_off + 256:sb_off + 256 + TILE], onesS[:, 0:1],
                                  V(sqo + c * TILE, TILE),
                                  start=(c == 0), stop=(c == NCH - 1))
                I(te, mm)
            st['sums_te'] = te.n

        def b_L1v(st, sbk):
            sro, bro = st['srow'], st['brow']
            sb_name, sb_ps, sb_off = st['stb']
            sr = lambda a, bn: ar_t.ap()[0:1, sro + 2 * a:sro + 2 * a + 2 * bn].bitcast(F32)
            br = lambda a, bn: ar_t.ap()[0:1, bro + a:bro + a + bn]

            @block.vector
            def _(v, sb_ps=sb_ps, sb_off=sb_off):
                v.wait_ge(te.h, st['sums_te'])
                v.tensor_scalar(br(0, TILE), sb_ps[0:1, sb_off:sb_off + TILE],
                                -1.0 / D, None, ALU.mult)
                v.drain()
                v.tensor_tensor(sr(0, TILE), br(0, TILE), br(0, TILE), op=ALU.mult)
                v.drain()
                v.scalar_tensor_tensor(sr(TILE, TILE),
                                       sb_ps[0:1, sb_off + 256:sb_off + 256 + TILE],
                                       1.0 / D, sr(0, TILE),
                                       op0=ALU.mult, op1=ALU.subtract)
                v.drain()
                v.tensor_scalar(sr(TILE, TILE), sr(TILE, TILE), EPS, None, ALU.add)
                I(ve, v.drain())
            st['var_ve'] = ve.n

            @block.scalar
            def _(s):
                s.wait_ge(ve.h, st['var_ve'])
                with nc.allow_low_precision(reason="bf16 LN sigma for broadcast matmul"):
                    I(sc, s.activation(br(TILE, TILE), sr(TILE, TILE), AF.Sqrt))
            st['sig_sc'] = sc.n

        def b_L1c(st, sbk):
            bro, pp = st['brow'], st['pp']
            sb_name, sb_ps, sb_off = st['stb']
            br = lambda a, bn: ar_t.ap()[0:1, bro + a:bro + a + bn]

            @block.tensor
            def _(t, sb_ps=sb_ps, sb_off=sb_off):
                t.wait_ge(sc.h, st['sig_sc'])
                mm = t.matmul(sb_ps[:, sb_off:sb_off + 512], onesR[0:1, :], br(0, 512),
                              start=True, stop=True)
                I(te, mm)
            st['bc_te'] = te.n

            @block.vector
            def _(v, sb_ps=sb_ps, sb_off=sb_off, pp=pp):
                v.wait_ge(te.h, st['bc_te'])
                v.reciprocal(V(o_invb + pp * 512, 512).bitcast(F32),
                             sb_ps[:, sb_off + TILE:sb_off + 2 * TILE])
                I(ve, v.drain())
            st['inv_ve'] = ve.n

        def b_L2(st, sbk):
            y1o, go = st['y1F'], st['gF']
            sb_name, sb_ps, sb_off = st['stb']

            @block.vector
            def _(v, sbk=sbk, sb_ps=sb_ps, sb_off=sb_off, pp=st['pp']):
                v.wait_ge(te.h, st['bc_te'])
                mv = sb_ps.ap()[:, sb_off:sb_off + TILE].unsqueeze(1).broadcast_to(
                    (128, NCH, TILE))
                v.tensor_tensor(V(y1o, NCH * TILE).rearrange("p (c t) -> p c t", c=NCH),
                                V(y1o, NCH * TILE).rearrange("p (c t) -> p c t", c=NCH),
                                mv, op=ALU.add)
                v.drain()
                iv = V(o_invb + pp * 512, 512).bitcast(F32).unsqueeze(1).broadcast_to(
                    (128, NCH, TILE))
                v.tensor_tensor(V(go, NCH * TILE).rearrange("p (c t) -> p c t", c=NCH),
                                V(y1o, NCH * TILE).rearrange("p (c t) -> p c t", c=NCH),
                                iv, op=ALU.mult)
                I(ve, v.drain())
            st['t1_ve'] = ve.n
            bank_set(sb_name, 've', st['t1_ve'])

        def b_Ga(st, sbk):
            go = st['gF']

            @block.scalar
            def _(s, sbk=sbk, go=go, sro=st['srow']):
                s.wait_ge(ve.h, st['t1_ve'])
                for c in range(NCH):
                    I(sc, s.activation(
                        V(go + c * TILE, TILE), V(go + c * TILE, TILE), AF.Gelu,
                        bias=lnbS[:, sbk * 8 + c:sbk * 8 + c + 1],
                        scale=lngS[:, sbk * 8 + c:sbk * 8 + c + 1]))
                st['g_done'] = sc.n

        def b_Y2(st, sbk):
            i1 = 2 * sbk + 1
            go, btr_o = st['gF'], st['btr']

            @block.tensor
            def _(t, i1=i1, go=go):
                t.wait_ge(sc.h, st['g_done'])
                bank_wait(t, 'C6')
                for c in range(NCH):
                    mm = t.matmul(psC[0:32, 0:TILE],
                                  laS[:, i1 * NCH * 32 + c * 32:i1 * NCH * 32 + (c + 1) * 32],
                                  V(go + c * TILE, TILE),
                                  start=(c == 0), stop=(c == NCH - 1))
                I(te, mm)
            st['btr2_te'] = te.n

            @block.vector
            def _(v, i1=i1, btr_o=btr_o):
                v.wait_ge(te.h, st['btr2_te'])
                bp = (i1 % 2) * 64
                v.tensor_scalar(ar_t.ap()[bp:bp + 32, btr_o:btr_o + TILE],
                                psC[0:32, 0:TILE], 1.0, None, ALU.mult)
                bank_set('C6', 've', I(ve, v.drain()))
            st['btr2_ve'] = ve.n

            te_y2 = {}

            @block.tensor
            def _(t, i1=i1, go=go, btr_o=btr_o):
                ev0 = ev2.n
                for oc in range(NCH):
                    reg = Y2REGS[oc % 6]
                    if oc >= 6:
                        t.wait_ge(ev2.h, ev0 + ((oc - 6) | 1) + 1)
                    else:
                        bank_wait(t, Y2BANKS[oc % 6])
                    for c in range(NCH):
                        t.matmul(reg, wTv(i1, c, oc * 128, 128),
                                 V(go + c * TILE, TILE),
                                 start=(c == 0), stop=False)
                    if oc == 0:
                        t.wait_ge(ve.h, st['btr2_ve'])
                    bp = (i1 % 2) * 64
                    mm = t.matmul(reg,
                                  lbS[bp:bp + 32,
                                      (i1 // 2) * 1024 + oc * 128:(i1 // 2) * 1024 + oc * 128 + 128],
                                  ar_t.ap()[bp:bp + 32, btr_o:btr_o + TILE], start=False, stop=True)
                    te_y2[oc] = I(te, mm)
            st['te_y2'] = te_y2

        def b_E2(st, sbk):
            i1 = 2 * sbk + 1
            hin_o, hout_o = b_hio(st, sbk)
            te_y2 = st['te_y2']
            ti, pp = st['ti'], st['pp']

            @block.vector
            def _(v, i1=i1, hin_o=hin_o, hout_o=hout_o, sbk=sbk, ti=ti, pp=pp):
                if sbk == 0 and ti >= 2:
                    v.wait_ge(dtr[pp].h, htm_done_d[ti - 2])
                for oc in range(NCH):
                    v.wait_ge(te.h, te_y2[min(oc | 1, NCH - 1)])
                    reg = Y2REGS[oc % 6]
                    if sbk < 2:
                        v.scalar_tensor_tensor(V(hout_o + oc * TILE, TILE), reg,
                                               biasS[:, i1 * 8 + oc:i1 * 8 + oc + 1],
                                               V(hin_o + oc * TILE, TILE),
                                               op0=ALU.add, op1=ALU.add)
                        bank_set(Y2BANKS[oc % 6], 'ev2', I(ev2, v.drain()))
                    else:
                        hov = V(hout_o, QC * 1024).rearrange(
                            "p (qc c f) -> p qc c f", qc=QC, c=NCH)[:, :, oc, :]
                        v.scalar_tensor_tensor(
                            hov, reg.rearrange("p (qc f) -> p qc f", qc=QC),
                            biasS[:, i1 * 8 + oc:i1 * 8 + oc + 1],
                            V(hin_o + oc * TILE, TILE).rearrange(
                                "p (qc f) -> p qc f", qc=QC),
                            op0=ALU.add, op1=ALU.add)
                        bank_set(Y2BANKS[oc % 6], 'ev2', I(ev2, v.drain()))
            st['h_ready'] = ev2.n
            if sbk == 2:
                te_after_tile[(st['ti'], 'B')] = te.n
                hfin_ev2_d[st['ti']] = ev2.n

        def b_tail(st):
            ti, pp, b, q0 = st['ti'], st['pp'], st['b'], st['q0']
            x2v = st['x2v']
            hfin_o = o_hF + st['hb'] * NCH * TILE

            @block.sync
            def _(sy, pp=pp, hfin_o=hfin_o):
                sy.wait_ge(ev2.h, ev2.n)
                sy.wait_ge(ve.h, last_yout['ve'])
                trbatch(sy, pp, [
                    (V(o_hTM + qc * 1024, 1024).rearrange("p (c f) -> p c f", c=NCH),
                     V(hfin_o + qc * 1024, 1024)) for qc in range(QC)])
            htm_done_d[ti] = dtr[pp].n

            @block.vector
            def _(v, pp=pp, ti=ti, x2v=x2v):
                v.wait_ge(dtr[pp].h, htm_done_d[ti])
                v.wait_ge(dys.h, dys.n)
                for qc in range(QC):
                    v.tensor_tensor(V(o_yout + qc * 2048, 2048).bitcast(F32),
                                    x2v[:, qc * 1024:(qc + 1) * 1024],
                                    V(o_hTM + qc * 1024, 1024), op=ALU.add)
                    v.drain()
                I(ve, v.drain())
                ve_after_x2[ti] = ve.n
            last_yout['ve'] = ve.n

            @block.gpsimd
            def _(g, b=b, q0=q0):
                g.wait_ge(ve.h, last_yout['ve'])
                I(dys, g.dma_start(
                    y_out[b, q0:q0 + TILE, :].rearrange("(tc p) f -> p tc f", p=128),
                    V(o_yout, 2 * QC * 1024).bitcast(F32).rearrange(
                        "p (tc f) -> p tc f", tc=QC)), 16)

        assert nbt_total % 2 == 0 or nbt_total == 0
        for pair in range(nbt_total // 2):
            A = b_state(2 * pair)
            Bst = b_state(2 * pair + 1)
            b_Y1(A, 0)
            b_L1e(A, 0)
            b_Y1(Bst, 0)
            b_L1e(Bst, 0)
            for k in range(3):
                b_L1m(A, k)
                b_L1v(A, k)
                if k > 0:
                    b_Y1a(Bst, k)
                b_L1c(A, k)
                if k > 0:
                    b_Y1b(Bst, k)
                    b_L2(A, k)
                    b_Ga(A, k)
                    b_L1e(Bst, k)
                else:
                    b_L2(A, k)
                    b_Ga(A, k)
                if k == 1 and 2 * pair + 2 < nbt_total:
                    emit_x2load(2 * pair + 2)
                b_Y2(A, k)
                b_E2(A, k)
                b_L1m(Bst, k)
                b_L1v(Bst, k)
                if k < 2:
                    b_Y1a(A, k + 1)
                b_L1c(Bst, k)
                if k < 2:
                    b_Y1b(A, k + 1)
                    b_L2(Bst, k)
                    b_Ga(Bst, k)
                    b_L1e(A, k + 1)
                else:
                    b_L2(Bst, k)
                    b_Ga(Bst, k)
                    b_tail(A)
                b_Y2(Bst, k)
                b_E2(Bst, k)
            b_tail(Bst)
            if 2 * pair + 2 < nbt_total:
                emit_ln2(2 * pair + 2)
                emit_h0(2 * pair + 2)
            if 2 * pair + 3 < nbt_total:
                emit_x2load(2 * pair + 3)
                emit_ln2(2 * pair + 3)
                emit_h0(2 * pair + 3)

        @block.sync
        def _(sy):
            sy.wait_ge(dys.h, dys.n)

    return nc


def _host_inputs(L0_attn_x, L0_attn_wq, L0_attn_wk, L0_attn_wv, L0_attn_wgq,
                 L0_attn_wgk, L0_attn_wgv, L0_attn_r_out_w, L0_attn_gate_local_w,
                 L0_attn_gate_remote_w, L0_attn_w_out, L0_ffn_qweights,
                 L0_ffn_scales, L0_ffn_biases, L0_ffn_lora_a, L0_ffn_lora_b,
                 L0_ffn_ln_gamma, L0_ffn_ln_beta):
    f32 = np.float32
    x = np.ascontiguousarray(np.asarray(L0_attn_x, dtype=f32))

    gl = np.asarray(L0_attn_gate_local_w, dtype=f32)
    gr = np.asarray(L0_attn_gate_remote_w, dtype=f32)
    glgr = np.zeros((64, 32), dtype=f32)
    for h in range(H):
        glgr[:, h * 2] = gl[h]
        glgr[:, h * 2 + 1] = gr[h]

    qTi = np.ascontiguousarray(np.asarray(L0_ffn_qweights, dtype=np.int32).transpose(0, 2, 1))
    qT = np.ascontiguousarray(qTi.view(np.int8)[:, :, 0::4])
    scT = np.ascontiguousarray(np.asarray(L0_ffn_scales, dtype=f32).transpose(0, 2, 1))
    rsel = np.zeros((64, NCH * 128), dtype=f32)
    for c in range(NCH):
        for p in range(128):
            rsel[8 * c + p // 16, c * 128 + p] = 1.0
    laT = np.ascontiguousarray(np.asarray(L0_ffn_lora_a, dtype=f32).transpose(0, 2, 1))
    lbT = np.asarray(L0_ffn_lora_b, dtype=f32).transpose(0, 2, 1)
    lb4 = np.zeros((3, 128, D), dtype=f32)
    for i in range(6):
        lb4[i // 2, (i % 2) * 64:(i % 2) * 64 + 32, :] = lbT[i]
    biasT = np.ascontiguousarray(
        np.asarray(L0_ffn_biases, dtype=f32).reshape(6, 8, 128).transpose(0, 2, 1))
    lngT = np.ascontiguousarray(
        np.asarray(L0_ffn_ln_gamma, dtype=f32).reshape(3, 8, 128).transpose(0, 2, 1))
    lnbT = np.ascontiguousarray(
        np.asarray(L0_ffn_ln_beta, dtype=f32).reshape(3, 8, 128).transpose(0, 2, 1))

    i_ = np.arange(128)
    m1 = (i_[None, :] < i_[:, None]).astype(f32)
    m2 = (i_[None, :] >= i_[:, None]).astype(f32)
    ones_t = np.ones((128, 128), f32)
    zeros_t = np.zeros((128, 128), f32)
    mloc = np.zeros((4, 128, 256), dtype=f32)
    mloc[0] = np.concatenate([m1, zeros_t], axis=1)
    mloc[1] = np.concatenate([ones_t, m1], axis=1)
    mloc[2] = np.concatenate([m2, ones_t], axis=1)
    mloc[3] = np.concatenate([zeros_t, m2], axis=1)
    mlat = np.zeros((NT, NL, TILE), dtype=f32)
    for T in range(NT):
        qabs = T * TILE + np.arange(TILE)
        mlat[T] = (np.arange(NL)[:, None] <= (qabs[None, :] // POOL)).astype(f32)
    idm = np.eye(128, dtype=f32)
    onesv = np.zeros((128, 2), dtype=f32)
    onesv[:, 0] = 1.0
    onesr = np.ones((1, 128), dtype=f32)

    com = dict(
        wq=np.ascontiguousarray(np.asarray(L0_attn_wq, dtype=f32)),
        wk=np.ascontiguousarray(np.asarray(L0_attn_wk, dtype=f32)),
        wv=np.ascontiguousarray(np.asarray(L0_attn_wv, dtype=f32)),
        wgq=np.ascontiguousarray(np.asarray(L0_attn_wgq, dtype=f32)),
        wgk=np.ascontiguousarray(np.asarray(L0_attn_wgk, dtype=f32)),
        wgv=np.ascontiguousarray(np.asarray(L0_attn_wgv, dtype=f32)),
        rout=np.ascontiguousarray(np.asarray(L0_attn_r_out_w, dtype=f32)),
        wout=np.ascontiguousarray(np.asarray(L0_attn_w_out, dtype=f32)),
        glgr=glgr, qT=qT, scT=scT, rsel=rsel, laT=laT, lb4=lb4, biasT=biasT,
        lngT=lngT, lnbT=lnbT, mloc=mloc, mlat=mlat, idm=idm, onesv=onesv,
        onesr=onesr,
    )
    in_maps = []
    for c in range(NCORES):
        m = dict(com)
        m["x_in"] = np.ascontiguousarray(x[c * B:(c + 1) * B])
        in_maps.append(m)
    return in_maps


def kernel(**inputs):
    in_maps = _host_inputs(**inputs)
    nc = build_program()
    res = run_bass_kernel_spmd(nc, in_maps, core_ids=list(range(NCORES)))
    return np.concatenate([r["y"] for r in res.results], axis=0)


# version 21
# speedup vs baseline: 1.0389x; 1.0216x over previous
import contextlib
import numpy as np

import concourse.bass as bass
import concourse.mybir as mybir
from concourse.bass_utils import run_bass_kernel_spmd
from concourse.alu_op_type import AluOpType as ALU

F32 = mybir.dt.float32
BF16 = mybir.dt.bfloat16
I32 = mybir.dt.int32
AF = mybir.ActivationFunctionType
AX = mybir.AxisListType.X

NCORES = 8
B = 2
S = 2048
D = 1024
NCH = 8
H = 16
HD = 64
LH = 8
NL = 64
POOL = 32
W = 256
TILE = 256
NT = S // TILE
QC = TILE // 128
NEG = -1.0e9
EPS = 1e-5
ISQ = 0.125

DBG_A_TILES = 16
DBG_B_TILES = 16
DBG_B_PRO = True
DBG_A_STEPS = 10 ** 9


class Sem:
    def __init__(self, es, nc, name):
        self.h = es.enter_context(nc.semaphore(name))
        self.n = 0


class Arena:

    def __init__(self, tensor, cols):
        self.t = tensor
        self.cols = cols
        self.off = {"P": 0}
        self.base_p = None

    def alloc(self, phase, ncols):
        if phase == "P":
            o = self.off["P"]
            self.off["P"] = o + ncols
            return o
        if self.base_p is None:
            self.base_p = self.off["P"]
        o = self.off.get(phase, self.base_p)
        self.off[phase] = o + ncols
        assert self.off[phase] <= self.cols, (phase, self.off[phase], self.cols)
        return o

    def view(self, off, ncols):
        return self.t.ap()[:, off:off + ncols]


def build_program():
    nc = bass.Bass("TRN2", target_bir_lowering=False)
    din = lambda name, shp, dtype: nc.dram_tensor(name, shp, dtype, kind="ExternalInput").ap()

    x_in = din("x_in", [B, S, D], F32)
    wq_d = din("wq", [D, D], F32)
    wk_d = din("wk", [D, D], F32)
    wv_d = din("wv", [D, D], F32)
    wgq_d = din("wgq", [D, 512], F32)
    wgk_d = din("wgk", [D, 512], F32)
    wgv_d = din("wgv", [D, 512], F32)
    rout_d = din("rout", [512, D], F32)
    wout_d = din("wout", [D, D], F32)
    glgr_d = din("glgr", [64, 32], F32)
    qT_d = din("qT", [6, D, D], mybir.dt.int8)
    scT_d = din("scT", [6, 64, D], F32)
    rsel_d = din("rsel", [64, NCH * 128], F32)
    laT_d = din("laT", [6, D, 32], F32)
    lb4_d = din("lb4", [3, 128, D], F32)
    biasT_d = din("biasT", [6, 128, 8], F32)
    lngT_d = din("lngT", [3, 128, 8], F32)
    lnbT_d = din("lnbT", [3, 128, 8], F32)
    mloc_d = din("mloc", [4, 128, 256], F32)
    mlat_d = din("mlat", [NT, NL, TILE], F32)
    id_d = din("idm", [128, 128], F32)
    ones_d = din("onesv", [128, 2], F32)
    onesr_d = din("onesr", [1, 128], F32)

    y_out = nc.dram_tensor("y", [B, S, D], F32, kind="ExternalOutput").ap()
    x2_dram = nc.dram_tensor("x2scr", [B, S, D], BF16, kind="Internal").ap()

    with contextlib.ExitStack() as es:
        E = es.enter_context
        sb = lambda name, shp, dtype: E(nc.sbuf_tensor(name, shp, dtype))

        WB = 6 * 8192
        wbig = sb("wbig", [128, WB], BF16)
        wv_ = lambda o, n: wbig.ap()[:, o:o + n]
        o_wq, o_wk, o_wv, o_wo = 0, 8192, 16384, 24576
        o_wgq, o_wgk, o_wgv, o_ro = 32768, 36864, 40960, 45056

        AC = 48640
        ar_t = sb("arena", [128, AC], BF16)
        ar = Arena(ar_t, AC)

        glgrS = sb("glgrS", [64, 32], BF16)
        mlocS = sb("mlocS", [128, 4 * 256], BF16)
        mlatS = sb("mlatS", [NL, NT * TILE], BF16)
        idS = sb("idS", [128, 128], BF16)
        onesS = sb("onesS", [128, 2], BF16)
        onesR = sb("onesR", [1, 128], BF16)
        laS = sb("laS", [128, 6 * NCH * 32], BF16)
        lbS = sb("lbS", [128, 3 * 1024], BF16)
        biasS = sb("biasS", [128, 6 * 8], F32)
        lngS = sb("lngS", [128, 3 * 8], F32)
        lnbS = sb("lnbS", [128, 3 * 8], F32)

        A = lambda n: ar.alloc("A", n)
        o_xt = A(2 * NCH * TILE)
        o_xn1 = A(NCH * TILE)
        o_XF = A(NCH * TILE)
        o_QF = A(H * TILE)
        o_GQF = A(LH * TILE)
        o_KR = A(H * 2 * W)
        o_VR = A(4 * 1024)
        o_expP = A(4 * 4 * TILE)
        o_expL = A(LH * TILE)
        o_XPOOL = A(NCH * NL)
        o_GKF = A(LH * NL)
        o_GVT = A(512)
        o_gvtmp = o_GVT
        o_remTM = A(QC * 1024)
        o_remNF = o_GQF
        o_remPF = A(NCH * TILE)
        o_aoF = o_remPF
        o_aoTM = A(QC * 1024)
        o_x2st = A(QC * 1024)
        o_mixF = A(NCH * TILE)

        Bv = lambda n: ar.alloc("B", n)
        o_x2t = Bv(2 * NCH * TILE)
        o_xn2 = Bv(NCH * TILE)
        o_hF = Bv(4 * NCH * TILE)
        o_y1F = Bv(NCH * TILE)
        o_sqF = Bv(NCH * TILE)
        o_t1F = Bv(NCH * TILE)
        o_gF = Bv(NCH * TILE)
        o_hTM = Bv(QC * 1024)
        o_qtmp = Bv(2 * 1024)
        o_sctmp = Bv(2 * 1024)
        o_aftmp = Bv(2 * 1024)
        o_scc = Bv(2 * 1024)
        o_rsel = Bv(1024)
        o_gF2 = Bv(NCH * TILE)
        o_btr = Bv(TILE)
        o_brow = Bv(2 * TILE)
        o_yout = Bv(2 * QC * 1024)
        o_srow = Bv(2 * 4 * TILE)
        o_x2t3 = Bv(NCH * TILE)
        o_invb = Bv(2 * 512)

        V = ar.view
        global OFFS
        OFFS = {k: v for k, v in locals().items() if k.startswith("o_")}

        glgrT = sb("glgrT", [128, QC * 32], F32)
        rl16 = sb("rl16", [128, QC * 16], F32)
        rr16 = sb("rr16", [128, QC * 16], F32)
        rden8 = sb("rden8", [128, QC * 8], F32)
        stat6 = sb("stat6", [128, 12], F32)
        stat2 = sb("stat2", [128, 2 * QC], F32)
        sigv = sb("sigv", [128, QC], F32)
        invv = sb("invv", [128, QC], F32)

        psA = E(nc.psum_tensor("psA", [128, 2048], F32))
        psB = E(nc.psum_tensor("psB", [128, 1024], F32))
        psC = E(nc.psum_tensor("psC", [128, 512], F32))
        psD = E(nc.psum_tensor("psD", [128, 512], F32))

        te = Sem(es, nc, "te")
        sc = Sem(es, nc, "sc")
        ve = Sem(es, nc, "ve")
        dwa = Sem(es, nc, "dwa")
        dx = [Sem(es, nc, f"dx{p}") for p in range(2)]
        dtr = [Sem(es, nc, f"dtr{p}") for p in range(2)]
        dgv = [Sem(es, nc, f"dgv{p}") for p in range(2)]
        dx2s = Sem(es, nc, "dx2s")
        dx2l = [Sem(es, nc, f"dx2l{p}") for p in range(2)]
        dwb = [Sem(es, nc, f"dwb{p}") for p in range(2)]
        dscc = [Sem(es, nc, f"dscc{p}") for p in range(2)]
        drsel = Sem(es, nc, "drsel")
        dys = Sem(es, nc, "dys")
        ev1 = Sem(es, nc, "ev1")
        ev2 = Sem(es, nc, "ev2")

        blk_raw = E(nc.Block())

        class _GatedBlock:
            def __init__(self, b):
                self._b = b
                self.count = 0

            def _gate(self, dec):
                def wrap(fn):
                    self.count += 1
                    if self.count <= DBG_A_STEPS:
                        return dec(fn)
                    return None
                return wrap

            @property
            def gpsimd(self):
                return self._gate(self._b.gpsimd)

            @property
            def vector(self):
                return self._gate(self._b.vector)

            @property
            def scalar(self):
                return self._gate(self._b.scalar)

            @property
            def tensor(self):
                return self._gate(self._b.tensor)

            @property
            def sync(self):
                return self._gate(self._b.sync)

        block = _GatedBlock(blk_raw)

        def I(sem, inst, k=1):
            sem.n += k
            inst.then_inc(sem.h, k)
            return sem.n

        def trbatch(sy, par, calls):
            sy.wait_ge(dtr[par].h, dtr[par].n)
            sy.wait_ge(dgv[0].h, dgv[0].n)
            sy.wait_ge(dgv[1].h, dgv[1].n)
            for dst, src in calls:
                sy.dma_start_transpose(dst, src).then_inc(dtr[par].h, 16)
                dtr[par].n += 16
            return dtr[par].n

        @block.gpsimd
        def _(g):
            r3 = lambda ap: ap.rearrange("(c p) o -> p c o", p=128)
            rr = lambda off, nch, w: V(off, nch * w).rearrange("p (c o) -> p c o", c=nch)
            wbr = lambda off, nch, w: wbig.ap()[:, off:off + nch * w].rearrange("p (c o) -> p c o", c=nch)
            I(dwa, g.dma_start(wbr(o_wq, NCH, 1024), r3(wq_d)), 16)
            I(dwa, g.dma_start(wbr(o_wk, NCH, 1024), r3(wk_d)), 16)
            I(dwa, g.dma_start(wbr(o_wv, NCH, 1024), r3(wv_d)), 16)
            I(dwa, g.dma_start(wbr(o_wo, NCH, 1024), r3(wout_d)), 16)
            I(dwa, g.dma_start(wbr(o_wgq, NCH, 512), r3(wgq_d)), 16)
            I(dwa, g.dma_start(wbr(o_wgk, NCH, 512), r3(wgk_d)), 16)
            I(dwa, g.dma_start(wbr(o_wgv, NCH, 512), r3(wgv_d)), 16)
            I(dwa, g.dma_start(wbr(o_ro, 4, 1024), r3(rout_d)), 16)
            I(dwa, g.dma_start(glgrS[:], glgr_d[:, :]), 16)
            I(dwa, g.dma_start(mlocS.ap().rearrange("p (c o) -> p c o", c=4),
                               mloc_d.transpose([1, 0, 2])), 16)
            I(dwa, g.dma_start(mlatS.ap().rearrange("p (c o) -> p c o", c=NT),
                               mlat_d.transpose([1, 0, 2])), 16)
            I(dwa, g.dma_start(idS[:], id_d[:, :]), 16)
            I(dwa, g.dma_start(onesS[:], ones_d[:, :]), 16)
            I(dwa, g.dma_start(onesR[:], onesr_d[:, :]), 16)
            I(dx[0], g.dma_start(
                V(o_xt, NCH * TILE).rearrange("p (tc f) -> p tc f", tc=QC),
                x_in[0, 0:TILE, :].rearrange("(tc p) f -> p tc f", p=128)), 16)

        @block.vector
        def _(v):
            v.wait_ge(dwa.h, dwa.n)
            v.memset(V(o_expP, 4 * 4 * TILE), 0.0)
            v.memset(V(o_expL, LH * TILE), 0.0)
            v.memset(ar_t.ap()[0:64, o_KR:o_KR + H * 2 * W], 0.0)
            v.memset(V(o_VR, 4 * 1024), 0.0)
            v.memset(ar_t.ap()[0:64, o_GKF:o_GKF + LH * NL], 0.0)
            v.memset(ar_t.ap()[0:64, o_GVT:o_GVT + 512], 0.0)
            v.memset(V(o_XPOOL, NCH * NL), 0.0)
            I(ve, v.drain())
        init_ve = ve.n

        te_after_latAV = {}
        te_after_tile = {}
        te_projq_d = {}
        kg_evac_d = {}
        exp_last_sc_d = {}
        qv_evac_d = {}
        sc_gate_d = {}
        ve_after_xuse = {}
        ve_after_x2 = {}
        ln1_done_d = {}
        xf_done_d = {}
        aof_done_d = {}
        mix_done_d = {}

        def emit_ln1_xf(tj):
            bj, Tj = divmod(tj, NT)
            pj = tj % 2
            xtvj = V(o_xt + pj * NCH * TILE, NCH * TILE)
            ve_stats = {}

            @block.vector
            def _(v, tj=tj, pj=pj, xtvj=xtvj):
                v.wait_ge(dx[pj].h, dx[pj].n)
                if tj == 0:
                    v.wait_ge(dwa.h, dwa.n)
                for tc in range(QC):
                    xv = xtvj[:, tc * 1024:(tc + 1) * 1024]
                    v.bn_stats(stat6[:, 0:6], xv[:, 0:512])
                    v.drain()
                    v.bn_stats(stat6[:, 6:12], xv[:, 512:1024])
                    v.drain()
                    v.bn_aggr(stat2[:, 2 * tc:2 * tc + 2], stat6[:, 0:12])
                    v.drain()
                    v.tensor_scalar(stat2[:, 2 * tc + 1:2 * tc + 2],
                                    stat2[:, 2 * tc + 1:2 * tc + 2], EPS, None, ALU.add)
                    v.drain()
                    ve_stats[tc] = I(ve, v.drain())

            sc_sig = {}

            @block.scalar
            def _(s):
                for tc in range(QC):
                    s.wait_ge(ve.h, ve_stats[tc])
                    sc_sig[tc] = I(sc, s.activation(sigv[:, tc:tc + 1],
                                                    stat2[:, 2 * tc + 1:2 * tc + 2],
                                                    AF.Sqrt))

            @block.vector
            def _(v, xtvj=xtvj):
                for tc in range(QC):
                    v.wait_ge(sc.h, sc_sig[tc])
                    v.reciprocal(invv[:, tc:tc + 1], sigv[:, tc:tc + 1])
                    v.drain()
                    v.tensor_scalar(V(o_xn1 + tc * 1024, 1024),
                                    xtvj[:, tc * 1024:(tc + 1) * 1024],
                                    stat2[:, 2 * tc:2 * tc + 1], invv[:, tc:tc + 1],
                                    ALU.subtract, ALU.mult)
                    v.drain()
                I(ve, v.drain())
            ln1_done_d[tj] = ve.n

            @block.sync
            def _(sy, tj=tj, pj=pj):
                sy.wait_ge(ve.h, ln1_done_d[tj])
                if tj >= 1:
                    sy.wait_ge(te.h, te_projq_d[tj - 1])
                    sy.wait_ge(ve.h, kg_evac_d[tj - 1])
                trbatch(sy, pj, [
                    (V(o_XF, NCH * TILE).rearrange("p (c t) -> p c t", c=NCH)[:, :, tc * 128:(tc + 1) * 128],
                     V(o_xn1 + tc * 1024, 1024)) for tc in range(QC)])
            xf_done_d[tj] = dtr[pj].n

        def emit_vq(tj):
            pj = tj % 2
            slotj = (tj % NT) % 2
            te_vq = {}
            vev = {}

            @block.tensor
            def _(t, pj=pj, tj=tj):
                t.wait_ge(dtr[pj].h, xf_done_d[tj])
                if tj == 0:
                    t.wait_ge(ve.h, init_ve)
                else:
                    t.wait_ge(sc.h, exp_last_sc_d[tj - 1])
                ve0v = ve.n
                for tc in range(QC):
                    for og in range(2):
                        if tc == 1:
                            t.wait_ge(ve.h, ve0v + og + 1)
                        reg = psA[:, 1024 + og * 512:1024 + (og + 1) * 512]
                        for c in range(NCH):
                            mm = t.matmul(reg,
                                          V(o_XF + c * TILE + tc * 128, 128),
                                          wv_(o_wv + c * 1024 + og * 512, 512),
                                          start=(c == 0), stop=(c == NCH - 1))
                        te_vq[("v", tc, og)] = I(te, mm)

            @block.vector
            def _(v, slotj=slotj):
                for tc in range(QC):
                    for og in range(2):
                        v.wait_ge(te.h, te_vq[("v", tc, og)])
                        vev[(tc, og)] = I(ve, v.tensor_copy(
                            V(o_VR + (slotj * 2 + tc) * 1024 + og * 512, 512),
                            psA[:, 1024 + og * 512:1024 + (og + 1) * 512]))

            @block.tensor
            def _(t, tj=tj):
                if tj >= 1:
                    t.wait_ge(ve.h, mix_done_d[tj - 1])
                ve0q = ve.n
                for oc in range(NCH):
                    reg = psA[:, (oc % 4) * 512:(oc % 4) * 512 + 256]
                    if oc >= 4:
                        t.wait_ge(ve.h, ve0q + oc - 3)
                    elif oc == 2:
                        t.wait_ge(ve.h, vev[(1, 0)])
                    elif oc == 3:
                        t.wait_ge(ve.h, vev[(1, 1)])
                    for c in range(NCH):
                        mm = t.matmul(reg, wv_(o_wq + c * 1024 + oc * 128, 128),
                                      V(o_XF + c * TILE, TILE),
                                      start=(c == 0), stop=(c == NCH - 1))
                    te_vq[("q", oc)] = I(te, mm)

            @block.vector
            def _(v):
                for oc in range(NCH):
                    v.wait_ge(te.h, te_vq[("q", oc)])
                    reg = psA[:, (oc % 4) * 512:(oc % 4) * 512 + 256]
                    v.tensor_copy(ar_t.ap()[0:64, o_QF + (2 * oc) * TILE:o_QF + (2 * oc + 1) * TILE],
                                  reg[0:64, :])
                    v.drain()
                    v.tensor_copy(ar_t.ap()[0:64, o_QF + (2 * oc + 1) * TILE:o_QF + (2 * oc + 2) * TILE],
                                  reg[64:128, :])
                    I(ve, v.drain())
            qv_evac_d[tj] = ve.n

        emit_ln1_xf(0)
        pend_tail = [None]

        for ti in range(min(B * NT, DBG_A_TILES)):
            b, T = divmod(ti, NT)
            par = ti % 2
            q0 = T * TILE
            n = T
            slot = n % 2
            pslot = (n - 1) % 2
            xtv = V(o_xt + par * NCH * TILE, NCH * TILE)

            if ti == 0:
                emit_vq(0)
            te_v = {}

            @block.tensor
            def _(t, ti=ti):
                t.wait_ge(ve.h, qv_evac_d[ti])
                if ti >= 1:
                    t.wait_ge(ve.h, aof_done_d[ti - 1])
                    t.wait_ge(sc.h, sc_gate_d[ti - 1])
                ve0k = ve.n
                for oc in range(NCH):
                    reg = psA[:, (oc % 4) * 512:(oc % 4) * 512 + 256]
                    if oc >= 4:
                        t.wait_ge(ve.h, ve0k + oc - 3)
                    for c in range(NCH):
                        mm = t.matmul(reg, wv_(o_wk + c * 1024 + oc * 128, 128),
                                      V(o_XF + c * TILE, TILE),
                                      start=(c == 0), stop=(c == NCH - 1))
                    te_v[("k", oc)] = I(te, mm)
                for oc in range(4):
                    reg = psB[:, (oc % 2) * 512:(oc % 2) * 512 + 256]
                    if oc >= 2:
                        t.wait_ge(ve.h, ve0k + NCH + oc - 1)
                    for c in range(NCH):
                        mm = t.matmul(reg, wv_(o_wgq + c * 512 + oc * 128, 128),
                                      V(o_XF + c * TILE, TILE),
                                      start=(c == 0), stop=(c == NCH - 1))
                    te_v[("gq", oc)] = I(te, mm)
                for qc in range(QC):
                    reg = psC[:, 64:96] if qc == 0 else psD[:, 64:96]
                    for h in range(H):
                        mm = t.matmul(reg[:, h * 2:h * 2 + 2],
                                      ar_t.ap()[0:64, o_QF + h * TILE + qc * 128:
                                                o_QF + h * TILE + qc * 128 + 128],
                                      glgrS[:, h * 2:h * 2 + 2],
                                      start=True, stop=True)
                    te_v[("gate", qc)] = I(te, mm)
            te_projq_d[ti] = te.n

            @block.vector
            def _(v, T=T, slot=slot):
                for oc in range(NCH):
                    v.wait_ge(te.h, te_v[("k", oc)])
                    reg = psA[:, (oc % 4) * 512:(oc % 4) * 512 + 256]
                    v.tensor_scalar(ar_t.ap()[0:64, o_KR + (2 * oc) * 2 * W + slot * W:
                                              o_KR + (2 * oc) * 2 * W + (slot + 1) * W],
                                    reg[0:64, :], ISQ, None, ALU.mult)
                    v.drain()
                    v.tensor_scalar(ar_t.ap()[0:64, o_KR + (2 * oc + 1) * 2 * W + slot * W:
                                              o_KR + (2 * oc + 1) * 2 * W + (slot + 1) * W],
                                    reg[64:128, :], ISQ, None, ALU.mult)
                    I(ve, v.drain())
                for oc in range(4):
                    v.wait_ge(te.h, te_v[("gq", oc)])
                    reg = psB[:, (oc % 2) * 512:(oc % 2) * 512 + 256]
                    v.tensor_copy(ar_t.ap()[0:64, o_GQF + (2 * oc) * TILE:o_GQF + (2 * oc + 1) * TILE],
                                  reg[0:64, :])
                    v.drain()
                    v.tensor_copy(ar_t.ap()[0:64, o_GQF + (2 * oc + 1) * TILE:o_GQF + (2 * oc + 2) * TILE],
                                  reg[64:128, :])
                    I(ve, v.drain())
                for c in range(NCH):
                    v.tensor_reduce(
                        stat6[:, 0:8],
                        V(o_XF + c * TILE, TILE).rearrange("p (s w) -> p s w", w=POOL),
                        axis=AX, op=ALU.add)
                    v.drain()
                    v.tensor_scalar(V(o_XPOOL + c * NL + T * 8, 8),
                                    stat6[:, 0:8], 1.0 / POOL, None, ALU.mult)
                    v.drain()
                I(ve, v.drain())
            kg_evac = ve.n
            kg_evac_d[ti] = kg_evac

            if pend_tail[0] is not None:
                pend_tail[0]()
                pend_tail[0] = None

            if ti + 1 < B * NT:
                b2, T2 = divmod(ti + 1, NT)
                p2 = (ti + 1) % 2

                @block.gpsimd
                def _(g, b2=b2, T2=T2, p2=p2, ti=ti):
                    if ti >= 1:
                        g.wait_ge(ve.h, ve_after_xuse[ti - 1])
                    I(dx[p2], g.dma_start(
                        V(o_xt + p2 * NCH * TILE, NCH * TILE).rearrange(
                            "p (tc f) -> p tc f", tc=QC),
                        x_in[b2, T2 * TILE:(T2 + 1) * TILE, :].rearrange(
                            "(tc p) f -> p tc f", p=128)), 16)

            sc_gate = {}

            @block.scalar
            def _(s):
                for qc in range(QC):
                    s.wait_ge(te.h, te_v[("gate", qc)])
                    reg = psC[:, 64:96] if qc == 0 else psD[:, 64:96]
                    dst = glgrT.ap()[:, qc * 32:(qc + 1) * 32].rearrange(
                        "p (g h) -> p h g", g=2)
                    sc_gate[qc] = I(sc, s.activation(dst, reg, AF.Sigmoid))
            sc_gate_d[ti] = sc_gate[QC - 1]

            @block.tensor
            def _(t, T=T):
                t.wait_ge(ve.h, kg_evac)
                t.wait_ge(sc.h, sc_gate[QC - 1])
                for oc in range(4):
                    reg = psA[:, oc * 512:oc * 512 + 8]
                    for c in range(NCH):
                        mm = t.matmul(reg, wv_(o_wgk + c * 512 + oc * 128, 128),
                                      V(o_XPOOL + c * NL + T * 8, 8),
                                      start=(c == 0), stop=(c == NCH - 1))
                    te_v[("gk", oc)] = I(te, mm)
                for og in range(2):
                    for c in range(NCH):
                        mm = t.matmul(psD[0:8, 256 + og * 128:256 + (og + 1) * 128],
                                      V(o_XPOOL + c * NL + T * 8, 8),
                                      wv_(o_wgv + c * 512 + og * 128, 128),
                                      start=(c == 0), stop=(c == NCH - 1))
                    if og == 0:
                        I(te, mm)
                te_v[("gv",)] = I(te, mm)
                for og in range(2):
                    for c in range(NCH):
                        mm = t.matmul(psC[0:8, 256 + og * 128:256 + (og + 1) * 128],
                                      V(o_XPOOL + c * NL + T * 8, 8),
                                      wv_(o_wgv + c * 512 + 256 + og * 128, 128),
                                      start=(c == 0), stop=(c == NCH - 1))
                    if og == 0:
                        I(te, mm)
                te_v[("gv2",)] = I(te, mm)

            @block.vector
            def _(v, T=T, par=par, ti=ti):
                for oc in range(4):
                    v.wait_ge(te.h, te_v[("gk", oc)])
                    reg = psA[:, oc * 512:oc * 512 + 8]
                    v.tensor_scalar(ar_t.ap()[0:64, o_GKF + (2 * oc) * NL + T * 8:
                                              o_GKF + (2 * oc) * NL + T * 8 + 8],
                                    reg[0:64, :], ISQ, None, ALU.mult)
                    v.drain()
                    v.tensor_scalar(ar_t.ap()[0:64, o_GKF + (2 * oc + 1) * NL + T * 8:
                                              o_GKF + (2 * oc + 1) * NL + T * 8 + 8],
                                    reg[64:128, :], ISQ, None, ALU.mult)
                    I(ve, v.drain())
                v.wait_ge(te.h, te_v[("gv2",)])
                v.wait_ge(dgv[par].h, dgv[par].n)
                v.tensor_copy(ar_t.ap()[64:72, o_gvtmp:o_gvtmp + 256], psD[0:8, 256:512])
                v.drain()
                v.tensor_copy(ar_t.ap()[64:72, o_gvtmp + 256:o_gvtmp + 512], psC[0:8, 256:512])
                I(ve, v.drain())
            gv_evac = ve.n

            @block.gpsimd
            def _(g, T=T, par=par, ti=ti):
                g.wait_ge(ve.h, gv_evac)
                if ti >= 1:
                    g.wait_ge(te.h, te_after_latAV[ti - 1])
                g.wait_ge(dtr[0].h, dtr[0].n)
                g.wait_ge(dtr[1].h, dtr[1].n)
                I(dgv[par], g.dma_start(ar_t.ap()[T * 8:T * 8 + 8, o_GVT:o_GVT + 512],
                                        ar_t.ap()[64:72, o_gvtmp:o_gvtmp + 512]), 16)

            te_lat = {}

            @block.tensor
            def _(t, T=T, par=par):
                t.wait_ge(dgv[par].h, dgv[par].n)
                t.wait_ge(ve.h, gv_evac)
                sc0 = sc.n
                for lh in range(LH):
                    reg = psA[0:64, 1024 + (lh % 2) * 512:1024 + (lh % 2) * 512 + 256]
                    if lh >= 2:
                        t.wait_ge(sc.h, sc0 + lh - 1)
                    mm = t.matmul(reg, ar_t.ap()[0:64, o_GKF + lh * NL:o_GKF + (lh + 1) * NL],
                                  ar_t.ap()[0:64, o_GQF + lh * TILE:o_GQF + (lh + 1) * TILE],
                                  start=True, stop=True)
                    te_lat[lh] = I(te, mm)

            sc_lexp = {}

            @block.scalar
            def _(s):
                for lh in range(LH):
                    s.wait_ge(te.h, te_lat[lh])
                    reg = psA[0:64, 1024 + (lh % 2) * 512:1024 + (lh % 2) * 512 + 256]
                    sc_lexp[lh] = I(sc, s.activation(
                        ar_t.ap()[0:64, o_expL + lh * TILE:o_expL + (lh + 1) * TILE],
                        reg, AF.Exp))

            @block.vector
            def _(v, T=T):
                for lh in range(LH):
                    v.wait_ge(sc.h, sc_lexp[lh])
                    v.tensor_tensor(ar_t.ap()[0:64, o_expL + lh * TILE:o_expL + (lh + 1) * TILE],
                                    ar_t.ap()[0:64, o_expL + lh * TILE:o_expL + (lh + 1) * TILE],
                                    mlatS[:, T * TILE:(T + 1) * TILE], op=ALU.mult)
                    I(ve, v.drain())
            latexp_done = ve.n

            te_lav = {}

            @block.tensor
            def _(t):
                t.wait_ge(ve.h, latexp_done)
                ve0 = ve.n
                for qc in range(QC):
                    if qc == 1:
                        t.wait_ge(ve.h, ve0 + 1)
                    mmB = None
                    for lh in range(LH):
                        st = ar_t.ap()[0:64, o_expL + lh * TILE + qc * 128:
                                       o_expL + lh * TILE + qc * 128 + 128]
                        mmA = t.matmul(psB[:, lh * 64:(lh + 1) * 64], st,
                                       ar_t.ap()[0:64, o_GVT + lh * 64:o_GVT + (lh + 1) * 64],
                                       start=True, stop=True)
                        mmB = t.matmul(psC[:, 32 + qc * 8 + lh:32 + qc * 8 + lh + 1], st,
                                       onesS[0:64, 0:1], start=True, stop=True)
                    I(te, mmA)
                    te_lav[qc] = I(te, mmB)
                te_after_latAV[ti] = te.n

            @block.vector
            def _(v):
                for qc in range(QC):
                    v.wait_ge(te.h, te_lav[qc])
                    v.reciprocal(rden8[:, qc * 8:(qc + 1) * 8],
                                 psC[:, 32 + qc * 8:32 + (qc + 1) * 8])
                    v.drain()
                    rv = rden8.ap()[:, qc * 8:(qc + 1) * 8].unsqueeze(2).broadcast_to((128, 8, 64))
                    v.tensor_tensor(
                        V(o_remTM + qc * 1024, 512).rearrange("p (l e) -> p l e", l=8),
                        psB[:, 0:512].rearrange("p (l e) -> p l e", l=8), rv, op=ALU.mult)
                    I(ve, v.drain())
            remnorm_done = ve.n

            def emit_remgate():
                @block.vector
                def _(v, par=par):
                    for qc in range(QC):
                        v.wait_ge(sc.h, sc_gate[qc])
                        v.wait_ge(dtr[par].h, rem_marks['remtm'])
                        rrv = rden8.ap()[:, qc * 8:(qc + 1) * 8].unsqueeze(2) \
                            .broadcast_to((128, 8, 2))
                        v.tensor_tensor(rr16[:, qc * 16:(qc + 1) * 16],
                                        glgrT[:, qc * 32 + 16:qc * 32 + 32], rrv, op=ALU.mult)
                        v.drain()
                        rrv2 = rr16.ap()[:, qc * 16:(qc + 1) * 16].unsqueeze(2).broadcast_to((128, 16, 64))
                        v.tensor_tensor(
                            V(o_aoTM + qc * 1024, 1024).rearrange("p (h e) -> p h e", h=H),
                            V(o_aoTM + qc * 1024, 1024).rearrange("p (h e) -> p h e", h=H),
                            rrv2, op=ALU.mult)
                        I(ve, v.drain())
                remgate_done = ve.n

            rem_marks = {}

            def emit_remote_tail():
                @block.sync
                def _(sy, par=par):
                    sy.wait_ge(ve.h, remnorm_done)
                    trbatch(sy, par, [
                        (V(o_remNF, 4 * TILE).rearrange("p (c t) -> p c t", c=4)[:, :, qc * 128:(qc + 1) * 128],
                         V(o_remTM + qc * 1024, 512)) for qc in range(QC)])
                remnf_done = dtr[par].n

                te_ro = {}

                @block.tensor
                def _(t, par=par):
                    t.wait_ge(dtr[par].h, remnf_done)
                    sc0 = sc.n
                    for oc in range(NCH):
                        reg = psD[:, 0:256] if oc % 2 == 0 else psC[:, 256:512]
                        if oc >= 2:
                            t.wait_ge(sc.h, sc0 + oc - 1)
                        for c in range(4):
                            mm = t.matmul(reg, wv_(o_ro + c * 1024 + oc * 128, 128),
                                          V(o_remNF + c * TILE, TILE),
                                          start=(c == 0), stop=(c == 3))
                        te_ro[oc] = I(te, mm)

                @block.scalar
                def _(s):
                    for oc in range(NCH):
                        s.wait_ge(te.h, te_ro[oc])
                        reg = psD[:, 0:256] if oc % 2 == 0 else psC[:, 256:512]
                        for qc in range(QC):
                            s.activation(V(o_remPF + qc * 1024 + oc * 128, 128),
                                         reg[:, qc * 128:(qc + 1) * 128], AF.Copy)
                            s.drain()
                        I(sc, s.drain())
                rem_marks['rempf'] = sc.n

                @block.sync
                def _(sy, par=par):
                    sy.wait_ge(sc.h, rem_marks['rempf'])
                    trbatch(sy, par, [
                        (V(o_aoTM + qc * 1024, 1024).rearrange("p (c f) -> p c f", c=NCH),
                         V(o_remPF + qc * 1024, 1024)) for qc in range(QC)])
                rem_marks['remtm'] = dtr[par].n

            mix_marks = {}
            kc_all = [2, 3] if n == 0 else [0, 1, 2, 3]
            for hf in range(4):
                if hf == 2 and ti + 1 < B * NT:
                    emit_ln1_xf(ti + 1)
                te_sb = {}
                sc_exp = {}
                sregs = [psA[:, 1024:1536], psA[:, 1536:2048]]

                @block.tensor
                def _(t, hf=hf, slot=slot, pslot=pslot, n=n, kc_all=kc_all):
                    t.wait_ge(ve.h, kg_evac)
                    sc0 = sc.n
                    nb = 0
                    for kc in kc_all:
                        sl = pslot if kc < 2 else slot
                        koff = (kc % 2) * 128
                        lq = 0 if kc == 0 else (1 if kc == 3 else None)
                        for hg in range(2):
                            reg = sregs[nb % 2]
                            if nb >= 2:
                                t.wait_ge(sc.h, sc0 + nb - 1)
                            mm = None
                            for hh in range(2):
                                h = hf * 4 + hg * 2 + hh
                                q0c = 0 if lq is None else lq * 128
                                qn = TILE if lq is None else 128
                                mm = t.matmul(
                                    reg[:, hh * 256 + q0c:hh * 256 + q0c + qn],
                                    ar_t.ap()[0:64,
                                              o_KR + h * 2 * W + sl * W + koff:
                                              o_KR + h * 2 * W + sl * W + koff + 128],
                                    ar_t.ap()[0:64,
                                              o_QF + h * TILE + q0c:
                                              o_QF + h * TILE + q0c + qn],
                                    start=True, stop=True)
                            te_sb[nb] = I(te, mm)
                            nb += 1

                nbt = len(kc_all) * 2

                @block.scalar
                def _(s, hf=hf, kc_all=kc_all, nbt=nbt):
                    for nb in range(nbt):
                        s.wait_ge(te.h, te_sb[nb])
                        reg = sregs[nb % 2]
                        kc = kc_all[nb // 2]
                        hg = nb % 2
                        dst = ar_t.ap()[:, o_expP + kc * 4 * TILE + hg * 2 * TILE:
                                        o_expP + kc * 4 * TILE + (hg + 1) * 2 * TILE]
                        if kc in (0, 3):
                            lq = 0 if kc == 0 else 1
                            d3 = dst.rearrange("p (g q) -> p g q", g=2)[
                                :, :, lq * 128:(lq + 1) * 128]
                            r3 = reg.rearrange("p (g q) -> p g q", g=2)[
                                :, :, lq * 128:(lq + 1) * 128]
                            sc_exp[nb] = I(sc, s.activation(d3, r3, AF.Exp))
                        else:
                            sc_exp[nb] = I(sc, s.activation(dst, reg, AF.Exp))

                @block.vector
                def _(v, hf=hf, kc_all=kc_all, nbt=nbt):
                    for nb in range(nbt):
                        v.wait_ge(sc.h, sc_exp[nb])
                        kc = kc_all[nb // 2]
                        hg = nb % 2
                        mq = 0 if kc in (0, 2) else 1
                        dst = ar_t.ap()[:, o_expP + kc * 4 * TILE + hg * 2 * TILE:
                                        o_expP + kc * 4 * TILE + (hg + 1) * 2 * TILE]
                        d3 = dst.rearrange("p (g q) -> p g q", g=2)[
                            :, :, mq * 128:(mq + 1) * 128]
                        mkv = mlocS.ap()[:, kc * 256 + mq * 128:kc * 256 + mq * 128 + 128] \
                            .unsqueeze(1).broadcast_to((128, 2, 128))
                        v.tensor_tensor(d3, d3, mkv, op=ALU.mult)
                        I(ve, v.drain())
                locexp_done = ve.n

                if hf == 0:
                    emit_remote_tail()
                    emit_remgate()

                @block.tensor
                def _(t, hf=hf, slot=slot, pslot=pslot, n=n, kc_all=kc_all):
                    t.wait_ge(ve.h, locexp_done)
                    if hf == 0:
                        t.wait_ge(ve.h, remnorm_done)
                        t.wait_ge(sc.h, rem_marks['rempf'])
                    for qc in range(QC):
                        avreg = psB[:, 0:1024] if qc == 0 else psA[:, 0:1024]
                        dreg = psC[:, 0:16] if qc == 0 else psD[:, 0:16]
                        live = [kc for kc in kc_all
                                if not ((kc == 0 and qc == 1) or (kc == 3 and qc == 0))]
                        for hh in range(4):
                            h = hf * 4 + hh
                            for ki, kc in enumerate(live):
                                st = ar_t.ap()[:, o_expP + kc * 4 * TILE + hh * TILE + qc * 128:
                                               o_expP + kc * 4 * TILE + hh * TILE + qc * 128 + 128]
                                vsl = (slot * 2 + kc - 2) if kc >= 2 else (pslot * 2 + kc)
                                first = ki == 0
                                last = ki == len(live) - 1
                                mmA = t.matmul(avreg[:, h * 64:(h + 1) * 64], st,
                                               V(o_VR + vsl * 1024 + h * 64, 64),
                                               start=first, stop=last)
                                mmB = t.matmul(dreg[:, h:h + 1], st,
                                               onesS[:, 0:1], start=first, stop=last)
                        I(te, mmA)
                        mix_marks[(hf, qc)] = I(te, mmB)

            exp_last_sc_d[ti] = sc.n

            @block.vector
            def _(v, par=par):
                for qc in range(QC):
                    v.wait_ge(te.h, mix_marks[(3, qc)])
                    if qc == 0:
                        v.wait_ge(dx2s.h, dx2s.n)
                    avreg = psB[:, 0:1024] if qc == 0 else psA[:, 0:1024]
                    dreg = psC[:, 0:16] if qc == 0 else psD[:, 0:16]
                    v.reciprocal(rl16[:, qc * 16:(qc + 1) * 16], dreg)
                    v.drain()
                    v.tensor_tensor(rl16[:, qc * 16:(qc + 1) * 16],
                                    glgrT[:, qc * 32:qc * 32 + 16],
                                    rl16[:, qc * 16:(qc + 1) * 16], op=ALU.mult)
                    v.drain()
                    rlv = rl16.ap()[:, qc * 16:(qc + 1) * 16].unsqueeze(2).broadcast_to((128, 16, 64))
                    v.tensor_tensor(
                        V(o_x2st + qc * 1024, 1024).rearrange("p (h e) -> p h e", h=H),
                        avreg.rearrange("p (h e) -> p h e", h=H), rlv, op=ALU.mult)
                    v.drain()
                    v.tensor_tensor(V(o_x2st + qc * 1024, 1024),
                                    V(o_x2st + qc * 1024, 1024),
                                    V(o_aoTM + qc * 1024, 1024), op=ALU.add)
                    I(ve, v.drain())
            mix_done = ve.n
            mix_done_d[ti] = mix_done

            if ti + 1 < B * NT:
                emit_vq(ti + 1)

            @block.sync
            def _(sy, par=par):
                sy.wait_ge(ve.h, mix_done)
                trbatch(sy, par, [
                    (V(o_mixF, NCH * TILE).rearrange("p (c t) -> p c t", c=NCH)[:, :, qc * 128:(qc + 1) * 128],
                     V(o_x2st + qc * 1024, 1024)) for qc in range(QC)])
            mixf_done = dtr[par].n

            te_wo = {}

            @block.tensor
            def _(t, par=par):
                t.wait_ge(dtr[par].h, mixf_done)
                ve0wo = ve.n
                for oc in range(NCH):
                    reg = psB[:, (oc % 4) * 256:(oc % 4) * 256 + 256]
                    if oc >= 4:
                        t.wait_ge(ve.h, ve0wo + ((oc - 4) | 1) + 1)
                    for c in range(NCH):
                        mm = t.matmul(reg, wv_(o_wo + c * 1024 + oc * 128, 128),
                                      V(o_mixF + c * TILE, TILE),
                                      start=(c == 0), stop=(c == NCH - 1))
                    te_wo[oc] = I(te, mm)
            te_after_tile[ti] = te.n

            @block.vector
            def _(v):
                for oc in range(NCH):
                    v.wait_ge(te.h, te_wo[min(oc | 1, NCH - 1)])
                    reg = psB[:, (oc % 4) * 256:(oc % 4) * 256 + 256]
                    aov = V(o_aoF, QC * 1024).rearrange(
                        "p (qc c f) -> p qc c f", qc=QC, c=NCH)[:, :, oc, :]
                    v.tensor_scalar(aov, reg.rearrange("p (qc f) -> p qc f", qc=QC),
                                    1.0, None, ALU.mult)
                    I(ve, v.drain())
            aof_done = ve.n
            aof_done_d[ti] = aof_done

            def emit_tail_store(par=par, ti=ti, xtv=xtv, b=b, q0=q0,
                                aof_done=aof_done):
                @block.sync
                def _(sy):
                    sy.wait_ge(ve.h, aof_done)
                    trbatch(sy, par, [
                        (V(o_x2st + qc * 1024, 1024).rearrange("p (c f) -> p c f", c=NCH),
                         V(o_aoF + qc * 1024, 1024)) for qc in range(QC)])
                aotm_done = dtr[par].n

                @block.vector
                def _(v):
                    v.wait_ge(dtr[par].h, aotm_done)
                    for qc in range(QC):
                        v.tensor_tensor(V(o_x2st + qc * 1024, 1024),
                                        xtv[:, qc * 1024:(qc + 1) * 1024],
                                        V(o_x2st + qc * 1024, 1024), op=ALU.add)
                        v.drain()
                    I(ve, v.drain())
                    ve_after_xuse[ti] = ve.n
                x2_ready = ve.n

                @block.sync
                def _(sy):
                    sy.wait_ge(ve.h, x2_ready)
                    I(dx2s, sy.dma_start(
                        x2_dram[b, q0:q0 + TILE, :].rearrange("(tc p) f -> p tc f", p=128),
                        V(o_x2st, QC * 1024).rearrange("p (tc f) -> p tc f", tc=QC)), 16)

            pend_tail[0] = emit_tail_store
            if ti + 1 >= min(B * NT, DBG_A_TILES):
                emit_tail_store()
                pend_tail[0] = None

        ve_phA = ve.n
        sc_phA = sc.n
        te_phA = te.n

        deq_sc = {}
        deq_ve = {}
        deq_te = {}
        scc_mark = {}
        q_mark = {}
        rsel_mark = {}
        deq_te_pred = {}
        deq_ve_pred = {}
        _ten = te.n
        _ven = ve.n
        _scn = sc.n
        deq_sc_pred = {}
        for _i in range(6):
            for _c in range(NCH):
                _ten += 1
                _ven += 1
                _scn += 1
                deq_te_pred[(_i, _c)] = _ten
                deq_ve_pred[(_i, _c)] = _ven
                deq_sc_pred[(_i, _c)] = _scn

        @block.gpsimd
        def _(g):
            if not DBG_B_PRO:
                return
            I(dwa, g.dma_start(laS.ap().rearrange("p (i c r) -> p i c r", i=6, c=NCH),
                               laT_d.rearrange("i (c p) r -> p i c r", p=128)), 16)
            I(dwa, g.dma_start(lbS.ap().rearrange("p (i o) -> p i o", i=3),
                               lb4_d.transpose([1, 0, 2])), 16)
            I(dwa, g.dma_start(biasS.ap().rearrange("p (i c) -> p i c", i=6),
                               biasT_d.transpose([1, 0, 2])), 16)
            I(dwa, g.dma_start(lngS.ap().rearrange("p (i c) -> p i c", i=3),
                               lngT_d.transpose([1, 0, 2])), 16)
            I(dwa, g.dma_start(lnbS.ap().rearrange("p (i c) -> p i c", i=3),
                               lnbT_d.transpose([1, 0, 2])), 16)
            g.wait_ge(ve.h, ve_phA)
            g.wait_ge(sc.h, sc_phA)
            g.wait_ge(te.h, te_phA)
            g.wait_ge(dx2s.h, dx2s.n)
            g.wait_ge(dtr[0].h, dtr[0].n)
            g.wait_ge(dtr[1].h, dtr[1].n)
            rsel_mark[0] = I(drsel, g.dma_start(
                ar_t.ap()[0:64, o_rsel:o_rsel + 1024], rsel_d[:, :]), 16)
            for i in range(6):
                if i >= 2:
                    g.wait_ge(te.h, deq_te_pred[(i - 2, NCH - 1)])
                scc_mark[i] = I(dscc[i % 2], g.dma_start(
                    ar_t.ap()[0:64, o_scc + (i % 2) * 1024:o_scc + (i % 2) * 1024 + 1024],
                    scT_d[i]), 16)
                for c in range(NCH):
                    k = i * NCH + c
                    par = k % 2
                    if k >= 2:
                        pk = k - 2
                        g.wait_ge(ve.h, deq_ve_pred[(pk // NCH, pk % NCH)])
                    q_mark[(i, c)] = I(dwb[par], g.dma_start(
                        V(o_qtmp + par * 1024, 1024),
                        qT_d[i, c * 128:(c + 1) * 128, :]), 16)

        @block.tensor
        def _(t):
            if not DBG_B_PRO:
                return
            t.wait_ge(drsel.h, rsel_mark[0])
            for i in range(6):
                t.wait_ge(dscc[i % 2].h, scc_mark[i])
                for c in range(NCH):
                    k = i * NCH + c
                    reg = psA[:, (k % 2) * 1024:(k % 2) * 1024 + 1024]
                    if k >= 2:
                        pk = k - 2
                        t.wait_ge(sc.h, deq_sc_pred[(pk // NCH, pk % NCH)])
                    for og in range(2):
                        mm = t.matmul(reg[:, og * 512:(og + 1) * 512],
                                      ar_t.ap()[0:64, o_rsel + c * 128:o_rsel + (c + 1) * 128],
                                      ar_t.ap()[0:64, o_scc + (i % 2) * 1024 + og * 512:
                                                o_scc + (i % 2) * 1024 + (og + 1) * 512],
                                      start=True, stop=True)
                    deq_te[(i, c)] = I(te, mm)
                    assert deq_te[(i, c)] == deq_te_pred[(i, c)]

        @block.scalar
        def _(s):
            if not DBG_B_PRO:
                return
            for i in range(6):
                for c in range(NCH):
                    k = i * NCH + c
                    par = k % 2
                    s.wait_ge(te.h, deq_te[(i, c)])
                    if k >= 2:
                        pk = k - 2
                        s.wait_ge(ve.h, deq_ve_pred[(pk // NCH, pk % NCH)])
                    s.activation(V(o_sctmp + par * 1024, 1024),
                                 psA[:, par * 1024:par * 1024 + 1024], AF.Copy)
                    s.drain()
                    s.wait_ge(dwb[par].h, q_mark[(i, c)])
                    deq_sc[(i, c)] = I(sc, s.activation(V(o_aftmp + par * 1024, 1024),
                                                        V(o_qtmp + par * 1024, 1024),
                                                        AF.Copy, bias=-1.0, scale=1.0 / 7.5))
                    assert deq_sc[(i, c)] == deq_sc_pred[(i, c)]

        @block.vector
        def _(v):
            if not DBG_B_PRO:
                return
            v.wait_ge(te.h, te_phA)
            for i in range(6):
                for c in range(NCH):
                    k = i * NCH + c
                    par = k % 2
                    v.wait_ge(sc.h, deq_sc[(i, c)])
                    deq_ve[(i, c)] = I(ve, v.tensor_tensor(
                        wbig.ap()[:, i * 8192 + c * 1024:i * 8192 + (c + 1) * 1024],
                        V(o_aftmp + par * 1024, 1024),
                        V(o_sctmp + par * 1024, 1024), op=ALU.mult))
                    assert deq_ve[(i, c)] == deq_ve_pred[(i, c)]
        deq_done = ve.n

        wTv = lambda i, c, lo, nn: wbig.ap()[:, i * 8192 + c * 1024 + lo:i * 8192 + c * 1024 + lo + nn]

        pbank = {}
        pbank['A0'] = ('sc', deq_sc[(5, 6)])
        pbank['A1'] = ('sc', deq_sc[(5, 6)])
        pbank['A2'] = ('sc', deq_sc[(5, 7)])
        pbank['A3'] = ('sc', deq_sc[(5, 7)])
        pbank['B4'] = ('ve', aof_done_d[B * NT - 1]) if (B * NT - 1) in aof_done_d else None
        pbank['B5'] = pbank['B4']
        pbank['C6'] = ('ve', mix_done_d[B * NT - 1]) if (B * NT - 1) in mix_done_d else None
        pbank['D7'] = pbank['C6']
        pbank = {k: v for k, v in pbank.items() if v}

        def bank_wait(t, b):
            if b in pbank:
                eng, val = pbank[b]
                t.wait_ge({'ve': ve.h, 'sc': sc.h, 'te': te.h,
                           'ev1': ev1.h, 'ev2': ev2.h}[eng], val)

        def bank_set(b, eng, val):
            pbank[b] = (eng, val)

        YREG = {'A0': (psA, 0), 'A1': (psA, 512), 'A2': (psA, 1024), 'A3': (psA, 1536),
                'B4': (psB, 0), 'B5': (psB, 512)}

        ln2_done_d = {}
        h0_done_d = {}
        hfin_ve_d = {}
        hfin_ev2_d = {}
        htm_done_d = {}
        var_ve_d = {}
        nbt_total = min(B * NT, DBG_B_TILES) if DBG_B_PRO else 0

        X2SLOT = lambda tj: (o_x2t + (tj % 3) * NCH * TILE
                             if tj % 3 < 2 else o_x2t3)

        def emit_x2load(tj):
            bj, Tj = divmod(tj, NT)
            pj = tj % 2

            @block.gpsimd
            def _(g, tj=tj, bj=bj, Tj=Tj, pj=pj):
                if tj == 0:
                    g.wait_ge(dx2s.h, dx2s.n)
                if tj >= 3:
                    g.wait_ge(ve.h, ve_after_x2[tj - 3])
                g.wait_ge(dx2l[pj].h, dx2l[pj].n)
                I(dx2l[pj], g.dma_start(
                    V(X2SLOT(tj), NCH * TILE).rearrange("p (tc f) -> p tc f", tc=QC),
                    x2_dram[bj, Tj * TILE:(Tj + 1) * TILE, :].rearrange(
                        "(tc p) f -> p tc f", p=128)), 16)

        def emit_ln2(tj):
            pj = tj % 2
            x2vj = V(X2SLOT(tj), NCH * TILE)
            ve_stats2 = {}

            @block.vector
            def _(v, tj=tj, pj=pj, x2vj=x2vj):
                v.wait_ge(dx2l[pj].h, dx2l[pj].n)
                if tj == 0:
                    v.wait_ge(ve.h, deq_done)
                    v.wait_ge(dwa.h, dwa.n)
                for tc in range(QC):
                    xv = x2vj[:, tc * 1024:(tc + 1) * 1024]
                    v.bn_stats(stat6[:, 0:6], xv[:, 0:512])
                    v.drain()
                    v.bn_stats(stat6[:, 6:12], xv[:, 512:1024])
                    v.drain()
                    v.bn_aggr(stat2[:, 2 * tc:2 * tc + 2], stat6[:, 0:12])
                    v.drain()
                    v.tensor_scalar(stat2[:, 2 * tc + 1:2 * tc + 2],
                                    stat2[:, 2 * tc + 1:2 * tc + 2], EPS, None, ALU.add)
                    v.drain()
                    ve_stats2[tc] = I(ve, v.drain())

            sc_sig2 = {}

            @block.scalar
            def _(s):
                for tc in range(QC):
                    s.wait_ge(ve.h, ve_stats2[tc])
                    sc_sig2[tc] = I(sc, s.activation(sigv[:, tc:tc + 1],
                                                     stat2[:, 2 * tc + 1:2 * tc + 2],
                                                     AF.Sqrt))

            @block.vector
            def _(v, x2vj=x2vj, tj=tj, pj=pj):
                if tj >= 1:
                    v.wait_ge(dtr[1 - pj].h, h0_done_d[tj - 1])
                for tc in range(QC):
                    v.wait_ge(sc.h, sc_sig2[tc])
                    v.reciprocal(invv[:, tc:tc + 1], sigv[:, tc:tc + 1])
                    v.drain()
                    v.tensor_scalar(V(o_xn2 + tc * 1024, 1024),
                                    x2vj[:, tc * 1024:(tc + 1) * 1024],
                                    stat2[:, 2 * tc:2 * tc + 1], invv[:, tc:tc + 1],
                                    ALU.subtract, ALU.mult)
                    v.drain()
                I(ve, v.drain())
            ln2_done_d[tj] = ve.n

        def emit_h0(tj):
            pj = tj % 2

            @block.sync
            def _(sy, tj=tj, pj=pj):
                sy.wait_ge(ve.h, ln2_done_d[tj])
                if tj >= 2:
                    sy.wait_ge(ev2.h, hfin_ev2_d[tj - 2])
                    sy.wait_ge(te.h, te_after_tile[(tj - 2, 'B')])
                ha = (2 * (tj % 2)) * NCH * TILE
                trbatch(sy, pj, [
                    (V(o_hF + ha, NCH * TILE).rearrange(
                        "p (c t) -> p c t", c=NCH)[:, :, tc * 128:(tc + 1) * 128],
                     V(o_xn2 + tc * 1024, 1024)) for tc in range(QC)])
            h0_done_d[tj] = dtr[pj].n

        if nbt_total > 0:
            emit_x2load(0)
            if nbt_total > 1:
                emit_x2load(1)
            emit_ln2(0)
            emit_h0(0)
            if nbt_total > 1:
                emit_ln2(1)
                emit_h0(1)

        Y1BANKS = ['A0', 'A0', 'A1', 'A1']
        Y2BANKS = ['A3', 'A3', 'B4', 'B4', 'B5', 'B5']
        STATS = {0: ('D7', psD, 0), 1: ('A2', psA, 1024)}

        Y1REGS = [psA[:, r * 256:(r + 1) * 256] for r in range(4)]
        Y2REGS = ([psA[:, 1536 + r * 256:1536 + (r + 1) * 256] for r in range(2)]
                  + [psB[:, r * 256:(r + 1) * 256] for r in range(4)])
        last_yout = {'ve': 0}

        def b_state(ti):
            b, T = divmod(ti, NT)
            pp = ti % 2
            return dict(
                ti=ti, b=b, T=T, pp=pp, q0=T * TILE,
                x2v=V(X2SLOT(ti), NCH * TILE),
                y1F=(o_y1F if pp == 0 else o_t1F),
                gF=(o_gF if pp == 0 else o_qtmp),
                sqF=(o_sqF if pp == 0 else o_aftmp),
                btr=(o_btr if pp == 0 else o_sctmp),
                brow=(o_brow if pp == 0 else o_sctmp + 1024),
                srow=(o_srow if pp == 0 else o_scc),
                ha=2 * pp, hb=2 * pp + 1,
                stb=STATS[pp],
            )

        def b_hio(st, sbk):
            a, bb = st['ha'], st['hb']
            hin = a if sbk != 1 else bb
            hout = bb if sbk != 1 else a
            return o_hF + hin * NCH * TILE, o_hF + hout * NCH * TILE

        def b_Y1a(st, sbk):
            i0 = 2 * sbk
            hin_o, _ = b_hio(st, sbk)
            ti, pp = st['ti'], st['pp']
            btr_o = st['btr']

            @block.tensor
            def _(t, i0=i0, hin_o=hin_o, sbk=sbk, ti=ti, pp=pp):
                if sbk == 0:
                    t.wait_ge(dtr[pp].h, h0_done_d[ti])
                    if ti == 0:
                        t.wait_ge(ve.h, deq_done)
                else:
                    t.wait_ge(ev2.h, st['h_ready'])
                bank_wait(t, 'C6')
                for c in range(NCH):
                    mm = t.matmul(psC[0:32, 0:TILE],
                                  laS[:, i0 * NCH * 32 + c * 32:i0 * NCH * 32 + (c + 1) * 32],
                                  V(hin_o + c * TILE, TILE),
                                  start=(c == 0), stop=(c == NCH - 1))
                I(te, mm)
            st['btr1_te'] = te.n

            @block.vector
            def _(v, i0=i0, btr_o=btr_o):
                v.wait_ge(te.h, st['btr1_te'])
                bp = (i0 % 2) * 64
                v.tensor_scalar(ar_t.ap()[bp:bp + 32, btr_o:btr_o + TILE],
                                psC[0:32, 0:TILE], 1.0, None, ALU.mult)
                bank_set('C6', 've', I(ve, v.drain()))
            st['btr1_ve'] = ve.n

        def b_Y1b(st, sbk):
            i0 = 2 * sbk
            hin_o, _ = b_hio(st, sbk)
            btr_o = st['btr']
            te_y1 = {}

            @block.tensor
            def _(t, i0=i0, hin_o=hin_o, btr_o=btr_o):
                ev0 = ev1.n
                for oc in range(NCH):
                    reg = Y1REGS[oc % 4]
                    if oc >= 4:
                        t.wait_ge(ev1.h, ev0 + ((oc - 4) | 1) + 1)
                    else:
                        bank_wait(t, Y1BANKS[oc % 4])
                    for c in range(NCH):
                        t.matmul(reg, wTv(i0, c, oc * 128, 128),
                                 V(hin_o + c * TILE, TILE),
                                 start=(c == 0), stop=False)
                    if oc == 0:
                        t.wait_ge(ve.h, st['btr1_ve'])
                    bp = (i0 % 2) * 64
                    mm = t.matmul(reg,
                                  lbS[bp:bp + 32,
                                      (i0 // 2) * 1024 + oc * 128:(i0 // 2) * 1024 + oc * 128 + 128],
                                  ar_t.ap()[bp:bp + 32, btr_o:btr_o + TILE], start=False, stop=True)
                    te_y1[oc] = I(te, mm)
            st['te_y1'] = te_y1

        def b_Y1(st, sbk):
            b_Y1a(st, sbk)
            b_Y1b(st, sbk)

        def b_L1e(st, sbk):
            i0 = 2 * sbk
            te_y1 = st['te_y1']
            y1o, sqo = st['y1F'], st['sqF']

            @block.vector
            def _(v, i0=i0, y1o=y1o):
                for oc in range(NCH):
                    v.wait_ge(te.h, te_y1[min(oc | 1, NCH - 1)])
                    v.tensor_scalar(V(y1o + oc * TILE, TILE),
                                    Y1REGS[oc % 4],
                                    biasS[:, i0 * 8 + oc:i0 * 8 + oc + 1], None, ALU.add)
                    bank_set(Y1BANKS[oc % 4], 'ev1', I(ev1, v.drain()))
            st['y1_evac'] = ev1.n

            @block.scalar
            def _(s, y1o=y1o, sqo=sqo, sro=st['srow']):
                s.wait_ge(ev1.h, st['y1_evac'])
                I(sc, s.activation(V(sqo, NCH * TILE), V(y1o, NCH * TILE), AF.Square))
                I(sc, s.activation(ar_t.ap()[0:1, sro + 4 * TILE:sro + 4 * TILE + 1],
                                   onesS[0:1, 0:1], AF.Sqrt))
            st['sq_sc'] = sc.n - 1

        def b_L1m(st, sbk):
            y1o, sqo = st['y1F'], st['sqF']
            sb_name, sb_ps, sb_off = st['stb']

            @block.tensor
            def _(t, y1o=y1o, sqo=sqo, sb_ps=sb_ps, sb_off=sb_off):
                t.wait_ge(ev1.h, st['y1_evac'])
                bank_wait(t, sb_name)
                for c in range(NCH):
                    mm = t.matmul(sb_ps[0:1, sb_off:sb_off + TILE], onesS[:, 0:1],
                                  V(y1o + c * TILE, TILE),
                                  start=(c == 0), stop=(c == NCH - 1))
                I(te, mm)
                t.wait_ge(sc.h, st['sq_sc'])
                for c in range(NCH):
                    mm = t.matmul(sb_ps[0:1, sb_off + 256:sb_off + 256 + TILE], onesS[:, 0:1],
                                  V(sqo + c * TILE, TILE),
                                  start=(c == 0), stop=(c == NCH - 1))
                I(te, mm)
            st['sums_te'] = te.n

        def b_L1v(st, sbk):
            sro, bro = st['srow'], st['brow']
            sb_name, sb_ps, sb_off = st['stb']
            sr = lambda a, bn: ar_t.ap()[0:1, sro + 2 * a:sro + 2 * a + 2 * bn].bitcast(F32)
            br = lambda a, bn: ar_t.ap()[0:1, bro + a:bro + a + bn]

            @block.vector
            def _(v, sb_ps=sb_ps, sb_off=sb_off):
                v.wait_ge(te.h, st['sums_te'])
                v.tensor_scalar(br(0, TILE), sb_ps[0:1, sb_off:sb_off + TILE],
                                -1.0 / D, None, ALU.mult)
                v.drain()
                v.tensor_tensor(sr(0, TILE), br(0, TILE), br(0, TILE), op=ALU.mult)
                v.drain()
                v.scalar_tensor_tensor(sr(TILE, TILE),
                                       sb_ps[0:1, sb_off + 256:sb_off + 256 + TILE],
                                       1.0 / D, sr(0, TILE),
                                       op0=ALU.mult, op1=ALU.subtract)
                v.drain()
                v.tensor_scalar(sr(TILE, TILE), sr(TILE, TILE), EPS, None, ALU.add)
                I(ve, v.drain())
            st['var_ve'] = ve.n

            @block.scalar
            def _(s):
                s.wait_ge(ve.h, st['var_ve'])
                with nc.allow_low_precision(reason="bf16 LN sigma for broadcast matmul"):
                    I(sc, s.activation(br(TILE, TILE), sr(TILE, TILE), AF.Sqrt))
            st['sig_sc'] = sc.n

        def b_L1c(st, sbk):
            bro, pp = st['brow'], st['pp']
            sb_name, sb_ps, sb_off = st['stb']
            br = lambda a, bn: ar_t.ap()[0:1, bro + a:bro + a + bn]

            @block.tensor
            def _(t, sb_ps=sb_ps, sb_off=sb_off):
                t.wait_ge(sc.h, st['sig_sc'])
                mm = t.matmul(sb_ps[:, sb_off:sb_off + 512], onesR[0:1, :], br(0, 512),
                              start=True, stop=True)
                I(te, mm)
            st['bc_te'] = te.n

            @block.vector
            def _(v, sb_ps=sb_ps, sb_off=sb_off, pp=pp):
                v.wait_ge(te.h, st['bc_te'])
                v.reciprocal(V(o_invb + pp * 512, 512).bitcast(F32),
                             sb_ps[:, sb_off + TILE:sb_off + 2 * TILE])
                I(ve, v.drain())
            st['inv_ve'] = ve.n

        def b_L2(st, sbk):
            y1o, go = st['y1F'], st['gF']
            sb_name, sb_ps, sb_off = st['stb']

            @block.vector
            def _(v, sbk=sbk, sb_ps=sb_ps, sb_off=sb_off, pp=st['pp']):
                v.wait_ge(te.h, st['bc_te'])
                mv = sb_ps.ap()[:, sb_off:sb_off + TILE].unsqueeze(1).broadcast_to(
                    (128, NCH, TILE))
                v.tensor_tensor(V(y1o, NCH * TILE).rearrange("p (c t) -> p c t", c=NCH),
                                V(y1o, NCH * TILE).rearrange("p (c t) -> p c t", c=NCH),
                                mv, op=ALU.add)
                v.drain()
                iv = V(o_invb + pp * 512, 512).bitcast(F32).unsqueeze(1).broadcast_to(
                    (128, NCH, TILE))
                v.tensor_tensor(V(go, NCH * TILE).rearrange("p (c t) -> p c t", c=NCH),
                                V(y1o, NCH * TILE).rearrange("p (c t) -> p c t", c=NCH),
                                iv, op=ALU.mult)
                I(ve, v.drain())
            st['t1_ve'] = ve.n
            bank_set(sb_name, 've', st['t1_ve'])

        def b_Ga(st, sbk):
            go = st['gF']

            @block.scalar
            def _(s, sbk=sbk, go=go, sro=st['srow']):
                s.wait_ge(ve.h, st['t1_ve'])
                for c in range(NCH):
                    I(sc, s.activation(
                        V(go + c * TILE, TILE), V(go + c * TILE, TILE), AF.Gelu,
                        bias=lnbS[:, sbk * 8 + c:sbk * 8 + c + 1],
                        scale=lngS[:, sbk * 8 + c:sbk * 8 + c + 1]))
                st['g_done'] = sc.n

        def b_Y2(st, sbk):
            i1 = 2 * sbk + 1
            go, btr_o = st['gF'], st['btr']

            @block.tensor
            def _(t, i1=i1, go=go):
                t.wait_ge(sc.h, st['g_done'])
                bank_wait(t, 'C6')
                for c in range(NCH):
                    mm = t.matmul(psC[0:32, 0:TILE],
                                  laS[:, i1 * NCH * 32 + c * 32:i1 * NCH * 32 + (c + 1) * 32],
                                  V(go + c * TILE, TILE),
                                  start=(c == 0), stop=(c == NCH - 1))
                I(te, mm)
            st['btr2_te'] = te.n

            @block.vector
            def _(v, i1=i1, btr_o=btr_o):
                v.wait_ge(te.h, st['btr2_te'])
                bp = (i1 % 2) * 64
                v.tensor_scalar(ar_t.ap()[bp:bp + 32, btr_o:btr_o + TILE],
                                psC[0:32, 0:TILE], 1.0, None, ALU.mult)
                bank_set('C6', 've', I(ve, v.drain()))
            st['btr2_ve'] = ve.n

            te_y2 = {}

            @block.tensor
            def _(t, i1=i1, go=go, btr_o=btr_o):
                ev0 = ev2.n
                for oc in range(NCH):
                    reg = Y2REGS[oc % 6]
                    if oc >= 6:
                        t.wait_ge(ev2.h, ev0 + ((oc - 6) | 1) + 1)
                    else:
                        bank_wait(t, Y2BANKS[oc % 6])
                    for c in range(NCH):
                        t.matmul(reg, wTv(i1, c, oc * 128, 128),
                                 V(go + c * TILE, TILE),
                                 start=(c == 0), stop=False)
                    if oc == 0:
                        t.wait_ge(ve.h, st['btr2_ve'])
                    bp = (i1 % 2) * 64
                    mm = t.matmul(reg,
                                  lbS[bp:bp + 32,
                                      (i1 // 2) * 1024 + oc * 128:(i1 // 2) * 1024 + oc * 128 + 128],
                                  ar_t.ap()[bp:bp + 32, btr_o:btr_o + TILE], start=False, stop=True)
                    te_y2[oc] = I(te, mm)
            st['te_y2'] = te_y2

        def b_E2(st, sbk):
            i1 = 2 * sbk + 1
            hin_o, hout_o = b_hio(st, sbk)
            te_y2 = st['te_y2']
            ti, pp = st['ti'], st['pp']

            @block.vector
            def _(v, i1=i1, hin_o=hin_o, hout_o=hout_o, sbk=sbk, ti=ti, pp=pp):
                if sbk == 0 and ti >= 2:
                    v.wait_ge(dtr[pp].h, htm_done_d[ti - 2])
                for oc in range(NCH):
                    v.wait_ge(te.h, te_y2[min(oc | 1, NCH - 1)])
                    reg = Y2REGS[oc % 6]
                    if sbk < 2:
                        v.scalar_tensor_tensor(V(hout_o + oc * TILE, TILE), reg,
                                               biasS[:, i1 * 8 + oc:i1 * 8 + oc + 1],
                                               V(hin_o + oc * TILE, TILE),
                                               op0=ALU.add, op1=ALU.add)
                        bank_set(Y2BANKS[oc % 6], 'ev2', I(ev2, v.drain()))
                    else:
                        hov = V(hout_o, QC * 1024).rearrange(
                            "p (qc c f) -> p qc c f", qc=QC, c=NCH)[:, :, oc, :]
                        v.scalar_tensor_tensor(
                            hov, reg.rearrange("p (qc f) -> p qc f", qc=QC),
                            biasS[:, i1 * 8 + oc:i1 * 8 + oc + 1],
                            V(hin_o + oc * TILE, TILE).rearrange(
                                "p (qc f) -> p qc f", qc=QC),
                            op0=ALU.add, op1=ALU.add)
                        bank_set(Y2BANKS[oc % 6], 'ev2', I(ev2, v.drain()))
            st['h_ready'] = ev2.n
            if sbk == 2:
                te_after_tile[(st['ti'], 'B')] = te.n
                hfin_ev2_d[st['ti']] = ev2.n

        def b_tail(st):
            ti, pp, b, q0 = st['ti'], st['pp'], st['b'], st['q0']
            x2v = st['x2v']
            hfin_o = o_hF + st['hb'] * NCH * TILE

            @block.sync
            def _(sy, pp=pp, hfin_o=hfin_o):
                sy.wait_ge(ev2.h, ev2.n)
                sy.wait_ge(ve.h, last_yout['ve'])
                trbatch(sy, pp, [
                    (V(o_hTM + qc * 1024, 1024).rearrange("p (c f) -> p c f", c=NCH),
                     V(hfin_o + qc * 1024, 1024)) for qc in range(QC)])
            htm_done_d[ti] = dtr[pp].n

            @block.vector
            def _(v, pp=pp, ti=ti, x2v=x2v):
                v.wait_ge(dtr[pp].h, htm_done_d[ti])
                v.wait_ge(dys.h, dys.n)
                for qc in range(QC):
                    v.tensor_tensor(V(o_yout + qc * 2048, 2048).bitcast(F32),
                                    x2v[:, qc * 1024:(qc + 1) * 1024],
                                    V(o_hTM + qc * 1024, 1024), op=ALU.add)
                    v.drain()
                I(ve, v.drain())
                ve_after_x2[ti] = ve.n
            last_yout['ve'] = ve.n

            @block.gpsimd
            def _(g, b=b, q0=q0):
                g.wait_ge(ve.h, last_yout['ve'])
                I(dys, g.dma_start(
                    y_out[b, q0:q0 + TILE, :].rearrange("(tc p) f -> p tc f", p=128),
                    V(o_yout, 2 * QC * 1024).bitcast(F32).rearrange(
                        "p (tc f) -> p tc f", tc=QC)), 16)

        assert nbt_total % 2 == 0 or nbt_total == 0
        for pair in range(nbt_total // 2):
            A = b_state(2 * pair)
            Bst = b_state(2 * pair + 1)
            b_Y1(A, 0)
            b_L1e(A, 0)
            b_Y1(Bst, 0)
            b_L1e(Bst, 0)
            for k in range(3):
                b_L1m(A, k)
                b_L1v(A, k)
                if k > 0:
                    b_Y1a(Bst, k)
                b_L1c(A, k)
                if k > 0:
                    b_Y1b(Bst, k)
                    b_L2(A, k)
                    b_Ga(A, k)
                    b_L1e(Bst, k)
                else:
                    b_L2(A, k)
                    b_Ga(A, k)
                if k == 1 and 2 * pair + 2 < nbt_total:
                    emit_x2load(2 * pair + 2)
                b_Y2(A, k)
                b_E2(A, k)
                b_L1m(Bst, k)
                b_L1v(Bst, k)
                if k < 2:
                    b_Y1a(A, k + 1)
                b_L1c(Bst, k)
                if k < 2:
                    b_Y1b(A, k + 1)
                    b_L2(Bst, k)
                    b_Ga(Bst, k)
                    b_L1e(A, k + 1)
                else:
                    b_L2(Bst, k)
                    b_Ga(Bst, k)
                    b_tail(A)
                b_Y2(Bst, k)
                b_E2(Bst, k)
            b_tail(Bst)
            if 2 * pair + 2 < nbt_total:
                emit_ln2(2 * pair + 2)
                emit_h0(2 * pair + 2)
            if 2 * pair + 3 < nbt_total:
                emit_x2load(2 * pair + 3)
                emit_ln2(2 * pair + 3)
                emit_h0(2 * pair + 3)

        @block.sync
        def _(sy):
            sy.wait_ge(dys.h, dys.n)

    return nc


def _host_inputs(L0_attn_x, L0_attn_wq, L0_attn_wk, L0_attn_wv, L0_attn_wgq,
                 L0_attn_wgk, L0_attn_wgv, L0_attn_r_out_w, L0_attn_gate_local_w,
                 L0_attn_gate_remote_w, L0_attn_w_out, L0_ffn_qweights,
                 L0_ffn_scales, L0_ffn_biases, L0_ffn_lora_a, L0_ffn_lora_b,
                 L0_ffn_ln_gamma, L0_ffn_ln_beta):
    f32 = np.float32
    x = np.ascontiguousarray(np.asarray(L0_attn_x, dtype=f32))

    gl = np.asarray(L0_attn_gate_local_w, dtype=f32)
    gr = np.asarray(L0_attn_gate_remote_w, dtype=f32)
    glgr = np.zeros((64, 32), dtype=f32)
    for h in range(H):
        glgr[:, h * 2] = gl[h]
        glgr[:, h * 2 + 1] = gr[h]

    qTi = np.ascontiguousarray(np.asarray(L0_ffn_qweights, dtype=np.int32).transpose(0, 2, 1))
    qT = np.ascontiguousarray(qTi.view(np.int8)[:, :, 0::4])
    scT = np.ascontiguousarray(np.asarray(L0_ffn_scales, dtype=f32).transpose(0, 2, 1))
    rsel = np.zeros((64, NCH * 128), dtype=f32)
    for c in range(NCH):
        for p in range(128):
            rsel[8 * c + p // 16, c * 128 + p] = 1.0
    laT = np.ascontiguousarray(np.asarray(L0_ffn_lora_a, dtype=f32).transpose(0, 2, 1))
    lbT = np.asarray(L0_ffn_lora_b, dtype=f32).transpose(0, 2, 1)
    lb4 = np.zeros((3, 128, D), dtype=f32)
    for i in range(6):
        lb4[i // 2, (i % 2) * 64:(i % 2) * 64 + 32, :] = lbT[i]
    biasT = np.ascontiguousarray(
        np.asarray(L0_ffn_biases, dtype=f32).reshape(6, 8, 128).transpose(0, 2, 1))
    lngT = np.ascontiguousarray(
        np.asarray(L0_ffn_ln_gamma, dtype=f32).reshape(3, 8, 128).transpose(0, 2, 1))
    lnbT = np.ascontiguousarray(
        np.asarray(L0_ffn_ln_beta, dtype=f32).reshape(3, 8, 128).transpose(0, 2, 1))

    i_ = np.arange(128)
    m1 = (i_[None, :] < i_[:, None]).astype(f32)
    m2 = (i_[None, :] >= i_[:, None]).astype(f32)
    ones_t = np.ones((128, 128), f32)
    zeros_t = np.zeros((128, 128), f32)
    mloc = np.zeros((4, 128, 256), dtype=f32)
    mloc[0] = np.concatenate([m1, zeros_t], axis=1)
    mloc[1] = np.concatenate([ones_t, m1], axis=1)
    mloc[2] = np.concatenate([m2, ones_t], axis=1)
    mloc[3] = np.concatenate([zeros_t, m2], axis=1)
    mlat = np.zeros((NT, NL, TILE), dtype=f32)
    for T in range(NT):
        qabs = T * TILE + np.arange(TILE)
        mlat[T] = (np.arange(NL)[:, None] <= (qabs[None, :] // POOL)).astype(f32)
    idm = np.eye(128, dtype=f32)
    onesv = np.zeros((128, 2), dtype=f32)
    onesv[:, 0] = 1.0
    onesr = np.ones((1, 128), dtype=f32)

    com = dict(
        wq=np.ascontiguousarray(np.asarray(L0_attn_wq, dtype=f32)),
        wk=np.ascontiguousarray(np.asarray(L0_attn_wk, dtype=f32)),
        wv=np.ascontiguousarray(np.asarray(L0_attn_wv, dtype=f32)),
        wgq=np.ascontiguousarray(np.asarray(L0_attn_wgq, dtype=f32)),
        wgk=np.ascontiguousarray(np.asarray(L0_attn_wgk, dtype=f32)),
        wgv=np.ascontiguousarray(np.asarray(L0_attn_wgv, dtype=f32)),
        rout=np.ascontiguousarray(np.asarray(L0_attn_r_out_w, dtype=f32)),
        wout=np.ascontiguousarray(np.asarray(L0_attn_w_out, dtype=f32)),
        glgr=glgr, qT=qT, scT=scT, rsel=rsel, laT=laT, lb4=lb4, biasT=biasT,
        lngT=lngT, lnbT=lnbT, mloc=mloc, mlat=mlat, idm=idm, onesv=onesv,
        onesr=onesr,
    )
    in_maps = []
    for c in range(NCORES):
        m = dict(com)
        m["x_in"] = np.ascontiguousarray(x[c * B:(c + 1) * B])
        in_maps.append(m)
    return in_maps


def kernel(**inputs):
    in_maps = _host_inputs(**inputs)
    nc = build_program()
    res = run_bass_kernel_spmd(nc, in_maps, core_ids=list(range(NCORES)))
    return np.concatenate([r["y"] for r in res.results], axis=0)
